# version 03
# baseline (speedup 1.0000x reference)
import math
from contextlib import ExitStack

import numpy as np
import ml_dtypes

import concourse.bass as bass
import concourse.mybir as mybir
from concourse.ap import AP as APClass
from concourse.bass_utils import run_bass_kernel_spmd

BFNP = ml_dtypes.bfloat16
F32 = mybir.dt.float32
BF16 = mybir.dt.bfloat16
I32 = mybir.dt.int32
U8 = mybir.dt.uint8
AF = mybir.ActivationFunctionType
ALU = mybir.AluOpType
AX = mybir.AxisListType.X

R = 8
D = 2048
SEQ = 8192
TOK = 2048
NH = 16
HD = 128
HPC = 4
NSUB = 4
QKV_F = 6144
INTER = 8192
SCALE_A = HD ** -0.5
ROT = 32
LN_EPS = 1e-5

NCTRL = 130156
NSAMP = 16777216
NCH = NSAMP // 128
SP_TILES = NCH // 128
SP_TPC = SP_TILES // R
SP_STEP = float(NCTRL - 1) / float(NSAMP - 1)

ENGS = ("sync", "tensor", "vector", "scalar", "gpsimd")
G4 = [[0, 1, 2, 3], [4, 5, 6, 7]]
G8 = [[0, 1, 2, 3, 4, 5, 6, 7]]


def _spline_chunk_grid():
    t = np.linspace(0.0, float(NCTRL - 1), NSAMP, dtype=np.float64)
    j0 = np.arange(NCH, dtype=np.int64) * 128
    t0 = t[j0]
    i0 = np.clip(np.floor(t0).astype(np.int64), 0, NCTRL - 2)
    f0 = (t0 - i0).astype(np.float32)
    return i0, f0


def _swin(cp, i0, f0):
    out = np.zeros((NCH, 8), np.float32)
    for k in range(-1, 4):
        out[:, k + 1] = cp[np.clip(i0 + k, 0, NCTRL - 1)]
    out[:, 5] = f0
    return out


def _rotary_tables():
    inv_freq = 1.0 / (10000.0 ** (np.arange(0, ROT, 2, dtype=np.float32) / ROT))
    freqs = np.arange(SEQ, dtype=np.float32)[:, None] * inv_freq[None, :]
    pos = np.concatenate([freqs, freqs], axis=-1)
    cos = np.ascontiguousarray(np.cos(pos).T.astype(np.float32))
    sin = np.sin(pos).T.astype(np.float32)
    sin[:16] *= -1.0
    return cos, np.ascontiguousarray(sin)


def _alibi_slopes():
    start = 2.0 ** (-(2.0 ** (-(math.log2(NH) - 3))))
    return np.array([start * (start ** i) for i in range(NH)], dtype=np.float32)


class Prog:
    def __init__(self, nc, es):
        self.nc = nc
        self.es = es
        self.ops = {e: [] for e in ENGS}
        self.sems = {}
        self.stash = {}

    def sem(self, name):
        if name not in self.sems:
            h = self.es.enter_context(self.nc.semaphore(name))
            self.sems[name] = [h, 0]
        return self.sems[name]

    def val(self, name):
        return self.sem(name)[1]

    def op(self, eng, fn, waits=(), incs=()):
        wl = [(self.sem(n)[0], v) for (n, v) in waits if v > 0]
        il = []
        for (n, a) in incs:
            s = self.sem(n)
            s[1] += a
            il.append((s[0], a))
        if fn is None or len(il) <= 1:
            self.ops[eng].append((fn, wl, il))
            return
        first = il[0]
        self.ops[eng].append((fn, wl, [first]))
        self.ops[eng].append((None, [(first[0], self._cur(first[0]))], il[1:]))

    def _cur(self, handle):
        for name, s in self.sems.items():
            if s[0] is handle:
                return s[1]
        raise KeyError

    def edge(self, eng):
        fn, wl, il = self.ops[eng][-1]
        if il:
            self.ops[eng].append((None, [(il[0][0], self._cur(il[0][0]))], []))
            return
        assert fn is not None, "edge after a bare wait"
        name = "edge_" + eng
        s = self.sem(name)
        s[1] += 1
        il.append((s[0], 1))
        self.ops[eng][-1] = (fn, wl, il)
        self.ops[eng].append((None, [(s[0], s[1])], []))

    def barrier(self):
        for e in ENGS:
            self.op(e, None, incs=[("bar", 1)])
        tgt = self.val("bar")
        for e in ENGS:
            self.op(e, None, waits=[("bar", tgt)])

    def replay(self):
        nc = self.nc
        with nc.Block() as block:
            for ename in ENGS:
                ops = self.ops[ename]

                def body(eng, ops=ops):
                    for fn, wl, il in ops:
                        for (h, v) in wl:
                            eng.wait_ge(h, v)
                        if fn is None:
                            for (h, a) in il:
                                eng.sem_inc(h, a)
                        else:
                            inst = fn(eng)
                            for (h, a) in il:
                                inst = inst.then_inc(h, a)

                getattr(block, ename)(body)


class Slots:

    def __init__(self, p, name, nslots):
        self.p = p
        self.name = name
        self.n = nslots
        self.w = 0
        self.c = 0
        self.ccum = [0] * nslots
        self.crel = [[] for _ in range(nslots)]
        self.pcum = [0] * nslots

    def wait_free(self):
        i = self.w
        if i < self.n:
            return []
        sl = i % self.n
        k = i // self.n - 1
        assert len(self.crel[sl]) > k, f"slot {self.name}: missing consumed()"
        self.p.sem(f"{self.name}_c{sl}")
        return [(f"{self.name}_c{sl}", self.crel[sl][k])]

    def prod_inc(self, amount=16):
        sl = self.w % self.n
        self.pcum[sl] += amount
        self.p.sem(f"{self.name}_p{sl}")
        return (f"{self.name}_p{sl}", amount)

    def prod_wait(self):
        sl = (self.w - 1) % self.n
        return [(f"{self.name}_p{sl}", self.pcum[sl])]

    def wrote(self):
        self.w += 1

    def consumed(self, amount=1):
        sl = self.c % self.n
        self.c += 1
        self.ccum[sl] += amount
        self.crel[sl].append(self.ccum[sl])
        self.p.sem(f"{self.name}_c{sl}")
        return (f"{self.name}_c{sl}", amount)

    def done_waits(self):
        return [(f"{self.name}_c{i}", self.ccum[i])
                for i in range(self.n) if self.ccum[i] > 0]


def _ap(t, offset, dims):
    return APClass(t.tensor, offset, [list(d) for d in dims])


def build(n_sub=NSUB, dev_skip=(), dbg=()):
    nc = bass.Bass("TRN2", target_bir_lowering=False)
    es = ExitStack()
    p = Prog(nc, es)

    def ein(name, shape, dt=F32):
        return nc.dram_tensor(name, shape, dt, kind="ExternalInput").ap()

    x_in = ein("x", [D, TOK])
    wqkvT = ein("wqkvT", [D, QKV_F])
    woutT = ein("woutT", [D, D])
    wq = ein("wq", [D, D])
    wk = ein("wk", [D, 512])
    wv = ein("wv", [D, 512])
    wo = ein("wo", [D, D])
    swing = ein("swing", [NCH // R, 8])
    swinu = ein("swinu", [NCH // R, 8])
    swind = ein("swind", [NCH // R, 8])
    cosT = ein("cosT", [ROT, TOK])
    sinT = ein("sinT", [ROT, TOK])
    stile = ein("stile", [128, 128])
    kgrid = ein("kgrid", [128, 64])
    slopebc = ein("slopebc", [128, HPC])
    relg = ein("relg", [128, 512])
    cinfo = ein("cinfo", [1, 8], I32)

    out_t = nc.dram_tensor("out", [D, TOK], F32, kind="ExternalOutput").ap()

    def idram(name, shape, dt=F32):
        return nc.dram_tensor(name, shape, dt, kind="Internal").ap()

    residA = idram("residA", [D, TOK])
    residB = idram("residB", [D, TOK])
    qsh = idram("qsh", [D, TOK], BF16)
    ksh = idram("ksh", [D, TOK], BF16)
    vsh = idram("vsh", [TOK, D], BF16)
    nsh = idram("nsh", [2, NH * TOK])
    qga = idram("qga", [4 * D, TOK], BF16)
    kga = idram("kga", [4 * D, TOK], BF16)
    vga = idram("vga", [4 * TOK, D], BF16)
    nga = idram("nga", [8, NH * TOK])
    aosh = idram("aosh", [HPC * HD, SEQ], BF16)
    aoga = idram("aoga", [4 * HPC * HD, SEQ], BF16)
    kshb = idram("kshb", [512, TOK], BF16)
    vshb = idram("vshb", [TOK, 512], BF16)
    kgab = idram("kgab", [4 * 512, TOK], BF16)
    vgab = idram("vgab", [4 * TOK, 512], BF16)
    kloc = idram("kloc", [512, SEQ], BF16)
    vloc = idram("vloc", [SEQ, 512], BF16)
    qloc = idram("qloc", [512, SEQ], BF16)
    aoloc = idram("aoloc", [D, TOK], BF16)
    nlock = idram("nlock", [16, TOK])
    nlocq = idram("nlocq", [16, TOK])
    wg_sh = idram("wg_sh", [NSAMP // R, 1], BF16)
    wu_sh = idram("wu_sh", [NSAMP // R, 1], BF16)
    wd_sh = idram("wd_sh", [NSAMP // R, 1], BF16)
    wg_f = idram("wg_f", [INTER, D], BF16)
    wu_f = idram("wu_f", [INTER, D], BF16)
    wd_f = idram("wd_f", [D, INTER], BF16)

    def sbuf(name, shape, dt):
        return es.enter_context(nc.sbuf_tensor(name, shape, dt))

    XLN = sbuf("XLN", [128, 16 * TOK], BF16)
    LNRS = sbuf("LNRS", [128, 2 * 512], F32)
    RSC = sbuf("RSC", [128, 2 * 512], F32)
    FEV = sbuf("FEV", [128, 4 * 512], F32)
    WCV = sbuf("WCV", [128, 2 * 512], F32)
    WBF = sbuf("WBF", [128, 16 * 512], BF16)
    QKVS = sbuf("QKVS", [128, 2 * 512], BF16)
    VTST = sbuf("VTST", [128, 2 * 512], BF16)
    SHPRE = sbuf("SHPRE", [32, 512], F32)
    SH32 = sbuf("SH32", [32, 512], F32)
    RT1 = sbuf("RT1", [32, 512], F32)
    RT2 = sbuf("RT2", [32, 512], F32)
    COSS = sbuf("COSS", [ROT, TOK], BF16)
    SINS = sbuf("SINS", [ROT, TOK], BF16)
    SQB = sbuf("SQB", [128, 2 * 512], BF16)
    ROWS = sbuf("ROWS", [1, 4 * 512], F32)
    NROW = ROWS[0:1, 512:1024]
    BCA = sbuf("BCA", [128, 512], F32)
    BCB = sbuf("BCB", [128, 512], F32)
    ONE_B = sbuf("ONE_B", [128, 1], BF16)
    ONE_F = sbuf("ONE_F", [128, 1], F32)
    ONE_RF = sbuf("ONE_RF", [1, 128], F32)
    IDENT = sbuf("IDENT", [128, 128], BF16)
    CINF = sbuf("CINF", [1, 8], I32)
    POOL = sbuf("POOL", [128, 26624], BF16)
    KSB = POOL[:, 0:SEQ]
    VSB = POOL[:, SEQ:2 * SEQ]
    QTL = POOL[:, 2 * SEQ:2 * SEQ + 512]
    PT = POOL[:, 2 * SEQ + 512:2 * SEQ + 512 + 1024]
    NMAX = sbuf("NMAX", [1, 1024], F32)
    STAT = sbuf("STAT", [1, 64], F32)
    STCOL = sbuf("STCOL", [128, 8], F32)
    AOST = sbuf("AOST", [128, 512], BF16)
    RECR = ROWS[0:1, 0:512]
    SWALL = sbuf("SWALL", [128, SP_TPC * 8], F32)
    SPF = sbuf("SPF", [128, 128], F32)
    SPF2 = sbuf("SPF2", [128, 128], F32)
    SPF3 = sbuf("SPF3", [128, 128], F32)
    SPM = sbuf("SPM", [128, 128], U8)
    SPU = sbuf("SPU", [128, 128], F32)
    SPV = sbuf("SPV", [128, 128], F32)
    SPA = sbuf("SPA", [128, 128], F32)
    SPB = sbuf("SPB", [128, 128], F32)
    SPCOL = sbuf("SPCOL", [128, 32], F32)
    SPW = sbuf("SPW", [128, 4 * 128], BF16)
    STILE = sbuf("STILE", [128, 128], F32)
    SLOPE = sbuf("SLOPE", [128, HPC], F32)
    RELG = sbuf("RELG", [128, 512], F32)
    FWG = POOL[:, 0:8192]
    FWU = POOL[:, 8192:16384]
    FWD = POOL[:, 16384:24576]
    FIN = POOL[:, 24576:26624]
    FSG = sbuf("FSG", [128, 2 * 512], F32)
    FSD = FSG

    def psum(name, shape, dt=F32):
        return es.enter_context(nc.psum_tensor(name, shape, dt))

    PS0 = psum("PS0", [128, 512])
    PS1 = psum("PS1", [128, 512])
    PS2 = psum("PS2", [128, 512])
    PS3 = psum("PS3", [128, 512])
    PROW = psum("PROW", [1, 512])
    PROW2 = psum("PROW2", [1, 512])
    PBC = psum("PBC", [128, 512])
    PTR = psum("PTR", [128, 128], BF16)

    sl_lnrs = Slots(p, "s_lnrs", 2)
    sl_rsc = Slots(p, "s_rsc", 2)
    sl_fev = Slots(p, "s_fev", 1)
    sl_fsd = Slots(p, "s_fsd", 2)
    sl_wcv = Slots(p, "s_wcv", 2)
    sl_wbf = Slots(p, "s_wbf", 1)
    sl_qkvs = Slots(p, "s_qkvs", 2)
    sl_vtst = Slots(p, "s_vtst", 2)
    sl_sqb = Slots(p, "s_sqb", 2)
    sl_psA = Slots(p, "s_psA", 1)
    sl_psB = Slots(p, "s_psB", 1)
    sl_psC = Slots(p, "s_psC", 1)
    sl_psD = Slots(p, "s_psD", 1)
    sl_prow = Slots(p, "s_prow", 1)
    sl_prow2 = Slots(p, "s_prow2", 1)
    sl_pbc = Slots(p, "s_pbc", 1)
    sl_ptr = Slots(p, "s_ptr", 1)
    sl_bc = Slots(p, "s_bc", 1)
    sl_xln = Slots(p, "s_xln", 1)
    sl_pt = Slots(p, "s_pt", 2)
    sl_qtl = Slots(p, "s_qtl", 1)
    sl_ksb = Slots(p, "s_ksb", 1)
    sl_kst = Slots(p, "s_kst", 1)
    sl_aol = Slots(p, "s_aol", 1)
    sl_aost = Slots(p, "s_aost", 1)
    sl_fin = Slots(p, "s_fin", 1)
    sl_fwg = Slots(p, "s_fwg", 1)
    sl_swall = Slots(p, "s_swall", 1)
    sl_spw = Slots(p, "s_spw", 1)
    sl_nmax = Slots(p, "s_nmax", 1)
    sl_bcol = Slots(p, "s_bcol", 2)

    for buf, src in ((CINF, cinfo), (STILE, stile),
                     (SLOPE, slopebc), (RELG, relg)):
        p.op("sync", lambda e, b=buf, s=src: e.dma_start(out=b[:], in_=s[:]),
             incs=[("ld", 16)])
    for (buf, src) in ((COSS, cosT), (SINS, sinT)):
        for half in range(2):
            p.op("sync", lambda e, src=src, half=half: e.dma_start(
                out=WCV[0:32, 0:1024],
                in_=src[0:32, half * 1024:(half + 1) * 1024]),
                waits=[("cvt", p.val("cvt"))], incs=[("ld", 16)])
            p.op("vector", lambda e, buf=buf, half=half: e.tensor_copy(
                buf[0:32, half * 1024:(half + 1) * 1024], WCV[0:32, 0:1024]),
                waits=[("ld", p.val("ld"))], incs=[("cvt", 1)])
            p.edge("vector")
    ld0 = p.val("ld")
    p.op("vector", lambda e: e.memset(ONE_B[:], 1.0))
    p.op("vector", lambda e: e.memset(ONE_F[:], 1.0))
    p.op("vector", lambda e: e.memset(ONE_RF[:], 1.0), incs=[("cst", 1)])
    p.op("gpsimd", lambda e: e.memset(IDENT[:], 0.0))
    p.edge("gpsimd")
    p.op("gpsimd", lambda e: e.affine_select(
        IDENT[:], IDENT[:], [[-1, 128]], ALU.not_equal, 1.0, base=0,
        channel_multiplier=1), incs=[("cst", 1)])

    _DYN = {"hK": 0, "hV": 1, "hKb": 2, "hVb": 3, "tl": 4, "hN": 5, "hNk": 6}

    def _loadregs(e):
        p.stash["r0"] = e.alloc_register("dynbase")
        p.stash["r1"] = e.alloc_register("dynoff")
        return e.nop()

    def dyn(e, name, const=0):
        i = _DYN[name]
        e.load(p.stash["r0"], CINF[0:1, i:i + 1])
        e.reg_add(p.stash["r1"], p.stash["r0"], int(const))
        return p.stash["r1"]

    p.op("sync", _loadregs, waits=[("ld", ld0)], incs=[("cst", 1)])
    p.op("sync", lambda e: e.dma_start(out=residA[:], in_=x_in[:]),
         incs=[("rst", 16)])

    def spline_phase():
        mats = [(swing, wg_sh), (swinu, wu_sh), (swind, wd_sh)]
        C = SPCOL

        def vop(fn, waits=(), incs=(), eng="vector", edge=True):
            p.op(eng, fn, waits=waits, incs=incs)
            if edge:
                p.edge(eng)

        for mi, (src, dst) in enumerate(mats):
            dims = [[8, 128], [8 * 128, SP_TPC], [1, 8]]
            ap = _ap(src, 0, dims)
            p.op("sync", lambda e, ap=ap: e.dma_start(out=SWALL[:], in_=ap),
                 waits=sl_swall.wait_free(), incs=[("spld", 16)])
            sl_swall.wrote()
            spld = p.val("spld")
            for tt in range(SP_TPC):
                cw = SWALL[:, tt * 8:(tt + 1) * 8]
                wm1, w0, w1, w2, w3 = (cw[:, 0:1], cw[:, 1:2], cw[:, 2:3],
                                       cw[:, 3:4], cw[:, 4:5])
                f0c = cw[:, 5:6]
                wA = [("spld", spld)] if tt == 0 else []
                vop(lambda e, a=C[:, 0:1], x=w0: e.tensor_copy(a, x), waits=wA, edge=False)
                vop(lambda e, a=C[:, 1:2], x=w1, y=wm1: e.tensor_sub(a, x, y))
                vop(lambda e, a=C[:, 1:2]: e.tensor_scalar(a, a, 0.5, None, ALU.mult), edge=False)
                vop(lambda e, a=C[:, 8:9], x=wm1: e.tensor_scalar(a, x, 2.0, None, ALU.mult), edge=False)
                vop(lambda e, a=C[:, 9:10], x=w0: e.tensor_scalar(a, x, -5.0, None, ALU.mult), edge=False)
                vop(lambda e, a=C[:, 10:11], x=w1: e.tensor_scalar(a, x, 4.0, None, ALU.mult))
                vop(lambda e, a=C[:, 2:3], x=C[:, 8:9], y=C[:, 9:10]: e.tensor_add(a, x, y))
                vop(lambda e, a=C[:, 2:3], y=C[:, 10:11]: e.tensor_add(a, a, y))
                vop(lambda e, a=C[:, 2:3], y=w2: e.tensor_sub(a, a, y))
                vop(lambda e, a=C[:, 2:3]: e.tensor_scalar(a, a, 0.5, None, ALU.mult), edge=False)
                vop(lambda e, a=C[:, 11:12], x=w0, y=w1: e.tensor_sub(a, x, y))
                vop(lambda e, a=C[:, 3:4], x=C[:, 11:12]: e.tensor_scalar(a, x, 3.0, None, ALU.mult))
                vop(lambda e, a=C[:, 3:4], y=wm1: e.tensor_sub(a, a, y))
                vop(lambda e, a=C[:, 3:4], y=w2: e.tensor_add(a, a, y))
                vop(lambda e, a=C[:, 3:4]: e.tensor_scalar(a, a, 0.5, None, ALU.mult), edge=False)
                vop(lambda e, a=C[:, 12:13], x=w2, y=w0: e.tensor_sub(a, x, y))
                vop(lambda e, a=C[:, 12:13]: e.tensor_scalar(a, a, 0.5, None, ALU.mult), edge=False)
                vop(lambda e, a=C[:, 13:14], x=w0: e.tensor_scalar(a, x, 2.0, None, ALU.mult), edge=False)
                vop(lambda e, a=C[:, 14:15], x=w1: e.tensor_scalar(a, x, -5.0, None, ALU.mult), edge=False)
                vop(lambda e, a=C[:, 15:16], x=w2: e.tensor_scalar(a, x, 4.0, None, ALU.mult))
                vop(lambda e, a=C[:, 13:14], y=C[:, 14:15]: e.tensor_add(a, a, y))
                vop(lambda e, a=C[:, 13:14], y=C[:, 15:16]: e.tensor_add(a, a, y))
                vop(lambda e, a=C[:, 13:14], y=w3: e.tensor_sub(a, a, y))
                vop(lambda e, a=C[:, 13:14]: e.tensor_scalar(a, a, 0.5, None, ALU.mult), edge=False)
                vop(lambda e, a=C[:, 16:17], x=w1, y=w2: e.tensor_sub(a, x, y))
                vop(lambda e, a=C[:, 14:15], x=C[:, 16:17]: e.tensor_scalar(a, x, 3.0, None, ALU.mult))
                vop(lambda e, a=C[:, 14:15], y=w0: e.tensor_sub(a, a, y))
                vop(lambda e, a=C[:, 14:15], y=w3: e.tensor_add(a, a, y))
                vop(lambda e, a=C[:, 14:15]: e.tensor_scalar(a, a, 0.5, None, ALU.mult))
                vop(lambda e, a=C[:, 4:5], x=w1, y=C[:, 12:13]: e.tensor_sub(a, x, y))
                vop(lambda e, a=C[:, 4:5], y=C[:, 13:14]: e.tensor_add(a, a, y))
                vop(lambda e, a=C[:, 4:5], y=C[:, 14:15]: e.tensor_sub(a, a, y), edge=False)
                vop(lambda e, a=C[:, 17:18], x=C[:, 13:14]: e.tensor_scalar(a, x, -2.0, None, ALU.mult), edge=False)
                vop(lambda e, a=C[:, 18:19], x=C[:, 14:15]: e.tensor_scalar(a, x, 3.0, None, ALU.mult))
                vop(lambda e, a=C[:, 5:6], x=C[:, 12:13], y=C[:, 17:18]: e.tensor_add(a, x, y))
                vop(lambda e, a=C[:, 5:6], y=C[:, 18:19]: e.tensor_add(a, a, y), edge=False)
                vop(lambda e, a=C[:, 6:7], x=C[:, 13:14], y=C[:, 18:19]: e.tensor_sub(a, x, y), edge=False)
                vop(lambda e, a=C[:, 7:8], x=C[:, 14:15]: e.tensor_copy(a, x))
                vop(lambda e, f0c=f0c: e.tensor_scalar(SPF[:], STILE[:], f0c, None, ALU.add),
                    waits=[("cst", 3)] if (mi == 0 and tt == 0) else [])
                vop(lambda e: e.tensor_tensor(SPF2[:], SPF[:], SPF[:], ALU.mult))
                vop(lambda e: e.tensor_tensor(SPF3[:], SPF2[:], SPF[:], ALU.mult), edge=False)
                vop(lambda e: e.tensor_single_scalar(SPM[:], SPF[:], 1.0, ALU.is_lt))
                vop(lambda e: e.tensor_scalar(SPU[:], SPF[:], C[:, 1:2], C[:, 0:1], ALU.mult, ALU.add))
                vop(lambda e: e.scalar_tensor_tensor(SPU[:], SPF2[:], C[:, 2:3], SPU[:], ALU.mult, ALU.add))
                vop(lambda e: e.scalar_tensor_tensor(SPA[:], SPF3[:], C[:, 3:4], SPU[:], ALU.mult, ALU.add), edge=False)
                vop(lambda e: e.tensor_scalar(SPV[:], SPF[:], C[:, 5:6], C[:, 4:5], ALU.mult, ALU.add))
                vop(lambda e: e.scalar_tensor_tensor(SPV[:], SPF2[:], C[:, 6:7], SPV[:], ALU.mult, ALU.add))
                vop(lambda e: e.scalar_tensor_tensor(SPB[:], SPF3[:], C[:, 7:8], SPV[:], ALU.mult, ALU.add))
                slot = tt % 4
                wsel = sl_spw.wait_free() if slot == 0 else []
                vop(lambda e, slot=slot: e.select(
                    SPW[:, slot * 128:(slot + 1) * 128], SPM[:], SPA[:], SPB[:]),
                    waits=wsel, incs=[("spsel", 1)])
                if slot == 3:
                    base = (tt - 3) * 16384
                    dims2 = [[128, 128], [16384, 4], [1, 128]]
                    dap = _ap(dst, base, dims2)
                    sap = SPW[:].rearrange("p (t s) -> p t s", t=4)
                    p.op("sync", lambda e, dap=dap, sap=sap: e.dma_start(
                        out=dap, in_=sap),
                        waits=[("spsel", p.val("spsel"))],
                        incs=[sl_spw.consumed(16)])
                    sl_spw.wrote()
            p.op("vector", None, waits=[], incs=[sl_swall.consumed()])
        spw_done = sl_spw.done_waits()
        for sh, full in ((wg_sh, wg_f), (wu_sh, wu_f), (wd_sh, wd_f)):
            p.op("gpsimd", lambda e, sh=sh, full=full: e.collective_compute(
                "AllGather", ALU.bypass, replica_groups=G8,
                ins=[sh.opt()], outs=[full.opt()]),
                waits=spw_done, incs=[("wgath", 1)])
        return p.val("wgath")

    def ln_phase(resid, waits0=()):
        for tb in range(4):
            for ch in range(16):
                slot = ch % 2
                src = _ap(resid, (128 * ch) * TOK + tb * 512, [[TOK, 128], [1, 512]])
                w = list(waits0) + sl_lnrs.wait_free()
                p.op("sync", lambda e, src=src, slot=slot: e.dma_start(
                    out=LNRS[:, slot * 512:(slot + 1) * 512], in_=src),
                    waits=w, incs=[sl_lnrs.prod_inc()])
                sl_lnrs.wrote()
                lnw = sl_lnrs.prod_wait()
                wsq = list(lnw) + sl_sqb.wait_free()
                p.op("scalar", lambda e, slot=slot: e.activation(
                    SQB[:, slot * 512:(slot + 1) * 512],
                    LNRS[:, slot * 512:(slot + 1) * 512], AF.Square),
                    waits=wsq, incs=[("act_sq", 1)])
                sl_sqb.wrote()
                wmm = list(lnw) + [("act_sq", p.val("act_sq"))] + \
                      (([("cst", 3)] + sl_prow.wait_free()) if ch == 0 else [])
                p.op("tensor", lambda e, ch=ch, slot=slot: e.matmul(
                    PROW[0:1, :], ONE_F[:, 0:1], LNRS[:, slot * 512:(slot + 1) * 512],
                    start=(ch == 0), stop=(ch == 15)),
                    waits=wmm,
                    incs=[sl_lnrs.consumed()] + ([("pe_ln", 1)] if ch == 15 else []))
                if ch == 0:
                    sl_prow.wrote()
                wm2 = (sl_prow2.wait_free() if ch == 0 else [])
                p.op("tensor", lambda e, slot=slot, ch=ch: e.matmul(
                    PROW2[0:1, :], ONE_B[:, 0:1], SQB[:, slot * 512:(slot + 1) * 512],
                    start=(ch == 0), stop=(ch == 15)),
                    waits=wm2,
                    incs=[sl_sqb.consumed()] + ([("pe_ln", 1)] if ch == 15 else []))
                if ch == 0:
                    sl_prow2.wrote()
            pe_ln = p.val("pe_ln")
            mrow, vrow, rrow, nrow = (ROWS[0:1, 0:512], ROWS[0:1, 512:1024],
                                      ROWS[0:1, 1024:1536], ROWS[0:1, 1536:2048])
            p.op("vector", lambda e, m=mrow: e.tensor_scalar(
                m, PROW[0:1, :], 1.0 / D, None, ALU.mult),
                waits=[("pe_ln", pe_ln)],
                incs=[sl_prow.consumed()])
            p.op("vector", lambda e, v=vrow: e.tensor_scalar(
                v, PROW2[0:1, :], 1.0 / D, None, ALU.mult),
                incs=[sl_prow2.consumed()])
            p.edge("vector")
            p.op("vector", lambda e, r=rrow, m=mrow: e.tensor_tensor(r, m, m, ALU.mult))
            p.edge("vector")
            p.op("vector", lambda e, v=vrow, r=rrow: e.tensor_sub(v, v, r),
                 incs=[("dve_ln", 1)])
            p.edge("vector")
            p.op("vector", lambda e, v=vrow: e.tensor_scalar(
                v, v, LN_EPS, None, ALU.add), incs=[("dve_ln", 1)])
            p.edge("vector")
            p.op("scalar", lambda e, r=rrow, v=vrow: e.activation(r, v, AF.Ln),
                 waits=[("dve_ln", p.val("dve_ln"))], incs=[("act_ln", 1)])
            p.edge("scalar")
            p.op("vector", lambda e, r=rrow: e.tensor_scalar(
                r, r, -0.5, None, ALU.mult),
                waits=[("act_ln", p.val("act_ln"))], incs=[("dve_ln", 1)])
            p.edge("vector")
            p.op("scalar", lambda e, r=rrow: e.activation(r, r, AF.Exp),
                 waits=[("dve_ln", p.val("dve_ln"))], incs=[("act_ln", 1)])
            p.edge("scalar")
            p.op("vector", lambda e, n=nrow, m=mrow, r=rrow: e.tensor_tensor(n, m, r, ALU.mult),
                 waits=[("act_ln", p.val("act_ln"))])
            p.edge("vector")
            p.op("vector", lambda e, n=nrow: e.tensor_scalar(n, n, -1.0, None, ALU.mult),
                 incs=[("dve_ln", 1)])
            p.edge("vector")
            p.op("tensor", lambda e, r=rrow: e.matmul(
                PBC[:, :], ONE_RF[0:1, :], r, start=True, stop=True),
                waits=[("dve_ln", p.val("dve_ln"))] + sl_pbc.wait_free() + sl_bc.wait_free(),
                incs=[("pe_bc", 1)])
            sl_pbc.wrote()
            sl_bc.wrote()
            p.op("vector", lambda e: e.tensor_copy(BCA[:], PBC[:, :]),
                 waits=[("pe_bc", p.val("pe_bc"))],
                 incs=[sl_pbc.consumed(), ("dve_bc", 1)])
            p.op("tensor", lambda e, n=nrow: e.matmul(
                PBC[:, :], ONE_RF[0:1, :], n, start=True, stop=True),
                waits=[("dve_bc", p.val("dve_bc"))] + sl_pbc.wait_free(),
                incs=[("pe_bc", 1)])
            sl_pbc.wrote()
            p.op("vector", lambda e: e.tensor_copy(BCB[:], PBC[:, :]),
                 waits=[("pe_bc", p.val("pe_bc"))],
                 incs=[sl_pbc.consumed(), ("stat_done", 1)])
            p.edge("vector")
            for ch in range(16):
                slot = ch % 2
                src = _ap(resid, (128 * ch) * TOK + tb * 512, [[TOK, 128], [1, 512]])
                p.op("sync", lambda e, src=src, slot=slot: e.dma_start(
                    out=LNRS[:, slot * 512:(slot + 1) * 512], in_=src),
                    waits=list(waits0) + sl_lnrs.wait_free(),
                    incs=[sl_lnrs.prod_inc()])
                sl_lnrs.wrote()
                lnw = sl_lnrs.prod_wait()
                p.op("vector", lambda e, slot=slot: e.tensor_tensor(
                    FSD[:, slot * 512:(slot + 1) * 512],
                    LNRS[:, slot * 512:(slot + 1) * 512], BCA[:], ALU.mult),
                    waits=list(lnw) + [("stat_done", p.val("stat_done"))]
                          + sl_fsd.wait_free())
                sl_fsd.wrote()
                p.edge("vector")
                last = (ch == 15)
                p.op("vector", lambda e, ch=ch, tb=tb, slot=slot: e.tensor_tensor(
                    XLN[:, ch * TOK + tb * 512: ch * TOK + tb * 512 + 512],
                    FSD[:, slot * 512:(slot + 1) * 512], BCB[:], ALU.add),
                    incs=[("xlnorm", 1), sl_lnrs.consumed(), sl_fsd.consumed()]
                         + ([sl_bc.consumed()] if last else []))
                p.edge("vector")
        return p.val("xlnorm")

    def convert_wblock(wsrc, fb, scale=None):
        fdim = wsrc.shape[1]
        for ch in range(16):
            slot = ch % 2
            src = _ap(wsrc, (128 * ch) * fdim + fb * 512, [[fdim, 128], [1, 512]])
            p.op("sync", lambda e, src=src, slot=slot: e.dma_start(
                out=WCV[:, slot * 512:(slot + 1) * 512], in_=src),
                waits=sl_wcv.wait_free() + (sl_wbf.wait_free() if ch == 0 else []),
                incs=[sl_wcv.prod_inc()])
            sl_wcv.wrote()
            wcw = sl_wcv.prod_wait()
            if ch == 0:
                sl_wbf.wrote()
            if scale is None:
                p.op("vector", lambda e, ch=ch, slot=slot: e.tensor_copy(
                    WBF[:, ch * 512:(ch + 1) * 512], WCV[:, slot * 512:(slot + 1) * 512]),
                    waits=wcw, incs=[sl_wcv.consumed(), ("wconv", 1)])
            else:
                p.op("vector", lambda e, ch=ch, slot=slot, s=scale: e.tensor_scalar(
                    WBF[:, ch * 512:(ch + 1) * 512], WCV[:, slot * 512:(slot + 1) * 512],
                    s, None, ALU.mult),
                    waits=wcw, incs=[sl_wcv.consumed(), ("wconv", 1)])
            p.edge("vector")
        return p.val("wconv")

    def qkv_phase(kind, xln_tgt):
        if kind == "a":
            blocks = [("q", wqkvT, 0, 4, SCALE_A), ("k", wqkvT, 4, 4, None),
                      ("v", wqkvT, 8, 4, None)]
        else:
            blocks = [("q", wq, 0, 4, SCALE_A), ("k", wk, 0, 1, None),
                      ("v", wv, 0, 1, None)]
        kdst = ksh if kind == "a" else kshb
        vdst = vsh if kind == "a" else vshb
        for (role, wsrc, fb0, nfb, scale) in blocks:
            for fbl in range(nfb):
                wtgt = convert_wblock(wsrc, fb0 + fbl, scale)
                for fsub in range(4):
                    fc = fbl * 4 + fsub
                    for tb in range(4):
                        k2 = (fsub * 4 + tb) % 2
                        PSX, slx, psn = ((PS0, sl_psA, "psA") if k2 == 0
                                         else (PS1, sl_psB, "psB"))
                        for ch in range(16):
                            w = ([("wconv", wtgt), ("xlnorm", xln_tgt)]
                                 + slx.wait_free()) if ch == 0 else []
                            lastw = (fsub == 3 and tb == 3 and ch == 15)
                            p.op("tensor", lambda e, ch=ch, fsub=fsub, tb=tb, PSX=PSX: e.matmul(
                                PSX[:, :],
                                WBF[:, ch * 512 + fsub * 128: ch * 512 + fsub * 128 + 128],
                                XLN[:, ch * TOK + tb * 512: ch * TOK + tb * 512 + 512],
                                start=(ch == 0), stop=(ch == 15)),
                                waits=w, incs=([(psn, 1)] if ch == 15 else [])
                                     + ([sl_wbf.consumed()] if lastw else []))
                            if ch == 0:
                                slx.wrote()
                        pstgt = p.val(psn)
                        slot = tb % 2
                        rot = (kind == "a" and role in ("q", "k"))
                        p.op("scalar", lambda e, slot=slot, PSX=PSX: e.activation(
                            QKVS[:, slot * 512:(slot + 1) * 512], PSX[:, :], AF.Copy),
                            waits=[(psn, pstgt)] + sl_qkvs.wait_free(),
                            incs=[("qkev", 1)] + ([slx.consumed()] if not rot else []))
                        sl_qkvs.wrote()
                        qkev = p.val("qkev")
                        if rot:
                            ts = slice(tb * 512, tb * 512 + 512)
                            p.op("scalar", lambda e, PSX=PSX: e.copy(SHPRE[:, :], PSX[0:32, :]),
                                 incs=[("qkev", 1)])
                            p.edge("scalar")
                            qkev = p.val("qkev")
                            p.op("vector", lambda e, PSX=PSX, t=ts: e.tensor_tensor(
                                RT1[:, :], PSX[0:32, :], COSS[:, t], ALU.mult),
                                waits=[("qkev", qkev), ("cst", 3)],
                                incs=[slx.consumed()])
                            p.op("vector", lambda e: e.stream_shuffle(
                                SH32[:, :], SHPRE[:, :], [(i + 16) % 32 for i in range(32)]))
                            p.edge("vector")
                            p.op("vector", lambda e, t=ts: e.tensor_tensor(
                                RT2[:, :], SH32[:, :], SINS[:, t], ALU.mult))
                            p.edge("vector")
                            p.op("vector", lambda e: e.tensor_add(RT1[:, :], RT1[:, :], RT2[:, :]))
                            p.edge("vector")
                            p.op("vector", lambda e, slot=slot: e.tensor_copy(
                                QKVS[0:32, slot * 512:slot * 512 + 512], RT1[:, :]),
                                incs=[("qkev", 1)])
                            p.edge("vector")
                            qkev = p.val("qkev")
                        if role in ("q", "k"):
                            slot2 = tb % 2
                            p.op("scalar", lambda e, slot=slot, slot2=slot2: e.activation(
                                SQB[:, slot2 * 512:(slot2 + 1) * 512],
                                QKVS[:, slot * 512:(slot + 1) * 512], AF.Square),
                                waits=[("qkev", qkev)] + sl_sqb.wait_free(),
                                incs=[("act_nq", 1)])
                            sl_sqb.wrote()
                            p.op("tensor", lambda e, slot2=slot2: e.matmul(
                                PROW[0:1, :], ONE_B[:, 0:1],
                                SQB[:, slot2 * 512:(slot2 + 1) * 512],
                                start=True, stop=True),
                                waits=[("act_nq", p.val("act_nq"))] + sl_prow.wait_free(),
                                incs=[sl_sqb.consumed(), ("pe_nrm", 1)])
                            sl_prow.wrote()
                            p.op("vector", lambda e: e.tensor_copy(NROW[0:1, :], PROW[0:1, :]),
                                 waits=[("pe_nrm", p.val("pe_nrm")), ("nst", p.val("nst"))],
                                 incs=[sl_prow.consumed(), ("dve_nrm", 1)])
                            p.edge("vector")
                            row = 1 if role == "q" else 0
                            off = row * (NH * TOK) + fc * TOK + tb * 512
                            p.op("scalar", lambda e, off=off: e.dma_start(
                                out=_ap(nsh, off, [[1, 1], [1, 512]]), in_=NROW[0:1, :]),
                                waits=[("dve_nrm", p.val("dve_nrm"))], incs=[("nst", 16)])
                        if role == "v":
                            vslot = tb % 2
                            for u in range(4):
                                wv_ = [("qkev", qkev)] if u == 0 else []
                                p.op("tensor", lambda e, slot=slot, u=u: e.transpose(
                                    PTR[:, :],
                                    QKVS[:, slot * 512 + u * 128: slot * 512 + u * 128 + 128],
                                    IDENT[:]),
                                    waits=wv_ + sl_ptr.wait_free()
                                          + (sl_vtst.wait_free() if u == 0 else []),
                                    incs=[("pe_vt", 1)]
                                         + ([sl_qkvs.consumed()] if u == 3 else []))
                                sl_ptr.wrote()
                                if u == 0:
                                    sl_vtst.wrote()
                                p.op("vector", lambda e, vslot=vslot, u=u: e.tensor_copy(
                                    VTST[:, vslot * 512 + u * 128: vslot * 512 + u * 128 + 128],
                                    PTR[:, :]),
                                    waits=[("pe_vt", p.val("pe_vt"))],
                                    incs=[sl_ptr.consumed(), ("dve_vt", 1)])
                                p.edge("vector")
                            vw = vdst.shape[1]
                            dims = [[vw, 128], [128 * vw, 4], [1, 128]]
                            dap = _ap(vdst, (tb * 512) * vw + fc * 128, dims)
                            sap = VTST[:, vslot * 512: vslot * 512 + 512].rearrange(
                                "p (u d) -> p u d", u=4)
                            p.op("sync", lambda e, dap=dap, sap=sap: e.dma_start(
                                out=dap, in_=sap),
                                waits=[("dve_vt", p.val("dve_vt"))],
                                incs=[sl_vtst.consumed(16)])
                        else:
                            dst = qsh if role == "q" else kdst
                            dims = [[dst.shape[1], 128], [1, 512]]
                            dap = _ap(dst, (fc * 128) * dst.shape[1] + tb * 512, dims)
                            p.op("sync", lambda e, dap=dap, slot=slot: e.dma_start(
                                out=dap, in_=QKVS[:, slot * 512:(slot + 1) * 512]),
                                waits=[("qkev", qkev)], incs=[sl_qkvs.consumed(16)])
        return (sl_qkvs.done_waits() + sl_vtst.done_waits() + [("nst", p.val("nst"))])

    def attn_phase(kind, cc_tgt):
        st = p.stash
        if kind == "a":
            exts = [
                (kloc, "hK", 0, [[TOK, 512], [D * TOK, 4], [1, TOK]]),
                (qloc, "hK", 0, [[TOK, 512], [D * TOK, 4], [1, TOK]], qga),
                (vloc, "hV", 0, [[D, 4 * TOK], [1, 512]]),
                (nlock, "hN", 0, [[2 * NH * TOK, 4], [TOK, 4], [1, TOK]]),
                (nlocq, "hN", NH * TOK, [[2 * NH * TOK, 4], [TOK, 4], [1, TOK]]),
            ]
            srcs = {id(kloc): kga, id(vloc): vga, id(nlock): nga, id(nlocq): nga}
        else:
            exts = [
                (kloc, "hKb", 0, [[TOK, 128], [512 * TOK, 4], [1, TOK]]),
                (qloc, "hK", 0, [[TOK, 512], [D * TOK, 4], [1, TOK]], qga),
                (vloc, "hVb", 0, [[512, 4 * TOK], [1, 128]]),
                (nlock, "hNk", 0, [[2 * NH * TOK, 4], [1, TOK]]),
                (nlocq, "hN", NH * TOK, [[2 * NH * TOK, 4], [TOK, 4], [1, TOK]]),
            ]
            srcs = {id(kloc): kgab, id(vloc): vgab, id(nlock): nga, id(nlocq): nga}
        for ei, ext in enumerate(exts):
            dst, nm, cadd, dims = ext[0], ext[1], ext[2], ext[3]
            srct = ext[4] if len(ext) > 4 else srcs[id(dst)]
            nelem = 1
            for (_, c) in dims:
                nelem *= c
            dstap = _ap(dst, 0, [[1, nelem]])
            p.op("sync", lambda e, dstap=dstap, srct=srct, nm=nm, cadd=cadd, dims=dims:
                 e.dma_start(out=dstap,
                             in_=APClass(srct.tensor, dyn(e, nm, cadd), dims)),
                 waits=[("cc", cc_tgt)] + (sl_ksb.wait_free() if ei == 0 else []),
                 incs=[("exld", 16)])
            if ei == 0:
                sl_ksb.wrote()
        exld = p.val("exld")

        for h in range(HPC):
            for (kind2, cell, nl) in (("k", 0, nlock), ("q", 1, nlocq)):
                nrb = 4
                for rb2 in range(8):
                    rb = rb2 // 2
                    if kind == "b" and kind2 == "k":
                        roff = rb * TOK
                    else:
                        roff = (rb * 4 + h) * TOK
                    roff += (rb2 % 2) * 1024
                    p.op("sync", lambda e, roff=roff, nl=nl: e.dma_start(
                            out=NMAX[0:1, :],
                            in_=_ap(nl, roff, [[1, 1], [1, 1024]])),
                        waits=[("exld", exld)] + sl_nmax.wait_free(),
                        incs=[("nmld", 16)])
                    sl_nmax.wrote()
                    p.op("vector", lambda e, rb2=rb2: e.reduce_max(
                        STAT[0:1, 32 + rb2: 33 + rb2], NMAX[0:1, :], AX),
                        waits=[("nmld", p.val("nmld"))],
                        incs=[sl_nmax.consumed(), ("dve_c", 1)])
                    p.edge("vector")
                for (a, b, c) in ((32, 33, 40), (34, 35, 41), (36, 37, 42),
                                  (38, 39, 43), (40, 41, 44), (42, 43, 45)):
                    p.op("vector", lambda e, a=a, b=b, c=c: e.tensor_max(
                        STAT[0:1, c:c + 1], STAT[0:1, a:a + 1], STAT[0:1, b:b + 1]))
                    p.edge("vector")
                p.op("vector", lambda e, cell=cell, h=h: e.tensor_max(
                    STAT[0:1, h * 8 + cell: h * 8 + cell + 1],
                    STAT[0:1, 44:45], STAT[0:1, 45:46]),
                    incs=[("dve_c", 1)])
                p.edge("vector")
            p.op("vector", lambda e, h=h: e.tensor_tensor(
                STAT[0:1, h * 8 + 2: h * 8 + 3], STAT[0:1, h * 8: h * 8 + 1],
                STAT[0:1, h * 8 + 1: h * 8 + 2], ALU.mult), incs=[("dve_c", 1)])
            p.edge("vector")
            p.op("scalar", lambda e, h=h: e.activation(
                STAT[0:1, h * 8 + 3: h * 8 + 4], STAT[0:1, h * 8 + 2: h * 8 + 3], AF.Ln),
                waits=[("dve_c", p.val("dve_c"))], incs=[("act_c", 1)])
            p.edge("scalar")
            p.op("vector", lambda e, h=h: e.tensor_scalar(
                STAT[0:1, h * 8 + 3: h * 8 + 4], STAT[0:1, h * 8 + 3: h * 8 + 4],
                0.5, None, ALU.mult),
                waits=[("act_c", p.val("act_c"))], incs=[("dve_c", 1)])
            p.edge("vector")
            p.op("scalar", lambda e, h=h: e.activation(
                STAT[0:1, h * 8 + 4: h * 8 + 5], STAT[0:1, h * 8 + 3: h * 8 + 4],
                AF.Exp), waits=[("dve_c", p.val("dve_c"))], incs=[("act_c", 1)])
            p.edge("scalar")
            p.op("vector", lambda e, h=h: e.tensor_scalar(
                STAT[0:1, h * 8 + 5: h * 8 + 6], STAT[0:1, h * 8 + 4: h * 8 + 5],
                -1.0, None, ALU.mult),
                waits=[("act_c", p.val("act_c"))], incs=[("dve_c", 1)])
            p.edge("vector")
            p.op("tensor", lambda e, h=h: e.matmul(
                PBC[:, 0:1], ONE_RF[0:1, :], STAT[0:1, h * 8 + 5: h * 8 + 6],
                start=True, stop=True),
                waits=[("dve_c", p.val("dve_c"))] + sl_pbc.wait_free(),
                incs=[("pe_cb", 1)])
            sl_pbc.wrote()
            p.op("vector", lambda e, h=h: e.tensor_copy(STCOL[:, h:h + 1], PBC[:, 0:1]),
                 waits=[("pe_cb", p.val("pe_cb"))],
                 incs=[sl_pbc.consumed(), ("cready", 1)])
            p.edge("vector")
        cready = p.val("cready")

        for h in range(HPC):
            hl = h if kind == "a" else 0
            if kind == "a" or h == 0:
                p.op("sync", lambda e, hl=hl: e.dma_start(
                    out=KSB[:, :],
                    in_=_ap(kloc, (hl * 128) * SEQ, [[SEQ, 128], [1, SEQ]])),
                    waits=[("exld", exld)] + sl_kst.wait_free(),
                    incs=[("kld", 16)])
                sl_kst.wrote()
                vw = 512 if kind == "a" else 128
                p.op("sync", lambda e, hl=hl, vw=vw: e.dma_start(
                    out=VSB[:, :],
                    in_=_ap(vloc, hl * 128, [[vw, 128], [128 * vw, 64], [1, 128]])),
                    incs=[("kld", 16)])
            kld = p.val("kld")
            for qb in range(16):
                p.op("sync", lambda e, qb=qb, h=h: e.dma_start(
                    out=QTL[:, :],
                    in_=_ap(qloc, (h * 128) * SEQ + qb * 512, [[SEQ, 128], [1, 512]])),
                    waits=sl_qtl.wait_free(), incs=[("qld", 16)])
                sl_qtl.wrote()
                qld = p.val("qld")
                nkt = 4 * qb + 4
                for kt in range(nkt):
                    k2 = kt % 2
                    PSX, slx, psn = ((PS0, sl_psA, "psA") if k2 == 0
                                     else (PS1, sl_psB, "psB"))
                    w = slx.wait_free()
                    if kt == 0:
                        w += [("qld", qld)]
                        if qb == 0:
                            w += [("kld", kld), ("cready", cready)]
                    p.op("tensor", lambda e, kt=kt, PSX=PSX: e.matmul(
                        PSX[:, :], KSB[:, kt * 128:(kt + 1) * 128], QTL[:, :],
                        start=True, stop=True),
                        waits=w, incs=[("qk", 1)]
                             + ([sl_qtl.consumed()] if kt == nkt - 1 else []))
                    slx.wrote()
                    qk = p.val("qk")
                    if kind == "a":
                        bias = STCOL[:, h:h + 1]
                        exw = [("qk", qk)]
                    else:
                        bslot = 4 + (kt % 2)
                        dlt = float(kt * 128 - qb * 512)
                        p.op("vector", lambda e, h=h, bslot=bslot, dlt=dlt: e.tensor_scalar(
                            STCOL[:, bslot:bslot + 1], SLOPE[:, h:h + 1],
                            dlt, STCOL[:, h:h + 1], ALU.mult, ALU.add),
                            waits=sl_bcol.wait_free())
                        sl_bcol.wrote()
                        p.edge("vector")
                        p.op("vector", lambda e, h=h, PSX=PSX: e.scalar_tensor_tensor(
                            PSX[:, :], RELG[:, :], SLOPE[:, h:h + 1], PSX[:, :],
                            ALU.mult, ALU.add),
                            waits=[("qk", qk)], incs=[("ali", 1)])
                        p.edge("vector")
                        bias = STCOL[:, bslot:bslot + 1]
                        exw = [("ali", p.val("ali"))]
                    p.op("scalar", lambda e, kt=kt, PSX=PSX, bias=bias: e.activation(
                        PT[:, (kt % 2) * 512:(kt % 2) * 512 + 512], PSX[:, :],
                        AF.Exp, bias=bias),
                        waits=exw + sl_pt.wait_free(),
                        incs=[("ex", 1), slx.consumed()]
                             + ([sl_bcol.consumed()] if kind == "b" else []))
                    sl_pt.wrote()
                    ex = p.val("ex")
                    diag = (kt >= 4 * qb)
                    if diag:
                        base = qb * 512 - kt * 128
                        p.op("gpsimd", lambda e, kt=kt, base=base: e.affine_select(
                            PT[:, (kt % 2) * 512:(kt % 2) * 512 + 512],
                            PT[:, (kt % 2) * 512:(kt % 2) * 512 + 512],
                            [[1, 512]], ALU.is_ge, 0.0, base=base,
                            channel_multiplier=-1),
                            waits=[("ex", ex)], incs=[("msk", 1)])
                        wsum = [("msk", p.val("msk"))]
                    else:
                        wsum = [("ex", ex)]
                    if kt == 0:
                        wsum += [("dve_rc", p.val("dve_rc")), ("dve_ao", p.val("dve_ao"))]
                    p.op("tensor", lambda e, kt=kt, nkt=nkt: e.matmul(
                        PROW[0:1, :], ONE_B[:, 0:1],
                        PT[:, (kt % 2) * 512:(kt % 2) * 512 + 512],
                        start=(kt == 0), stop=(kt == nkt - 1)),
                        waits=wsum + (sl_prow.wait_free() if kt == 0 else []))
                    if kt == 0:
                        sl_prow.wrote()
                    last_stripe = (kt == nkt - 1 and qb == 15 and
                                   (kind == "a" or h == HPC - 1))
                    p.op("tensor", lambda e, kt=kt, nkt=nkt: e.matmul(
                        PS2[:, :], VSB[:, kt * 128:(kt + 1) * 128],
                        PT[:, (kt % 2) * 512:(kt % 2) * 512 + 512],
                        start=(kt == 0), stop=(kt == nkt - 1)),
                        incs=[("pv", 1), sl_pt.consumed()] +
                             ([("pe_q", 1)] if kt == nkt - 1 else []) +
                             ([sl_kst.consumed()] if last_stripe else []))
                petgt = p.val("pe_q")
                p.op("vector", lambda e: e.reciprocal(RECR, PROW[0:1, :]),
                     waits=[("pe_q", petgt)],
                     incs=[sl_prow.consumed(), ("dve_rc", 1)])
                p.edge("vector")
                p.op("tensor", lambda e: e.matmul(
                    PBC[:, :], ONE_RF[0:1, :], RECR, start=True, stop=True),
                    waits=[("dve_rc", p.val("dve_rc"))] + sl_pbc.wait_free(),
                    incs=[("pe_rb", 1)])
                sl_pbc.wrote()
                p.op("vector", lambda e: e.tensor_copy(BCA[:], PBC[:, :]),
                     waits=[("pe_rb", p.val("pe_rb"))],
                     incs=[sl_pbc.consumed()])
                p.edge("vector")
                p.op("vector", lambda e: e.tensor_tensor(
                    AOST[:, :], PS2[:, :], BCA[:], ALU.mult),
                    waits=sl_aost.wait_free(), incs=[("dve_ao", 1)])
                sl_aost.wrote()
                p.edge("vector")
                p.op("sync", lambda e, h=h, qb=qb: e.dma_start(
                    out=_ap(aosh, (h * 128) * SEQ + qb * 512, [[SEQ, 128], [1, 512]]),
                    in_=AOST[:, :]),
                    waits=[("dve_ao", p.val("dve_ao"))], incs=[sl_aost.consumed(16)])
            if h == HPC - 1:
                p.op("tensor", None, incs=[sl_ksb.consumed()])
        return sl_aost.done_waits()

    def outproj_phase(wsrcT, rold, rdst, cc_tgt):
        st = p.stash
        p.op("sync", lambda e: e.dma_start(
            out=_ap(aoloc, 0, [[1, D * TOK]]),
            in_=APClass(aoga.tensor, dyn(e, "tl", 0), [[SEQ, D], [1, TOK]])),
            waits=[("cc", cc_tgt)] + sl_aol.wait_free(), incs=[("exld", 16)])
        sl_aol.wrote()
        aoex = p.val("exld")
        for tb in range(4):
            dims = [[TOK, 128], [128 * TOK, 16], [1, 512]]
            lastld = (tb == 3)
            p.op("sync", lambda e, tb=tb, dims=dims: e.dma_start(
                out=XLN[:, 0:16 * 512],
                in_=_ap(aoloc, tb * 512, dims)),
                waits=[("exld", aoex)] + sl_xln.wait_free(),
                incs=[("aold", 16)] + ([sl_aol.consumed(16)] if lastld else []))
            sl_xln.wrote()
            aold = p.val("aold")
            for cb in range(4):
                wtgt = convert_wblock(wsrcT, cb)
                for csub in range(4):
                    cc = cb * 4 + csub
                    k2 = cc % 2
                    PSX, slx, psn = ((PS0, sl_psA, "psA") if k2 == 0
                                     else (PS1, sl_psB, "psB"))
                    rsrc = _ap(rold, (cc * 128) * TOK + tb * 512, [[TOK, 128], [1, 512]])
                    rslot = cc % 2
                    p.op("sync", lambda e, rsrc=rsrc, rslot=rslot: e.dma_start(
                        out=RSC[:, rslot * 512:(rslot + 1) * 512], in_=rsrc),
                        waits=sl_rsc.wait_free(), incs=[sl_rsc.prod_inc()])
                    sl_rsc.wrote()
                    rcw = sl_rsc.prod_wait()
                    for ch in range(16):
                        w = ([("wconv", wtgt), ("aold", aold)] + slx.wait_free()) \
                            if ch == 0 else []
                        last_xln = (cb == 3 and csub == 3 and ch == 15)
                        last_wbf = (csub == 3 and ch == 15)
                        p.op("tensor", lambda e, ch=ch, csub=csub, PSX=PSX: e.matmul(
                            PSX[:, :],
                            WBF[:, ch * 512 + csub * 128: ch * 512 + csub * 128 + 128],
                            XLN[:, ch * 512:(ch + 1) * 512],
                            start=(ch == 0), stop=(ch == 15)),
                            waits=w,
                            incs=([(psn, 1)] if ch == 15 else [])
                                 + ([sl_xln.consumed()] if last_xln else [])
                                 + ([sl_wbf.consumed()] if last_wbf else []))
                        if ch == 0:
                            slx.wrote()
                    p.op("vector", lambda e, rslot=rslot, PSX=PSX: e.tensor_tensor(
                        FSD[:, rslot * 512:(rslot + 1) * 512], PSX[:, :],
                        RSC[:, rslot * 512:(rslot + 1) * 512], ALU.add),
                        waits=[(psn, p.val(psn))] + rcw + sl_fsd.wait_free(),
                        incs=[("dve_op", 1), slx.consumed(), sl_rsc.consumed()])
                    sl_fsd.wrote()
                    p.edge("vector")
                    dims3 = [[TOK, 128], [1, 512]]
                    p.op("sync", lambda e, cc=cc, tb=tb, rdst=rdst, rslot=rslot: e.dma_start(
                        out=_ap(rdst, (cc * 128) * TOK + tb * 512, dims3),
                        in_=FSD[:, rslot * 512:(rslot + 1) * 512]),
                        waits=[("dve_op", p.val("dve_op"))],
                        incs=[sl_fsd.consumed(16)])
        return sl_fsd.done_waits()

    def ffn_phase(rold, rdst, wg_tgt, xln_tgt, rold_done=()):
        p.op("sync", lambda e, rold=rold, rdst=rdst: e.dma_start(
            out=rdst[:], in_=rold[:]), waits=list(rold_done),
            incs=[("ffinit", 16)])
        ffinit = p.val("ffinit")
        for sup in range(16):
            for hc in range(16):
                p.op("sync", lambda e, sup=sup, hc=hc: e.dma_start(
                    out=FWG[:, hc * 512:(hc + 1) * 512],
                    in_=_ap(wg_f, (sup * 512) * D + hc * 128, [[D, 512], [1, 128]]),
                    transpose=True),
                    waits=([("wgath", wg_tgt)] + sl_fwg.wait_free()) if hc == 0 else [],
                    incs=[("fwld", 16)])
                if hc == 0:
                    sl_fwg.wrote()
                p.op("sync", lambda e, sup=sup, hc=hc: e.dma_start(
                    out=FWU[:, hc * 512:(hc + 1) * 512],
                    in_=_ap(wu_f, (sup * 512) * D + hc * 128, [[D, 512], [1, 128]]),
                    transpose=True), incs=[("fwld", 16)])
            for oc in range(4):
                for cb in range(4):
                    p.op("sync", lambda e, sup=sup, oc=oc, cb=cb: e.dma_start(
                        out=FWD[:, (oc * 4 + cb) * 512:(oc * 4 + cb) * 512 + 512],
                        in_=_ap(wd_f, (cb * 512) * INTER + sup * 512 + oc * 128,
                                [[INTER, 512], [1, 128]]),
                        transpose=True), incs=[("fwld", 16)])
            fwld = p.val("fwld")
            for tb in range(4):
                for oc in range(4):
                    for ch in range(16):
                        w = []
                        if ch == 0:
                            w = ([("fwld", fwld), ("xlnorm", xln_tgt)]
                                 + sl_psC.wait_free() + sl_psD.wait_free())
                        p.op("tensor", lambda e, ch=ch, oc=oc, tb=tb: e.matmul(
                            PS2[:, :],
                            FWG[:, ch * 512 + oc * 128: ch * 512 + oc * 128 + 128],
                            XLN[:, ch * TOK + tb * 512: ch * TOK + tb * 512 + 512],
                            start=(ch == 0), stop=(ch == 15)),
                            waits=w, incs=([("ffg", 1)] if ch == 15 else []))
                        p.op("tensor", lambda e, ch=ch, oc=oc, tb=tb: e.matmul(
                            PS3[:, :],
                            FWU[:, ch * 512 + oc * 128: ch * 512 + oc * 128 + 128],
                            XLN[:, ch * TOK + tb * 512: ch * TOK + tb * 512 + 512],
                            start=(ch == 0), stop=(ch == 15)),
                            incs=([("ffg", 1)] if ch == 15 else []))
                        if ch == 0:
                            sl_psC.wrote()
                            sl_psD.wrote()
                    ffg = p.val("ffg")
                    fslot = oc % 2
                    wsilu = [("ffg", ffg)]
                    if p.val("ffin") >= 1:
                        wsilu.append(("ffin", p.val("ffin") - 1))
                    p.op("scalar", lambda e, fslot=fslot: e.activation(
                        FSG[:, fslot * 512:(fslot + 1) * 512], PS2[:, :], AF.Silu),
                        waits=wsilu, incs=[("ffs", 1), sl_psC.consumed()])
                    wfin = sl_fin.wait_free() if oc == 0 else []
                    p.op("vector", lambda e, oc=oc, fslot=fslot: e.tensor_tensor(
                        FIN[:, oc * 512:(oc + 1) * 512],
                        FSG[:, fslot * 512:(fslot + 1) * 512], PS3[:, :], ALU.mult),
                        waits=[("ffs", p.val("ffs"))] + wfin,
                        incs=[("ffin", 1), sl_psD.consumed()])
                    if oc == 0:
                        sl_fin.wrote()
                    p.edge("vector")
                ffin = p.val("ffin")
                for cc in range(16):
                    k2 = cc % 2
                    PSX, slx, psn = ((PS0, sl_psA, "psA") if k2 == 0
                                     else (PS1, sl_psB, "psB"))
                    for oc in range(4):
                        w = ([("ffin", ffin)] + slx.wait_free()) if oc == 0 else []
                        p.op("tensor", lambda e, oc=oc, cc=cc, PSX=PSX: e.matmul(
                            PSX[:, :],
                            FWD[:, (oc * 4 + cc // 4) * 512 + (cc % 4) * 128:
                                   (oc * 4 + cc // 4) * 512 + (cc % 4) * 128 + 128],
                            FIN[:, oc * 512:(oc + 1) * 512],
                            start=(oc == 0), stop=(oc == 3)),
                            waits=w,
                            incs=([(psn, 1)] if oc == 3 else [])
                                 + ([sl_fin.consumed()] if (cc == 15 and oc == 3) else [])
                                 + ([sl_fwg.consumed()] if (tb == 3 and cc == 15 and oc == 3) else []))
                        if oc == 0:
                            slx.wrote()
                    p.op("vector", lambda e, cc=cc, PSX=PSX: e.tensor_copy(
                        FEV[:, (cc % 4) * 512:(cc % 4) * 512 + 512], PSX[:, :]),
                        waits=[(psn, p.val(psn))]
                              + (sl_fev.wait_free() if cc % 4 == 0 else []),
                        incs=[("ffev", 1), slx.consumed()])
                    if cc % 4 == 0:
                        sl_fev.wrote()
                    p.edge("vector")
                    if cc % 4 == 3:
                        cg = cc // 4
                        ffev = p.val("ffev")
                        dims = [[TOK, 128], [128 * TOK, 4], [1, 512]]
                        wacc = [("ffev", ffev)]
                        if sup == 0:
                            wacc.append(("ffinit", ffinit))
                        p.op("gpsimd", lambda e, tb=tb, cg=cg, rdst=rdst: e.dma_start(
                            out=_ap(rdst, (cg * 4 * 128) * TOK + tb * 512, dims),
                            in_=FEV[:].rearrange("p (c t) -> p c t", c=4),
                            accum_op=ALU.add),
                            waits=wacc, incs=[sl_fev.consumed(16)])
        return sl_fev.done_waits()

    wg_tgt = spline_phase() if "spline" not in dev_skip else 0

    xt = ln_phase(residA, waits0=[("rst", 16)]) if "ln" not in dev_skip else 0
    qkv_done = qkv_phase("a", xt) if "qkv" not in dev_skip else []
    for (i_, o_) in ((qsh, qga), (ksh, kga), (vsh, vga), (nsh, nga)):
        p.op("gpsimd", lambda e, i_=i_, o_=o_: e.collective_compute(
            "AllGather", ALU.bypass, replica_groups=G4,
            ins=[i_.opt()], outs=[o_.opt()]),
            waits=qkv_done, incs=[("cc", 1)])
    ao_done = attn_phase("a", p.val("cc")) if "attn" not in dev_skip else []
    p.op("gpsimd", lambda e: e.collective_compute(
        "AllGather", ALU.bypass, replica_groups=G4,
        ins=[aosh.opt()], outs=[aoga.opt()]),
        waits=ao_done, incs=[("cc", 1)])
    cc_ao = p.val("cc")
    p.barrier()
    op1_done = []
    if "outproj" not in dev_skip:
        op1_done = outproj_phase(woutT, residA, residB, cc_ao)
    p.barrier()

    ffn1_done = None
    if n_sub >= 2:
        xt2 = ln_phase(residB, waits0=op1_done)
        ffn1_done = ffn_phase(residB, residA, wg_tgt, xt2, op1_done)
        p.barrier()
    if n_sub >= 3:
        xt3 = ln_phase(residA, waits0=ffn1_done)
        qkv_done3 = qkv_phase("b", xt3)
        for (i_, o_) in ((qsh, qga), (kshb, kgab), (vshb, vgab), (nsh, nga)):
            p.op("gpsimd", lambda e, i_=i_, o_=o_: e.collective_compute(
                "AllGather", ALU.bypass, replica_groups=G4,
                ins=[i_.opt()], outs=[o_.opt()]),
                waits=qkv_done3, incs=[("cc", 1)])
        ao_done3 = attn_phase("b", p.val("cc"))
        p.op("gpsimd", lambda e: e.collective_compute(
            "AllGather", ALU.bypass, replica_groups=G4,
            ins=[aosh.opt()], outs=[aoga.opt()]),
            waits=ao_done3, incs=[("cc", 1)])
        cc_ao = p.val("cc")
        p.barrier()
        op2_done = outproj_phase(wo, residA, residB, cc_ao)
        p.barrier()
    if n_sub >= 4:
        xt4 = ln_phase(residB, waits0=op2_done)
        ffn2_done = ffn_phase(residB, out_t, wg_tgt, xt4, op2_done)
        p.op("sync", None, waits=ffn2_done)
    else:
        final = {0: residA, 1: residB, 2: residA, 3: residB}[n_sub]
        wfin = ([("rst", p.val("rst"))] if n_sub == 0 else
                op1_done if n_sub == 1 else
                op2_done if n_sub == 3 else ffn1_done)
        p.op("sync", lambda e, final=final: e.dma_start(out=out_t[:], in_=final[:]),
             waits=wfin, incs=[("outst", 16)])
        p.op("sync", None, waits=[("outst", p.val("outst"))])

    if dbg:
        hold = {"qsh": qsh, "ksh": ksh, "vsh": vsh, "nsh": nsh, "qga": qga,
                "kga": kga, "vga": vga, "nga": nga, "aosh": aosh, "aoga": aoga,
                "kloc": kloc, "vloc": vloc, "qloc": qloc, "aoloc": aoloc,
                "nlock": nlock, "nlocq": nlocq,
                "kshb": kshb, "vshb": vshb, "kgab": kgab, "vgab": vgab,
                "wg_f": wg_f, "wu_f": wu_f, "wd_f": wd_f,
                "residA": residA, "residB": residB}
        sbufs = {"XLN": XLN, "STAT": STAT, "STCOL": STCOL, "BIASB": BIASB,
                 "COSS": COSS, "SINS": SINS}
        p.barrier()
        for nm in dbg:
            if nm in hold:
                t = hold[nm]
                o = nc.dram_tensor("dbg_" + nm, list(t.shape), t.dtype,
                                   kind="ExternalOutput").ap()
                p.op("sync", lambda e, o=o, t=t: e.dma_start(out=o, in_=t),
                     incs=[("dbgst", 16)])
            else:
                t = sbufs[nm]
                sh = list(t.shape) if hasattr(t, "shape") else None
                o = nc.dram_tensor("dbg_" + nm, sh, t.dtype,
                                   kind="ExternalOutput").ap()
                p.op("sync", lambda e, o=o, t=t: e.dma_start(out=o, in_=t[:]),
                     incs=[("dbgst", 16)])
        p.op("sync", None, waits=[("dbgst", p.val("dbgst"))])

    p.replay()
    es.close()
    return nc


_CACHE = {}


def kernel(L0_attn_a_x, L0_attn_a_w_qkv, L0_attn_a_w_out, L0_attn_a_g,
           L0_attn_b_Wq, L0_attn_b_Wk, L0_attn_b_Wv, L0_attn_b_Wo,
           L0_ffn_gate_cp, L0_ffn_up_cp, L0_ffn_down_cp, n_sub=NSUB, dbg=(), dev_skip=()):
    x = np.asarray(L0_attn_a_x, dtype=np.float32)
    if "grid" not in _CACHE:
        _CACHE["grid"] = _spline_chunk_grid()
    i0, f0 = _CACHE["grid"]
    swin_g = _swin(np.asarray(L0_ffn_gate_cp, np.float32), i0, f0)
    swin_u = _swin(np.asarray(L0_ffn_up_cp, np.float32), i0, f0)
    swin_d = _swin(np.asarray(L0_ffn_down_cp, np.float32), i0, f0)
    cos, sinS = _rotary_tables()
    slopes = _alibi_slopes()
    slopes = slopes[np.arange(NH) // 4]
    stile = np.ascontiguousarray(np.broadcast_to(
        (np.arange(128, dtype=np.float64) * SP_STEP).astype(np.float32)[None, :],
        (128, 128)))
    kgrid = (np.arange(64, dtype=np.float32)[None, :] * 128.0 +
             np.arange(128, dtype=np.float32)[:, None]).astype(np.float32)
    relg_h = (np.arange(128, dtype=np.float32)[:, None] -
              np.arange(512, dtype=np.float32)[None, :]).astype(np.float32)

    wqkvT_h = np.ascontiguousarray(np.asarray(L0_attn_a_w_qkv, np.float32).T)
    woutT_h = np.ascontiguousarray(np.asarray(L0_attn_a_w_out, np.float32).T)
    wq_h = np.ascontiguousarray(np.asarray(L0_attn_b_Wq, np.float32))
    wk_h = np.ascontiguousarray(np.asarray(L0_attn_b_Wk, np.float32))
    wv_h = np.ascontiguousarray(np.asarray(L0_attn_b_Wv, np.float32))
    wo_h = np.ascontiguousarray(np.asarray(L0_attn_b_Wo, np.float32))
    xt = np.ascontiguousarray(np.transpose(x, (0, 2, 1)))

    in_maps = []
    npc = NCH // R
    for r in range(R):
        b, q = r // 4, r % 4
        cinfo = np.zeros((1, 8), np.int32)
        cinfo[0, 0] = q * 512 * TOK
        cinfo[0, 1] = q * 512
        cinfo[0, 2] = q * 128 * TOK
        cinfo[0, 3] = q * 128
        cinfo[0, 4] = q * TOK
        cinfo[0, 5] = q * 4 * TOK
        cinfo[0, 6] = q * TOK
        tok0 = q * TOK
        in_maps.append({
            "x": np.ascontiguousarray(xt[b][:, tok0:tok0 + TOK]),
            "wqkvT": wqkvT_h, "woutT": woutT_h,
            "wq": wq_h, "wk": wk_h, "wv": wv_h, "wo": wo_h,
            "swing": np.ascontiguousarray(swin_g[r * npc:(r + 1) * npc]),
            "swinu": np.ascontiguousarray(swin_u[r * npc:(r + 1) * npc]),
            "swind": np.ascontiguousarray(swin_d[r * npc:(r + 1) * npc]),
            "cosT": np.ascontiguousarray(cos[:, tok0:tok0 + TOK]),
            "sinT": np.ascontiguousarray(sinS[:, tok0:tok0 + TOK]),
            "stile": stile,
            "kgrid": kgrid,
            "slopebc": np.ascontiguousarray(np.broadcast_to(
                slopes[q * 4:(q + 1) * 4][None, :], (128, HPC))),
            "relg": relg_h,
            "cinfo": cinfo,
        })

    nc = build(n_sub, dev_skip=dev_skip, dbg=dbg)
    res = run_bass_kernel_spmd(nc, in_maps, core_ids=list(range(R)))
    out = np.zeros((BATCH, SEQ, D), np.float32)
    for r in range(R):
        b, q = r // 4, r % 4
        out[b, q * TOK:(q + 1) * TOK, :] = res.results[r]["out"].T
    if dbg:
        dd = {nm: [np.asarray(res.results[r]["dbg_" + nm]) for r in range(R)]
              for nm in dbg}
        return out, dd
    return out


BATCH = 2


# version 04
# speedup vs baseline: 1.1560x; 1.1560x over previous
import math
from contextlib import ExitStack

import numpy as np
import ml_dtypes

import concourse.bass as bass
import concourse.mybir as mybir
from concourse.ap import AP as APClass
from concourse.bass_utils import run_bass_kernel_spmd

BFNP = ml_dtypes.bfloat16
F32 = mybir.dt.float32
BF16 = mybir.dt.bfloat16
I32 = mybir.dt.int32
U8 = mybir.dt.uint8
AF = mybir.ActivationFunctionType
ALU = mybir.AluOpType
AX = mybir.AxisListType.X

R = 8
D = 2048
SEQ = 8192
TOK = 2048
NH = 16
HD = 128
HPC = 4
NSUB = 4
QKV_F = 6144
INTER = 8192
SCALE_A = HD ** -0.5
ROT = 32
LN_EPS = 1e-5

NCTRL = 130156
NSAMP = 16777216
NCH = NSAMP // 128
SP_TILES = NCH // 128
SP_TPC = SP_TILES // R
SP_STEP = float(NCTRL - 1) / float(NSAMP - 1)

ENGS = ("sync", "tensor", "vector", "scalar", "gpsimd")
G4 = [[0, 1, 2, 3], [4, 5, 6, 7]]
G8 = [[0, 1, 2, 3, 4, 5, 6, 7]]


def _spline_chunk_grid():
    t = np.linspace(0.0, float(NCTRL - 1), NSAMP, dtype=np.float64)
    j0 = np.arange(NCH, dtype=np.int64) * 128
    t0 = t[j0]
    i0 = np.clip(np.floor(t0).astype(np.int64), 0, NCTRL - 2)
    f0 = (t0 - i0).astype(np.float32)
    return i0, f0


def _swin(cp, i0, f0):
    out = np.zeros((NCH, 8), np.float32)
    for k in range(-1, 4):
        out[:, k + 1] = cp[np.clip(i0 + k, 0, NCTRL - 1)]
    out[:, 5] = f0
    return out


def _rotary_tables():
    inv_freq = 1.0 / (10000.0 ** (np.arange(0, ROT, 2, dtype=np.float32) / ROT))
    freqs = np.arange(SEQ, dtype=np.float32)[:, None] * inv_freq[None, :]
    pos = np.concatenate([freqs, freqs], axis=-1)
    cos = np.ascontiguousarray(np.cos(pos).T.astype(np.float32))
    sin = np.sin(pos).T.astype(np.float32)
    sin[:16] *= -1.0
    return cos, np.ascontiguousarray(sin)


def _alibi_slopes():
    start = 2.0 ** (-(2.0 ** (-(math.log2(NH) - 3))))
    return np.array([start * (start ** i) for i in range(NH)], dtype=np.float32)


class Prog:
    def __init__(self, nc, es):
        self.nc = nc
        self.es = es
        self.ops = {e: [] for e in ENGS}
        self.sems = {}
        self.stash = {}

    ALIAS = {"act_sq": "acta", "act_nq": "acta", "act_ln": "acta",
             "act_c": "acta", "ffs": "acta",
             "dve_ln": "dvea", "dve_bc": "dvea", "dve_c": "dvea",
             "dve_nrm": "dvea", "dve_vt": "dvea", "dve_op": "dvea",
             "stat_done": "dvea", "dve_rc": "dvea", "dve_ao": "dvea",
             "ali": "dvea",
             "pe_ln": "pea", "pe_bc": "pea", "pe_nrm": "pea",
             "pe_cb": "pea", "pe_vt": "pea", "pe_rb": "pea",
             "qkev": "mix", "cready": "mix", "wcg": "mix", "cvt": "mix"}

    def sem(self, name):
        name = self.ALIAS.get(name, name)
        if name not in self.sems:
            h = self.es.enter_context(self.nc.semaphore(name))
            self.sems[name] = [h, 0]
        return self.sems[name]

    def val(self, name):
        return self.sem(self.ALIAS.get(name, name))[1]

    def op(self, eng, fn, waits=(), incs=()):
        wl = [(self.sem(n)[0], v) for (n, v) in waits if v > 0]
        il = []
        for (n, a) in incs:
            s = self.sem(n)
            s[1] += a
            il.append((s[0], a))
        if fn is None or len(il) <= 1:
            self.ops[eng].append((fn, wl, il))
            return
        first = il[0]
        self.ops[eng].append((fn, wl, [first]))
        self.ops[eng].append((None, [(first[0], self._cur(first[0]))], il[1:]))

    def _cur(self, handle):
        for name, s in self.sems.items():
            if s[0] is handle:
                return s[1]
        raise KeyError

    def edge(self, eng):
        fn, wl, il = self.ops[eng][-1]
        if il:
            self.ops[eng].append((None, [(il[0][0], self._cur(il[0][0]))], []))
            return
        assert fn is not None, "edge after a bare wait"
        name = "edge_" + eng
        s = self.sem(name)
        s[1] += 1
        il.append((s[0], 1))
        self.ops[eng][-1] = (fn, wl, il)
        self.ops[eng].append((None, [(s[0], s[1])], []))

    def barrier(self):
        for e in ENGS:
            self.op(e, None, incs=[("bar", 1)])
        tgt = self.val("bar")
        for e in ENGS:
            self.op(e, None, waits=[("bar", tgt)])

    def replay(self):
        nc = self.nc
        with nc.Block() as block:
            for ename in ENGS:
                ops = self.ops[ename]

                def body(eng, ops=ops):
                    for fn, wl, il in ops:
                        for (h, v) in wl:
                            eng.wait_ge(h, v)
                        if fn is None:
                            for (h, a) in il:
                                eng.sem_inc(h, a)
                        else:
                            inst = fn(eng)
                            for (h, a) in il:
                                inst = inst.then_inc(h, a)

                getattr(block, ename)(body)


class Slots:

    def __init__(self, p, name, nslots):
        self.p = p
        self.name = name
        self.n = nslots
        self.w = 0
        self.c = 0
        self.ccum = [0] * nslots
        self.crel = [[] for _ in range(nslots)]
        self.pcum = [0] * nslots

    def wait_free(self):
        i = self.w
        if i < self.n:
            return []
        sl = i % self.n
        k = i // self.n - 1
        assert len(self.crel[sl]) > k, f"slot {self.name}: missing consumed()"
        self.p.sem(f"{self.name}_c{sl}")
        return [(f"{self.name}_c{sl}", self.crel[sl][k])]

    def prod_inc(self, amount=16):
        sl = self.w % self.n
        self.pcum[sl] += amount
        self.p.sem(f"{self.name}_p{sl}")
        return (f"{self.name}_p{sl}", amount)

    def prod_wait(self):
        sl = (self.w - 1) % self.n
        return [(f"{self.name}_p{sl}", self.pcum[sl])]

    def wrote(self):
        self.w += 1

    def consumed(self, amount=1):
        sl = self.c % self.n
        self.c += 1
        self.ccum[sl] += amount
        self.crel[sl].append(self.ccum[sl])
        self.p.sem(f"{self.name}_c{sl}")
        return (f"{self.name}_c{sl}", amount)

    def done_waits(self):
        return [(f"{self.name}_c{i}", self.ccum[i])
                for i in range(self.n) if self.ccum[i] > 0]


def _ap(t, offset, dims):
    return APClass(t.tensor, offset, [list(d) for d in dims])


def build(n_sub=NSUB, dev_skip=(), dbg=()):
    nc = bass.Bass("TRN2", target_bir_lowering=False)
    es = ExitStack()
    p = Prog(nc, es)

    def ein(name, shape, dt=F32):
        return nc.dram_tensor(name, shape, dt, kind="ExternalInput").ap()

    x_in = ein("x", [D, TOK])
    wqkvT = ein("wqkvT", [D, QKV_F])
    woutT = ein("woutT", [D, D])
    wq = ein("wq", [D, D])
    wk = ein("wk", [D, 512])
    wv = ein("wv", [D, 512])
    wo = ein("wo", [D, D])
    swing = ein("swing", [NCH // R, 8])
    swinu = ein("swinu", [NCH // R, 8])
    swind = ein("swind", [NCH // R, 8])
    cosT = ein("cosT", [ROT, TOK])
    sinT = ein("sinT", [ROT, TOK])
    stile = ein("stile", [128, 128])
    kgrid = ein("kgrid", [128, 64])
    slopebc = ein("slopebc", [128, HPC])
    relg = ein("relg", [128, 512])
    cinfo = ein("cinfo", [1, 8], I32)

    out_t = nc.dram_tensor("out", [D, TOK], F32, kind="ExternalOutput").ap()

    def idram(name, shape, dt=F32):
        return nc.dram_tensor(name, shape, dt, kind="Internal").ap()

    residA = idram("residA", [D, TOK])
    residB = idram("residB", [D, TOK])
    qsh = idram("qsh", [D, TOK], BF16)
    ksh = idram("ksh", [D, TOK], BF16)
    vsh = idram("vsh", [TOK, D], BF16)
    nsh = idram("nsh", [2, NH * TOK])
    qga = idram("qga", [4 * D, TOK], BF16)
    kga = idram("kga", [4 * D, TOK], BF16)
    vga = idram("vga", [4 * TOK, D], BF16)
    nga = idram("nga", [8, NH * TOK])
    aosh = idram("aosh", [HPC * HD, SEQ], BF16)
    aoga = idram("aoga", [4 * HPC * HD, SEQ], BF16)
    kshb = idram("kshb", [512, TOK], BF16)
    vshb = idram("vshb", [TOK, 512], BF16)
    kgab = idram("kgab", [4 * 512, TOK], BF16)
    vgab = idram("vgab", [4 * TOK, 512], BF16)
    wqkvB = idram("wqkvB", [D, QKV_F], BF16)
    woutB = idram("woutB", [D, D], BF16)
    wqB = idram("wqB", [D, D], BF16)
    wkB = idram("wkB", [D, 512], BF16)
    wvB = idram("wvB", [D, 512], BF16)
    woB = idram("woB", [D, D], BF16)
    kloc = idram("kloc", [512, SEQ], BF16)
    vloc = idram("vloc", [SEQ, 512], BF16)
    qloc = idram("qloc", [512, SEQ], BF16)
    aoloc = idram("aoloc", [D, TOK], BF16)
    nlock = idram("nlock", [16, TOK])
    nlocq = idram("nlocq", [16, TOK])
    wg_sh = idram("wg_sh", [NSAMP // R, 1], BF16)
    wu_sh = idram("wu_sh", [NSAMP // R, 1], BF16)
    wd_sh = idram("wd_sh", [NSAMP // R, 1], BF16)
    wg_f = idram("wg_f", [INTER, D], BF16)
    wu_f = idram("wu_f", [INTER, D], BF16)
    wd_f = idram("wd_f", [D, INTER], BF16)

    def sbuf(name, shape, dt):
        return es.enter_context(nc.sbuf_tensor(name, shape, dt))

    XLN = sbuf("XLN", [128, 16 * TOK], BF16)
    LNRS = sbuf("LNRS", [128, 2 * 512], F32)
    RSC = sbuf("RSC", [128, 2 * 512], F32)
    FEV = sbuf("FEV", [128, 4 * 512], F32)
    WCV = sbuf("WCV", [128, 2 * 512], F32)
    WBF = sbuf("WBF", [128, 16 * 512], BF16)
    QKVS = sbuf("QKVS", [128, 2 * 512], BF16)
    VTST = sbuf("VTST", [128, 2 * 512], BF16)
    SHPRE = sbuf("SHPRE", [32, 512], F32)
    SH32 = sbuf("SH32", [32, 512], F32)
    RT1 = sbuf("RT1", [32, 512], F32)
    RT2 = sbuf("RT2", [32, 512], F32)
    COSS = sbuf("COSS", [ROT, TOK], BF16)
    SINS = sbuf("SINS", [ROT, TOK], BF16)
    SQB = sbuf("SQB", [128, 2 * 512], BF16)
    ROWS = sbuf("ROWS", [1, 4 * 512], F32)
    NROW = ROWS[0:1, 512:1024]
    BCA = sbuf("BCA", [128, 512], F32)
    BCB = sbuf("BCB", [128, 512], F32)
    ONE_B = sbuf("ONE_B", [128, 1], BF16)
    ONE_F = sbuf("ONE_F", [128, 1], F32)
    ONE_RF = sbuf("ONE_RF", [1, 128], F32)
    IDENT = sbuf("IDENT", [128, 128], BF16)
    CINF = sbuf("CINF", [1, 8], I32)
    POOL = sbuf("POOL", [128, 26624], BF16)
    KSB = POOL[:, 0:SEQ]
    VSB = POOL[:, SEQ:2 * SEQ]
    QTL = POOL[:, 2 * SEQ:2 * SEQ + 512]
    PT = POOL[:, 2 * SEQ + 512:2 * SEQ + 512 + 1536]
    NMAX = sbuf("NMAX", [1, 512], F32)
    STAT = sbuf("STAT", [1, 64], F32)
    STCOL = sbuf("STCOL", [128, 8], F32)
    AOST = sbuf("AOST", [128, 512], BF16)
    RECR = ROWS[0:1, 0:512]
    SWALL = sbuf("SWALL", [128, SP_TPC * 8], F32)
    SPF = sbuf("SPF", [128, 128], F32)
    SPF2 = sbuf("SPF2", [128, 128], F32)
    SPF3 = sbuf("SPF3", [128, 128], F32)
    SPM = sbuf("SPM", [128, 128], U8)
    SPU = sbuf("SPU", [128, 128], F32)
    SPV = sbuf("SPV", [128, 128], F32)
    SPA = sbuf("SPA", [128, 128], F32)
    SPB = sbuf("SPB", [128, 128], F32)
    SPCOL = sbuf("SPCOL", [128, 32], F32)
    SPW = sbuf("SPW", [128, 4 * 128], BF16)
    STILE = sbuf("STILE", [128, 128], F32)
    SLOPE = sbuf("SLOPE", [128, HPC], F32)
    RELG = sbuf("RELG", [128, 512], F32)
    FWG = POOL[:, 0:8192]
    FWU = POOL[:, 8192:16384]
    FWD = POOL[:, 16384:24576]
    FIN = POOL[:, 24576:26624]
    FSG = sbuf("FSG", [128, 2 * 512], F32)
    FSD = FSG

    def psum(name, shape, dt=F32):
        return es.enter_context(nc.psum_tensor(name, shape, dt))

    PS0 = psum("PS0", [128, 512])
    PS1 = psum("PS1", [128, 512])
    PS2 = psum("PS2", [128, 512])
    PS3 = psum("PS3", [128, 512])
    PROW = psum("PROW", [1, 512])
    PROW2 = psum("PROW2", [1, 512])
    PBC = psum("PBC", [128, 512])
    PTR = psum("PTR", [128, 128], BF16)

    sl_lnrs = Slots(p, "s_lnrs", 2)
    sl_rsc = Slots(p, "s_rsc", 2)
    sl_fev = Slots(p, "s_fev", 1)
    sl_fsd = Slots(p, "s_fsd", 2)
    sl_wcv = Slots(p, "s_wcv", 2)
    sl_wbf = Slots(p, "s_wbf", 1)
    sl_qkvs = Slots(p, "s_qkvs", 2)
    sl_vtst = Slots(p, "s_vtst", 2)
    sl_sqb = Slots(p, "s_sqb", 2)
    sl_psA = Slots(p, "s_psA", 1)
    sl_psB = Slots(p, "s_psB", 1)
    sl_psC = Slots(p, "s_psC", 1)
    sl_psD = Slots(p, "s_psD", 1)
    sl_prow = Slots(p, "s_prow", 1)
    sl_prow2 = Slots(p, "s_prow2", 1)
    sl_pbc = Slots(p, "s_pbc", 1)
    sl_ptr = Slots(p, "s_ptr", 1)
    sl_bc = Slots(p, "s_bc", 1)
    sl_xln = Slots(p, "s_xln", 1)
    sl_pt = Slots(p, "s_pt", 3)
    sl_qtl = Slots(p, "s_qtl", 1)
    sl_ksb = Slots(p, "s_ksb", 1)
    sl_kst = Slots(p, "s_kst", 1)
    sl_aol = Slots(p, "s_aol", 1)
    sl_aost = Slots(p, "s_aost", 1)
    sl_fin = Slots(p, "s_fin", 1)
    sl_fwg = Slots(p, "s_fwg", 1)
    sl_swall = Slots(p, "s_swall", 1)
    sl_spw = Slots(p, "s_spw", 1)
    sl_nmax = Slots(p, "s_nmax", 1)
    sl_bcol = Slots(p, "s_bcol", 2)

    for buf, src in ((CINF, cinfo), (STILE, stile),
                     (SLOPE, slopebc), (RELG, relg)):
        p.op("sync", lambda e, b=buf, s=src: e.dma_start(out=b[:], in_=s[:]),
             incs=[("ld", 16)])
    for (buf, src) in ((COSS, cosT), (SINS, sinT)):
        for half in range(2):
            p.op("sync", lambda e, src=src, half=half: e.dma_start(
                out=WCV[0:32, 0:1024],
                in_=src[0:32, half * 1024:(half + 1) * 1024]),
                waits=[("cvt", p.val("cvt"))], incs=[("ld", 16)])
            p.op("vector", lambda e, buf=buf, half=half: e.tensor_copy(
                buf[0:32, half * 1024:(half + 1) * 1024], WCV[0:32, 0:1024]),
                waits=[("ld", p.val("ld"))], incs=[("cvt", 1)])
            p.edge("vector")
    ld0 = p.val("ld")
    p.op("vector", lambda e: e.memset(ONE_B[:], 1.0))
    p.op("vector", lambda e: e.memset(ONE_F[:], 1.0))
    p.op("vector", lambda e: e.memset(ONE_RF[:], 1.0), incs=[("cst", 1)])
    p.op("gpsimd", lambda e: e.memset(IDENT[:], 0.0))
    p.edge("gpsimd")
    p.op("gpsimd", lambda e: e.affine_select(
        IDENT[:], IDENT[:], [[-1, 128]], ALU.not_equal, 1.0, base=0,
        channel_multiplier=1), incs=[("cst", 1)])

    _DYN = {"hK": 0, "hV": 1, "hKb": 2, "hVb": 3, "tl": 4, "hN": 5, "hNk": 6}

    def _loadregs(e):
        p.stash["r0"] = e.alloc_register("dynbase")
        p.stash["r1"] = e.alloc_register("dynoff")
        return e.nop()

    def dyn(e, name, const=0):
        i = _DYN[name]
        e.load(p.stash["r0"], CINF[0:1, i:i + 1])
        e.reg_add(p.stash["r1"], p.stash["r0"], int(const))
        return p.stash["r1"]

    p.op("sync", _loadregs, waits=[("ld", ld0)], incs=[("cst", 1)])
    p.op("sync", lambda e: e.dma_start(out=residA[:], in_=x_in[:]),
         incs=[("rst", 16)])

    def wconv_phase():
        jobs = [(wqkvT, wqkvB, QKV_F, 2048), (woutT, woutB, D, 0),
                (wq, wqB, D, D), (wk, wkB, 512, 0), (wv, wvB, 512, 0),
                (wo, woB, D, 0)]
        nw = 0
        for (src, dst, F, scale_until) in jobs:
            for ch in range(16):
                for fb in range(F // 512):
                    slot = nw % 2
                    nw += 1
                    sap = _ap(src, (128 * ch) * F + fb * 512, [[F, 128], [1, 512]])
                    p.op("scalar", lambda e, sap=sap, slot=slot: e.dma_start(
                        out=WCV[:, slot * 512:(slot + 1) * 512], in_=sap),
                        waits=sl_wcv.wait_free() + ([("ld", ld0)] if nw == 1 else []),
                        incs=[sl_wcv.prod_inc()])
                    sl_wcv.wrote()
                    wcw = sl_wcv.prod_wait()
                    do_scale = (fb * 512 < scale_until)
                    if do_scale:
                        p.op("gpsimd", lambda e, slot=slot: e.tensor_scalar(
                            QKVS[:, slot * 512:(slot + 1) * 512],
                            WCV[:, slot * 512:(slot + 1) * 512],
                            SCALE_A, None, ALU.mult),
                            waits=wcw + sl_qkvs.wait_free(),
                            incs=[sl_wcv.consumed(), ("wcg", 1)])
                    else:
                        p.op("gpsimd", lambda e, slot=slot: e.tensor_copy(
                            QKVS[:, slot * 512:(slot + 1) * 512],
                            WCV[:, slot * 512:(slot + 1) * 512]),
                            waits=wcw + sl_qkvs.wait_free(),
                            incs=[sl_wcv.consumed(), ("wcg", 1)])
                    sl_qkvs.wrote()
                    dap = _ap(dst, (128 * ch) * F + fb * 512, [[F, 128], [1, 512]])
                    p.op("scalar", lambda e, dap=dap, slot=slot: e.dma_start(
                        out=dap, in_=QKVS[:, slot * 512:(slot + 1) * 512]),
                        waits=[("wcg", p.val("wcg"))], incs=[sl_qkvs.consumed(16)])
        return sl_qkvs.done_waits()

    def spline_phase():
        mats = [(swing, wg_sh), (swinu, wu_sh), (swind, wd_sh)]
        C = SPCOL

        def vop(fn, waits=(), incs=(), eng="vector", edge=True):
            p.op(eng, fn, waits=waits, incs=incs)
            if edge:
                p.edge(eng)

        for mi, (src, dst) in enumerate(mats):
            dims = [[8, 128], [8 * 128, SP_TPC], [1, 8]]
            ap = _ap(src, 0, dims)
            p.op("sync", lambda e, ap=ap: e.dma_start(out=SWALL[:], in_=ap),
                 waits=sl_swall.wait_free(), incs=[("spld", 16)])
            sl_swall.wrote()
            spld = p.val("spld")
            for tt in range(SP_TPC):
                cw = SWALL[:, tt * 8:(tt + 1) * 8]
                wm1, w0, w1, w2, w3 = (cw[:, 0:1], cw[:, 1:2], cw[:, 2:3],
                                       cw[:, 3:4], cw[:, 4:5])
                f0c = cw[:, 5:6]
                wA = [("spld", spld)] if tt == 0 else []
                vop(lambda e, a=C[:, 0:1], x=w0: e.tensor_copy(a, x), waits=wA, edge=False)
                vop(lambda e, a=C[:, 1:2], x=w1, y=wm1: e.tensor_sub(a, x, y))
                vop(lambda e, a=C[:, 1:2]: e.tensor_scalar(a, a, 0.5, None, ALU.mult), edge=False)
                vop(lambda e, a=C[:, 8:9], x=wm1: e.tensor_scalar(a, x, 2.0, None, ALU.mult), edge=False)
                vop(lambda e, a=C[:, 9:10], x=w0: e.tensor_scalar(a, x, -5.0, None, ALU.mult), edge=False)
                vop(lambda e, a=C[:, 10:11], x=w1: e.tensor_scalar(a, x, 4.0, None, ALU.mult))
                vop(lambda e, a=C[:, 2:3], x=C[:, 8:9], y=C[:, 9:10]: e.tensor_add(a, x, y))
                vop(lambda e, a=C[:, 2:3], y=C[:, 10:11]: e.tensor_add(a, a, y))
                vop(lambda e, a=C[:, 2:3], y=w2: e.tensor_sub(a, a, y))
                vop(lambda e, a=C[:, 2:3]: e.tensor_scalar(a, a, 0.5, None, ALU.mult), edge=False)
                vop(lambda e, a=C[:, 11:12], x=w0, y=w1: e.tensor_sub(a, x, y))
                vop(lambda e, a=C[:, 3:4], x=C[:, 11:12]: e.tensor_scalar(a, x, 3.0, None, ALU.mult))
                vop(lambda e, a=C[:, 3:4], y=wm1: e.tensor_sub(a, a, y))
                vop(lambda e, a=C[:, 3:4], y=w2: e.tensor_add(a, a, y))
                vop(lambda e, a=C[:, 3:4]: e.tensor_scalar(a, a, 0.5, None, ALU.mult), edge=False)
                vop(lambda e, a=C[:, 12:13], x=w2, y=w0: e.tensor_sub(a, x, y))
                vop(lambda e, a=C[:, 12:13]: e.tensor_scalar(a, a, 0.5, None, ALU.mult), edge=False)
                vop(lambda e, a=C[:, 13:14], x=w0: e.tensor_scalar(a, x, 2.0, None, ALU.mult), edge=False)
                vop(lambda e, a=C[:, 14:15], x=w1: e.tensor_scalar(a, x, -5.0, None, ALU.mult), edge=False)
                vop(lambda e, a=C[:, 15:16], x=w2: e.tensor_scalar(a, x, 4.0, None, ALU.mult))
                vop(lambda e, a=C[:, 13:14], y=C[:, 14:15]: e.tensor_add(a, a, y))
                vop(lambda e, a=C[:, 13:14], y=C[:, 15:16]: e.tensor_add(a, a, y))
                vop(lambda e, a=C[:, 13:14], y=w3: e.tensor_sub(a, a, y))
                vop(lambda e, a=C[:, 13:14]: e.tensor_scalar(a, a, 0.5, None, ALU.mult), edge=False)
                vop(lambda e, a=C[:, 16:17], x=w1, y=w2: e.tensor_sub(a, x, y))
                vop(lambda e, a=C[:, 14:15], x=C[:, 16:17]: e.tensor_scalar(a, x, 3.0, None, ALU.mult))
                vop(lambda e, a=C[:, 14:15], y=w0: e.tensor_sub(a, a, y))
                vop(lambda e, a=C[:, 14:15], y=w3: e.tensor_add(a, a, y))
                vop(lambda e, a=C[:, 14:15]: e.tensor_scalar(a, a, 0.5, None, ALU.mult))
                vop(lambda e, a=C[:, 4:5], x=w1, y=C[:, 12:13]: e.tensor_sub(a, x, y))
                vop(lambda e, a=C[:, 4:5], y=C[:, 13:14]: e.tensor_add(a, a, y))
                vop(lambda e, a=C[:, 4:5], y=C[:, 14:15]: e.tensor_sub(a, a, y), edge=False)
                vop(lambda e, a=C[:, 17:18], x=C[:, 13:14]: e.tensor_scalar(a, x, -2.0, None, ALU.mult), edge=False)
                vop(lambda e, a=C[:, 18:19], x=C[:, 14:15]: e.tensor_scalar(a, x, 3.0, None, ALU.mult))
                vop(lambda e, a=C[:, 5:6], x=C[:, 12:13], y=C[:, 17:18]: e.tensor_add(a, x, y))
                vop(lambda e, a=C[:, 5:6], y=C[:, 18:19]: e.tensor_add(a, a, y), edge=False)
                vop(lambda e, a=C[:, 6:7], x=C[:, 13:14], y=C[:, 18:19]: e.tensor_sub(a, x, y), edge=False)
                vop(lambda e, a=C[:, 7:8], x=C[:, 14:15]: e.tensor_copy(a, x))
                vop(lambda e, f0c=f0c: e.tensor_scalar(SPF[:], STILE[:], f0c, None, ALU.add),
                    waits=[("cst", 3)] if (mi == 0 and tt == 0) else [])
                vop(lambda e: e.tensor_tensor(SPF2[:], SPF[:], SPF[:], ALU.mult))
                vop(lambda e: e.tensor_tensor(SPF3[:], SPF2[:], SPF[:], ALU.mult), edge=False)
                vop(lambda e: e.tensor_single_scalar(SPM[:], SPF[:], 1.0, ALU.is_lt))
                vop(lambda e: e.tensor_scalar(SPU[:], SPF[:], C[:, 1:2], C[:, 0:1], ALU.mult, ALU.add))
                vop(lambda e: e.scalar_tensor_tensor(SPU[:], SPF2[:], C[:, 2:3], SPU[:], ALU.mult, ALU.add))
                vop(lambda e: e.scalar_tensor_tensor(SPA[:], SPF3[:], C[:, 3:4], SPU[:], ALU.mult, ALU.add), edge=False)
                vop(lambda e: e.tensor_scalar(SPV[:], SPF[:], C[:, 5:6], C[:, 4:5], ALU.mult, ALU.add))
                vop(lambda e: e.scalar_tensor_tensor(SPV[:], SPF2[:], C[:, 6:7], SPV[:], ALU.mult, ALU.add))
                vop(lambda e: e.scalar_tensor_tensor(SPB[:], SPF3[:], C[:, 7:8], SPV[:], ALU.mult, ALU.add))
                slot = tt % 4
                wsel = sl_spw.wait_free() if slot == 0 else []
                vop(lambda e, slot=slot: e.select(
                    SPW[:, slot * 128:(slot + 1) * 128], SPM[:], SPA[:], SPB[:]),
                    waits=wsel, incs=[("spsel", 1)])
                if slot == 3:
                    base = (tt - 3) * 16384
                    dims2 = [[128, 128], [16384, 4], [1, 128]]
                    dap = _ap(dst, base, dims2)
                    sap = SPW[:].rearrange("p (t s) -> p t s", t=4)
                    p.op("sync", lambda e, dap=dap, sap=sap: e.dma_start(
                        out=dap, in_=sap),
                        waits=[("spsel", p.val("spsel"))],
                        incs=[sl_spw.consumed(16)])
                    sl_spw.wrote()
            p.op("vector", None, waits=[], incs=[sl_swall.consumed()])
        spw_done = sl_spw.done_waits()
        for sh, full in ((wg_sh, wg_f), (wu_sh, wu_f), (wd_sh, wd_f)):
            p.op("gpsimd", lambda e, sh=sh, full=full: e.collective_compute(
                "AllGather", ALU.bypass, replica_groups=G8,
                ins=[sh.opt()], outs=[full.opt()]),
                waits=spw_done, incs=[("wgath", 1)])
        return p.val("wgath")

    def ln_phase(resid, waits0=()):
        for tb in range(4):
            for ch in range(16):
                slot = ch % 2
                src = _ap(resid, (128 * ch) * TOK + tb * 512, [[TOK, 128], [1, 512]])
                w = list(waits0) + sl_lnrs.wait_free()
                p.op("sync", lambda e, src=src, slot=slot: e.dma_start(
                    out=LNRS[:, slot * 512:(slot + 1) * 512], in_=src),
                    waits=w, incs=[sl_lnrs.prod_inc()])
                sl_lnrs.wrote()
                lnw = sl_lnrs.prod_wait()
                wsq = list(lnw) + sl_sqb.wait_free()
                p.op("scalar", lambda e, slot=slot: e.activation(
                    SQB[:, slot * 512:(slot + 1) * 512],
                    LNRS[:, slot * 512:(slot + 1) * 512], AF.Square),
                    waits=wsq, incs=[("act_sq", 1)])
                sl_sqb.wrote()
                wmm = list(lnw) + [("act_sq", p.val("act_sq"))] + \
                      (([("cst", 3)] + sl_prow.wait_free()) if ch == 0 else [])
                p.op("tensor", lambda e, ch=ch, slot=slot: e.matmul(
                    PROW[0:1, :], ONE_F[:, 0:1], LNRS[:, slot * 512:(slot + 1) * 512],
                    start=(ch == 0), stop=(ch == 15)),
                    waits=wmm,
                    incs=[sl_lnrs.consumed()] + ([("pe_ln", 1)] if ch == 15 else []))
                if ch == 0:
                    sl_prow.wrote()
                wm2 = (sl_prow2.wait_free() if ch == 0 else [])
                p.op("tensor", lambda e, slot=slot, ch=ch: e.matmul(
                    PROW2[0:1, :], ONE_B[:, 0:1], SQB[:, slot * 512:(slot + 1) * 512],
                    start=(ch == 0), stop=(ch == 15)),
                    waits=wm2,
                    incs=[sl_sqb.consumed()] + ([("pe_ln", 1)] if ch == 15 else []))
                if ch == 0:
                    sl_prow2.wrote()
            pe_ln = p.val("pe_ln")
            mrow, vrow, rrow, nrow = (ROWS[0:1, 0:512], ROWS[0:1, 512:1024],
                                      ROWS[0:1, 1024:1536], ROWS[0:1, 1536:2048])
            p.op("vector", lambda e, m=mrow: e.tensor_scalar(
                m, PROW[0:1, :], 1.0 / D, None, ALU.mult),
                waits=[("pe_ln", pe_ln)],
                incs=[sl_prow.consumed()])
            p.op("vector", lambda e, v=vrow: e.tensor_scalar(
                v, PROW2[0:1, :], 1.0 / D, None, ALU.mult),
                incs=[sl_prow2.consumed()])
            p.edge("vector")
            p.op("vector", lambda e, r=rrow, m=mrow: e.tensor_tensor(r, m, m, ALU.mult))
            p.edge("vector")
            p.op("vector", lambda e, v=vrow, r=rrow: e.tensor_sub(v, v, r),
                 incs=[("dve_ln", 1)])
            p.edge("vector")
            p.op("vector", lambda e, v=vrow: e.tensor_scalar(
                v, v, LN_EPS, None, ALU.add), incs=[("dve_ln", 1)])
            p.edge("vector")
            p.op("scalar", lambda e, r=rrow, v=vrow: e.activation(r, v, AF.Ln),
                 waits=[("dve_ln", p.val("dve_ln"))], incs=[("act_ln", 1)])
            p.edge("scalar")
            p.op("vector", lambda e, r=rrow: e.tensor_scalar(
                r, r, -0.5, None, ALU.mult),
                waits=[("act_ln", p.val("act_ln"))], incs=[("dve_ln", 1)])
            p.edge("vector")
            p.op("scalar", lambda e, r=rrow: e.activation(r, r, AF.Exp),
                 waits=[("dve_ln", p.val("dve_ln"))], incs=[("act_ln", 1)])
            p.edge("scalar")
            p.op("vector", lambda e, n=nrow, m=mrow, r=rrow: e.tensor_tensor(n, m, r, ALU.mult),
                 waits=[("act_ln", p.val("act_ln"))])
            p.edge("vector")
            p.op("vector", lambda e, n=nrow: e.tensor_scalar(n, n, -1.0, None, ALU.mult),
                 incs=[("dve_ln", 1)])
            p.edge("vector")
            p.op("tensor", lambda e, r=rrow: e.matmul(
                PBC[:, :], ONE_RF[0:1, :], r, start=True, stop=True),
                waits=[("dve_ln", p.val("dve_ln"))] + sl_pbc.wait_free() + sl_bc.wait_free(),
                incs=[("pe_bc", 1)])
            sl_pbc.wrote()
            sl_bc.wrote()
            p.op("vector", lambda e: e.tensor_copy(BCA[:], PBC[:, :]),
                 waits=[("pe_bc", p.val("pe_bc"))],
                 incs=[sl_pbc.consumed(), ("dve_bc", 1)])
            p.op("tensor", lambda e, n=nrow: e.matmul(
                PBC[:, :], ONE_RF[0:1, :], n, start=True, stop=True),
                waits=[("dve_bc", p.val("dve_bc"))] + sl_pbc.wait_free(),
                incs=[("pe_bc", 1)])
            sl_pbc.wrote()
            p.op("vector", lambda e: e.tensor_copy(BCB[:], PBC[:, :]),
                 waits=[("pe_bc", p.val("pe_bc"))],
                 incs=[sl_pbc.consumed(), ("stat_done", 1)])
            p.edge("vector")
            for ch in range(16):
                slot = ch % 2
                src = _ap(resid, (128 * ch) * TOK + tb * 512, [[TOK, 128], [1, 512]])
                p.op("sync", lambda e, src=src, slot=slot: e.dma_start(
                    out=LNRS[:, slot * 512:(slot + 1) * 512], in_=src),
                    waits=list(waits0) + sl_lnrs.wait_free(),
                    incs=[sl_lnrs.prod_inc()])
                sl_lnrs.wrote()
                lnw = sl_lnrs.prod_wait()
                p.op("vector", lambda e, slot=slot: e.tensor_tensor(
                    FSD[:, slot * 512:(slot + 1) * 512],
                    LNRS[:, slot * 512:(slot + 1) * 512], BCA[:], ALU.mult),
                    waits=list(lnw) + [("stat_done", p.val("stat_done"))]
                          + sl_fsd.wait_free())
                sl_fsd.wrote()
                p.edge("vector")
                last = (ch == 15)
                p.op("vector", lambda e, ch=ch, tb=tb, slot=slot: e.tensor_tensor(
                    XLN[:, ch * TOK + tb * 512: ch * TOK + tb * 512 + 512],
                    FSD[:, slot * 512:(slot + 1) * 512], BCB[:], ALU.add),
                    incs=[("xlnorm", 1), sl_lnrs.consumed(), sl_fsd.consumed()]
                         + ([sl_bc.consumed()] if last else []))
                p.edge("vector")
        return p.val("xlnorm")

    def convert_wblock(wsrc_b, fb, wconv_done):
        F = wsrc_b.shape[1]
        dims = [[F, 128], [128 * F, 16], [1, 512]]
        sap = _ap(wsrc_b, fb * 512, dims)
        p.op("sync", lambda e, sap=sap: e.dma_start(out=WBF[:], in_=sap),
             waits=sl_wbf.wait_free() + list(wconv_done),
             incs=[sl_wbf.prod_inc()])
        sl_wbf.wrote()
        return sl_wbf.prod_wait()

    def qkv_phase(kind, xln_tgt, wc_done):
        if kind == "a":
            blocks = [("q", wqkvB, 0, 4), ("k", wqkvB, 4, 4), ("v", wqkvB, 8, 4)]
        else:
            blocks = [("q", wqB, 0, 4), ("k", wkB, 0, 1), ("v", wvB, 0, 1)]
        kdst = ksh if kind == "a" else kshb
        vdst = vsh if kind == "a" else vshb
        for (role, wsrc, fb0, nfb) in blocks:
            for fbl in range(nfb):
                wtgt = convert_wblock(wsrc, fb0 + fbl, wc_done)
                for fsub in range(4):
                    fc = fbl * 4 + fsub
                    for tb in range(4):
                        k2 = (fsub * 4 + tb) % 2
                        PSX, slx, psn = ((PS0, sl_psA, "psA") if k2 == 0
                                         else (PS1, sl_psB, "psB"))
                        for ch in range(16):
                            w = (list(wtgt) + [("xlnorm", xln_tgt)]
                                 + slx.wait_free()) if ch == 0 else []
                            lastw = (fsub == 3 and tb == 3 and ch == 15)
                            p.op("tensor", lambda e, ch=ch, fsub=fsub, tb=tb, PSX=PSX: e.matmul(
                                PSX[:, :],
                                WBF[:, ch * 512 + fsub * 128: ch * 512 + fsub * 128 + 128],
                                XLN[:, ch * TOK + tb * 512: ch * TOK + tb * 512 + 512],
                                start=(ch == 0), stop=(ch == 15)),
                                waits=w, incs=([(psn, 1)] if ch == 15 else [])
                                     + ([sl_wbf.consumed()] if lastw else []))
                            if ch == 0:
                                slx.wrote()
                        pstgt = p.val(psn)
                        slot = tb % 2
                        rot = (kind == "a" and role in ("q", "k"))
                        p.op("scalar", lambda e, slot=slot, PSX=PSX: e.activation(
                            QKVS[:, slot * 512:(slot + 1) * 512], PSX[:, :], AF.Copy),
                            waits=[(psn, pstgt)] + sl_qkvs.wait_free(),
                            incs=[("qkev", 1)] + ([slx.consumed()] if not rot else []))
                        sl_qkvs.wrote()
                        qkev = p.val("qkev")
                        if rot:
                            ts = slice(tb * 512, tb * 512 + 512)
                            p.op("scalar", lambda e, PSX=PSX: e.copy(SHPRE[:, :], PSX[0:32, :]),
                                 incs=[("qkev", 1)])
                            p.edge("scalar")
                            qkev = p.val("qkev")
                            p.op("vector", lambda e, PSX=PSX, t=ts: e.tensor_tensor(
                                RT1[:, :], PSX[0:32, :], COSS[:, t], ALU.mult),
                                waits=[("qkev", qkev), ("cst", 3)],
                                incs=[slx.consumed()])
                            p.op("vector", lambda e: e.stream_shuffle(
                                SH32[:, :], SHPRE[:, :], [(i + 16) % 32 for i in range(32)]))
                            p.edge("vector")
                            p.op("vector", lambda e, t=ts: e.tensor_tensor(
                                RT2[:, :], SH32[:, :], SINS[:, t], ALU.mult))
                            p.edge("vector")
                            p.op("vector", lambda e: e.tensor_add(RT1[:, :], RT1[:, :], RT2[:, :]))
                            p.edge("vector")
                            p.op("vector", lambda e, slot=slot: e.tensor_copy(
                                QKVS[0:32, slot * 512:slot * 512 + 512], RT1[:, :]),
                                incs=[("qkev", 1)])
                            p.edge("vector")
                            qkev = p.val("qkev")
                        if role in ("q", "k"):
                            slot2 = tb % 2
                            p.op("scalar", lambda e, slot=slot, slot2=slot2: e.activation(
                                SQB[:, slot2 * 512:(slot2 + 1) * 512],
                                QKVS[:, slot * 512:(slot + 1) * 512], AF.Square),
                                waits=[("qkev", qkev)] + sl_sqb.wait_free(),
                                incs=[("act_nq", 1)])
                            sl_sqb.wrote()
                            p.op("tensor", lambda e, slot2=slot2: e.matmul(
                                PROW[0:1, :], ONE_B[:, 0:1],
                                SQB[:, slot2 * 512:(slot2 + 1) * 512],
                                start=True, stop=True),
                                waits=[("act_nq", p.val("act_nq"))] + sl_prow.wait_free(),
                                incs=[sl_sqb.consumed(), ("pe_nrm", 1)])
                            sl_prow.wrote()
                            p.op("vector", lambda e: e.tensor_copy(NROW[0:1, :], PROW[0:1, :]),
                                 waits=[("pe_nrm", p.val("pe_nrm")), ("nst", p.val("nst"))],
                                 incs=[sl_prow.consumed(), ("dve_nrm", 1)])
                            p.edge("vector")
                            row = 1 if role == "q" else 0
                            off = row * (NH * TOK) + fc * TOK + tb * 512
                            p.op("scalar", lambda e, off=off: e.dma_start(
                                out=_ap(nsh, off, [[1, 1], [1, 512]]), in_=NROW[0:1, :]),
                                waits=[("dve_nrm", p.val("dve_nrm"))], incs=[("nst", 16)])
                        if role == "v":
                            vslot = tb % 2
                            for u in range(4):
                                wv_ = [("qkev", qkev)] if u == 0 else []
                                p.op("tensor", lambda e, slot=slot, u=u: e.transpose(
                                    PTR[:, :],
                                    QKVS[:, slot * 512 + u * 128: slot * 512 + u * 128 + 128],
                                    IDENT[:]),
                                    waits=wv_ + sl_ptr.wait_free()
                                          + (sl_vtst.wait_free() if u == 0 else []),
                                    incs=[("pe_vt", 1)]
                                         + ([sl_qkvs.consumed()] if u == 3 else []))
                                sl_ptr.wrote()
                                if u == 0:
                                    sl_vtst.wrote()
                                p.op("vector", lambda e, vslot=vslot, u=u: e.tensor_copy(
                                    VTST[:, vslot * 512 + u * 128: vslot * 512 + u * 128 + 128],
                                    PTR[:, :]),
                                    waits=[("pe_vt", p.val("pe_vt"))],
                                    incs=[sl_ptr.consumed(), ("dve_vt", 1)])
                                p.edge("vector")
                            vw = vdst.shape[1]
                            dims = [[vw, 128], [128 * vw, 4], [1, 128]]
                            dap = _ap(vdst, (tb * 512) * vw + fc * 128, dims)
                            sap = VTST[:, vslot * 512: vslot * 512 + 512].rearrange(
                                "p (u d) -> p u d", u=4)
                            p.op("sync", lambda e, dap=dap, sap=sap: e.dma_start(
                                out=dap, in_=sap),
                                waits=[("dve_vt", p.val("dve_vt"))],
                                incs=[sl_vtst.consumed(16)])
                        else:
                            dst = qsh if role == "q" else kdst
                            dims = [[dst.shape[1], 128], [1, 512]]
                            dap = _ap(dst, (fc * 128) * dst.shape[1] + tb * 512, dims)
                            p.op("sync", lambda e, dap=dap, slot=slot: e.dma_start(
                                out=dap, in_=QKVS[:, slot * 512:(slot + 1) * 512]),
                                waits=[("qkev", qkev)], incs=[sl_qkvs.consumed(16)])
        return (sl_qkvs.done_waits() + sl_vtst.done_waits() + [("nst", p.val("nst"))])

    def attn_phase(kind, cc_tgt):
        st = p.stash
        if kind == "a":
            exts = [
                (kloc, "hK", 0, [[TOK, 512], [D * TOK, 4], [1, TOK]]),
                (qloc, "hK", 0, [[TOK, 512], [D * TOK, 4], [1, TOK]], qga),
                (vloc, "hV", 0, [[D, 4 * TOK], [1, 512]]),
                (nlock, "hN", 0, [[2 * NH * TOK, 4], [TOK, 4], [1, TOK]]),
                (nlocq, "hN", NH * TOK, [[2 * NH * TOK, 4], [TOK, 4], [1, TOK]]),
            ]
            srcs = {id(kloc): kga, id(vloc): vga, id(nlock): nga, id(nlocq): nga}
        else:
            exts = [
                (kloc, "hKb", 0, [[TOK, 128], [512 * TOK, 4], [1, TOK]]),
                (qloc, "hK", 0, [[TOK, 512], [D * TOK, 4], [1, TOK]], qga),
                (vloc, "hVb", 0, [[512, 4 * TOK], [1, 128]]),
                (nlock, "hNk", 0, [[2 * NH * TOK, 4], [1, TOK]]),
                (nlocq, "hN", NH * TOK, [[2 * NH * TOK, 4], [TOK, 4], [1, TOK]]),
            ]
            srcs = {id(kloc): kgab, id(vloc): vgab, id(nlock): nga, id(nlocq): nga}
        for ei, ext in enumerate(exts):
            dst, nm, cadd, dims = ext[0], ext[1], ext[2], ext[3]
            srct = ext[4] if len(ext) > 4 else srcs[id(dst)]
            nelem = 1
            for (_, c) in dims:
                nelem *= c
            dstap = _ap(dst, 0, [[1, nelem]])
            p.op("sync", lambda e, dstap=dstap, srct=srct, nm=nm, cadd=cadd, dims=dims:
                 e.dma_start(out=dstap,
                             in_=APClass(srct.tensor, dyn(e, nm, cadd), dims)),
                 waits=[("cc", cc_tgt)] + (sl_ksb.wait_free() if ei == 0 else []),
                 incs=[("exld", 16)])
            if ei == 0:
                sl_ksb.wrote()
        exld = p.val("exld")

        for h in range(HPC):
            for (kind2, cell, nl) in (("k", 0, nlock), ("q", 1, nlocq)):
                nrb = 4
                for rb2 in range(16):
                    rb = rb2 // 4
                    if kind == "b" and kind2 == "k":
                        roff = rb * TOK
                    else:
                        roff = (rb * 4 + h) * TOK
                    roff += (rb2 % 4) * 512
                    p.op("sync", lambda e, roff=roff, nl=nl: e.dma_start(
                            out=NMAX[0:1, :],
                            in_=_ap(nl, roff, [[1, 1], [1, 512]])),
                        waits=[("exld", exld)] + sl_nmax.wait_free(),
                        incs=[("nmld", 16)])
                    sl_nmax.wrote()
                    p.op("vector", lambda e, rb2=rb2: e.reduce_max(
                        STAT[0:1, 32 + rb2: 33 + rb2], NMAX[0:1, :], AX),
                        waits=[("nmld", p.val("nmld"))],
                        incs=[sl_nmax.consumed(), ("dve_c", 1)])
                    p.edge("vector")
                for (a, b, c) in ((32, 33, 48), (34, 35, 49), (36, 37, 50),
                                  (38, 39, 51), (40, 41, 52), (42, 43, 53),
                                  (44, 45, 54), (46, 47, 55), (48, 49, 56),
                                  (50, 51, 57), (52, 53, 58), (54, 55, 59),
                                  (56, 57, 60), (58, 59, 61)):
                    p.op("vector", lambda e, a=a, b=b, c=c: e.tensor_max(
                        STAT[0:1, c:c + 1], STAT[0:1, a:a + 1], STAT[0:1, b:b + 1]))
                    p.edge("vector")
                p.op("vector", lambda e, cell=cell, h=h: e.tensor_max(
                    STAT[0:1, h * 8 + cell: h * 8 + cell + 1],
                    STAT[0:1, 60:61], STAT[0:1, 61:62]),
                    incs=[("dve_c", 1)])
                p.edge("vector")
            p.op("vector", lambda e, h=h: e.tensor_tensor(
                STAT[0:1, h * 8 + 2: h * 8 + 3], STAT[0:1, h * 8: h * 8 + 1],
                STAT[0:1, h * 8 + 1: h * 8 + 2], ALU.mult), incs=[("dve_c", 1)])
            p.edge("vector")
            p.op("scalar", lambda e, h=h: e.activation(
                STAT[0:1, h * 8 + 3: h * 8 + 4], STAT[0:1, h * 8 + 2: h * 8 + 3], AF.Ln),
                waits=[("dve_c", p.val("dve_c"))], incs=[("act_c", 1)])
            p.edge("scalar")
            p.op("vector", lambda e, h=h: e.tensor_scalar(
                STAT[0:1, h * 8 + 3: h * 8 + 4], STAT[0:1, h * 8 + 3: h * 8 + 4],
                0.5, None, ALU.mult),
                waits=[("act_c", p.val("act_c"))], incs=[("dve_c", 1)])
            p.edge("vector")
            p.op("scalar", lambda e, h=h: e.activation(
                STAT[0:1, h * 8 + 4: h * 8 + 5], STAT[0:1, h * 8 + 3: h * 8 + 4],
                AF.Exp), waits=[("dve_c", p.val("dve_c"))], incs=[("act_c", 1)])
            p.edge("scalar")
            p.op("vector", lambda e, h=h: e.tensor_scalar(
                STAT[0:1, h * 8 + 5: h * 8 + 6], STAT[0:1, h * 8 + 4: h * 8 + 5],
                -1.0, None, ALU.mult),
                waits=[("act_c", p.val("act_c"))], incs=[("dve_c", 1)])
            p.edge("vector")
            p.op("tensor", lambda e, h=h: e.matmul(
                PBC[:, 0:1], ONE_RF[0:1, :], STAT[0:1, h * 8 + 5: h * 8 + 6],
                start=True, stop=True),
                waits=[("dve_c", p.val("dve_c"))] + sl_pbc.wait_free(),
                incs=[("pe_cb", 1)])
            sl_pbc.wrote()
            p.op("vector", lambda e, h=h: e.tensor_copy(STCOL[:, h:h + 1], PBC[:, 0:1]),
                 waits=[("pe_cb", p.val("pe_cb"))],
                 incs=[sl_pbc.consumed(), ("cready", 1)])
            p.edge("vector")
        cready = p.val("cready")

        for h in range(HPC):
            hl = h if kind == "a" else 0
            if kind == "a" or h == 0:
                p.op("sync", lambda e, hl=hl: e.dma_start(
                    out=KSB[:, :],
                    in_=_ap(kloc, (hl * 128) * SEQ, [[SEQ, 128], [1, SEQ]])),
                    waits=[("exld", exld)] + sl_kst.wait_free(),
                    incs=[("kld", 16)])
                sl_kst.wrote()
                vw = 512 if kind == "a" else 128
                p.op("sync", lambda e, hl=hl, vw=vw: e.dma_start(
                    out=VSB[:, :],
                    in_=_ap(vloc, hl * 128, [[vw, 128], [128 * vw, 64], [1, 128]])),
                    incs=[("kld", 16)])
            kld = p.val("kld")
            for qb in range(16):
                p.op("sync", lambda e, qb=qb, h=h: e.dma_start(
                    out=QTL[:, :],
                    in_=_ap(qloc, (h * 128) * SEQ + qb * 512, [[SEQ, 128], [1, 512]])),
                    waits=sl_qtl.wait_free(), incs=[("qld", 16)])
                sl_qtl.wrote()
                qld = p.val("qld")
                nkt = 4 * qb + 4

                def emit_qk(kt, qb=qb, h=h, qld=qld, kld=kld, nkt=nkt):
                    k3 = kt % 3
                    PSX, slx = ((PS0, sl_psA) if k3 == 0 else
                                (PS1, sl_psB) if k3 == 1 else (PS3, sl_psD))
                    w = slx.wait_free()
                    if kt == 0:
                        w += [("qld", qld)]
                        if qb == 0:
                            w += [("kld", kld), ("cready", cready)]
                    p.op("tensor", lambda e, kt=kt, PSX=PSX: e.matmul(
                        PSX[:, :], KSB[:, kt * 128:(kt + 1) * 128], QTL[:, :],
                        start=True, stop=True),
                        waits=w, incs=[("qk", 1)]
                             + ([sl_qtl.consumed()] if kt == nkt - 1 else []))
                    slx.wrote()
                    qk = p.val("qk")
                    if kind == "a":
                        bias = STCOL[:, h:h + 1]
                        exw = [("qk", qk)]
                    else:
                        bslot = 4 + (kt % 2)
                        dlt = float(kt * 128 - qb * 512)
                        p.op("vector", lambda e, h=h, bslot=bslot, dlt=dlt: e.tensor_scalar(
                            STCOL[:, bslot:bslot + 1], SLOPE[:, h:h + 1],
                            dlt, STCOL[:, h:h + 1], ALU.mult, ALU.add),
                            waits=sl_bcol.wait_free())
                        sl_bcol.wrote()
                        p.edge("vector")
                        p.op("vector", lambda e, h=h, PSX=PSX: e.scalar_tensor_tensor(
                            PSX[:, :], RELG[:, :], SLOPE[:, h:h + 1], PSX[:, :],
                            ALU.mult, ALU.add),
                            waits=[("qk", qk)], incs=[("ali", 1)])
                        p.edge("vector")
                        bias = STCOL[:, bslot:bslot + 1]
                        exw = [("ali", p.val("ali"))]
                    p.op("scalar", lambda e, kt=kt, PSX=PSX, bias=bias: e.activation(
                        PT[:, (kt % 3) * 512:(kt % 3) * 512 + 512], PSX[:, :],
                        AF.Exp, bias=bias),
                        waits=exw + sl_pt.wait_free(),
                        incs=[("ex", 1), slx.consumed()]
                             + ([sl_bcol.consumed()] if kind == "b" else []))
                    sl_pt.wrote()
                    ex = p.val("ex")
                    if kt >= 4 * qb:
                        base = qb * 512 - kt * 128
                        p.op("gpsimd", lambda e, kt=kt, base=base: e.affine_select(
                            PT[:, (kt % 3) * 512:(kt % 3) * 512 + 512],
                            PT[:, (kt % 3) * 512:(kt % 3) * 512 + 512],
                            [[1, 512]], ALU.is_ge, 0.0, base=base,
                            channel_multiplier=-1),
                            waits=[("ex", ex)], incs=[("msk", 1)])
                        return [("msk", p.val("msk"))]
                    return [("ex", ex)]

                def emit_sumpv(kt, wsum, qb=qb, h=h, nkt=nkt):
                    if kt == 0:
                        wsum = wsum + [("dve_rc", p.val("dve_rc")),
                                       ("dve_ao", p.val("dve_ao"))]
                    p.op("tensor", lambda e, kt=kt, nkt=nkt: e.matmul(
                        PROW[0:1, :], ONE_B[:, 0:1],
                        PT[:, (kt % 3) * 512:(kt % 3) * 512 + 512],
                        start=(kt == 0), stop=(kt == nkt - 1)),
                        waits=wsum + (sl_prow.wait_free() if kt == 0 else []))
                    if kt == 0:
                        sl_prow.wrote()
                    last_stripe = (kt == nkt - 1 and qb == 15 and
                                   (kind == "a" or h == HPC - 1))
                    p.op("tensor", lambda e, kt=kt, nkt=nkt: e.matmul(
                        PS2[:, :], VSB[:, kt * 128:(kt + 1) * 128],
                        PT[:, (kt % 3) * 512:(kt % 3) * 512 + 512],
                        start=(kt == 0), stop=(kt == nkt - 1)),
                        incs=[("pv", 1), sl_pt.consumed()] +
                             ([("pe_q", 1)] if kt == nkt - 1 else []) +
                             ([sl_kst.consumed()] if last_stripe else []))

                pend = []
                for kt in range(min(3, nkt)):
                    pend.append(emit_qk(kt))
                for kt in range(nkt):
                    emit_sumpv(kt, pend[kt])
                    if kt + 3 < nkt:
                        pend.append(emit_qk(kt + 3))
                petgt = p.val("pe_q")
                p.op("vector", lambda e: e.reciprocal(RECR, PROW[0:1, :]),
                     waits=[("pe_q", petgt)],
                     incs=[sl_prow.consumed(), ("dve_rc", 1)])
                p.edge("vector")
                p.op("tensor", lambda e: e.matmul(
                    PBC[:, :], ONE_RF[0:1, :], RECR, start=True, stop=True),
                    waits=[("dve_rc", p.val("dve_rc"))] + sl_pbc.wait_free(),
                    incs=[("pe_rb", 1)])
                sl_pbc.wrote()
                p.op("vector", lambda e: e.tensor_copy(BCA[:], PBC[:, :]),
                     waits=[("pe_rb", p.val("pe_rb"))],
                     incs=[sl_pbc.consumed()])
                p.edge("vector")
                p.op("vector", lambda e: e.tensor_tensor(
                    AOST[:, :], PS2[:, :], BCA[:], ALU.mult),
                    waits=sl_aost.wait_free(), incs=[("dve_ao", 1)])
                sl_aost.wrote()
                p.edge("vector")
                p.op("sync", lambda e, h=h, qb=qb: e.dma_start(
                    out=_ap(aosh, (h * 128) * SEQ + qb * 512, [[SEQ, 128], [1, 512]]),
                    in_=AOST[:, :]),
                    waits=[("dve_ao", p.val("dve_ao"))], incs=[sl_aost.consumed(16)])
            if h == HPC - 1:
                p.op("tensor", None, incs=[sl_ksb.consumed()])
        return sl_aost.done_waits()

    def outproj_phase(wsrcT, rold, rdst, cc_tgt, wc_done):
        st = p.stash
        p.op("sync", lambda e: e.dma_start(
            out=_ap(aoloc, 0, [[1, D * TOK]]),
            in_=APClass(aoga.tensor, dyn(e, "tl", 0), [[SEQ, D], [1, TOK]])),
            waits=[("cc", cc_tgt)] + sl_aol.wait_free(), incs=[("exld", 16)])
        sl_aol.wrote()
        aoex = p.val("exld")
        for tb in range(4):
            dims = [[TOK, 128], [128 * TOK, 16], [1, 512]]
            lastld = (tb == 3)
            p.op("sync", lambda e, tb=tb, dims=dims: e.dma_start(
                out=XLN[:, 0:16 * 512],
                in_=_ap(aoloc, tb * 512, dims)),
                waits=[("exld", aoex)] + sl_xln.wait_free(),
                incs=[("aold", 16)] + ([sl_aol.consumed(16)] if lastld else []))
            sl_xln.wrote()
            aold = p.val("aold")
            for cb in range(4):
                wtgt = convert_wblock(wsrcT, cb, wc_done)
                for csub in range(4):
                    cc = cb * 4 + csub
                    k2 = cc % 2
                    PSX, slx, psn = ((PS0, sl_psA, "psA") if k2 == 0
                                     else (PS1, sl_psB, "psB"))
                    rsrc = _ap(rold, (cc * 128) * TOK + tb * 512, [[TOK, 128], [1, 512]])
                    rslot = cc % 2
                    p.op("sync", lambda e, rsrc=rsrc, rslot=rslot: e.dma_start(
                        out=RSC[:, rslot * 512:(rslot + 1) * 512], in_=rsrc),
                        waits=sl_rsc.wait_free(), incs=[sl_rsc.prod_inc()])
                    sl_rsc.wrote()
                    rcw = sl_rsc.prod_wait()
                    for ch in range(16):
                        w = (list(wtgt) + [("aold", aold)] + slx.wait_free()) \
                            if ch == 0 else []
                        last_xln = (cb == 3 and csub == 3 and ch == 15)
                        last_wbf = (csub == 3 and ch == 15)
                        p.op("tensor", lambda e, ch=ch, csub=csub, PSX=PSX: e.matmul(
                            PSX[:, :],
                            WBF[:, ch * 512 + csub * 128: ch * 512 + csub * 128 + 128],
                            XLN[:, ch * 512:(ch + 1) * 512],
                            start=(ch == 0), stop=(ch == 15)),
                            waits=w,
                            incs=([(psn, 1)] if ch == 15 else [])
                                 + ([sl_xln.consumed()] if last_xln else [])
                                 + ([sl_wbf.consumed()] if last_wbf else []))
                        if ch == 0:
                            slx.wrote()
                    p.op("vector", lambda e, rslot=rslot, PSX=PSX: e.tensor_tensor(
                        FSD[:, rslot * 512:(rslot + 1) * 512], PSX[:, :],
                        RSC[:, rslot * 512:(rslot + 1) * 512], ALU.add),
                        waits=[(psn, p.val(psn))] + rcw + sl_fsd.wait_free(),
                        incs=[("dve_op", 1), slx.consumed(), sl_rsc.consumed()])
                    sl_fsd.wrote()
                    p.edge("vector")
                    dims3 = [[TOK, 128], [1, 512]]
                    p.op("sync", lambda e, cc=cc, tb=tb, rdst=rdst, rslot=rslot: e.dma_start(
                        out=_ap(rdst, (cc * 128) * TOK + tb * 512, dims3),
                        in_=FSD[:, rslot * 512:(rslot + 1) * 512]),
                        waits=[("dve_op", p.val("dve_op"))],
                        incs=[sl_fsd.consumed(16)])
        return sl_fsd.done_waits()

    def ffn_phase(rold, rdst, wg_tgt, xln_tgt, rold_done=()):
        p.op("sync", lambda e, rold=rold, rdst=rdst: e.dma_start(
            out=rdst[:], in_=rold[:]), waits=list(rold_done),
            incs=[("ffinit", 16)])
        ffinit = p.val("ffinit")
        for sup in range(16):
            for hc in range(16):
                p.op("sync", lambda e, sup=sup, hc=hc: e.dma_start(
                    out=FWG[:, hc * 512:(hc + 1) * 512],
                    in_=_ap(wg_f, (sup * 512) * D + hc * 128, [[D, 512], [1, 128]]),
                    transpose=True),
                    waits=([("wgath", wg_tgt)] + sl_fwg.wait_free()) if hc == 0 else [],
                    incs=[("fwld", 16)])
                if hc == 0:
                    sl_fwg.wrote()
                p.op("sync", lambda e, sup=sup, hc=hc: e.dma_start(
                    out=FWU[:, hc * 512:(hc + 1) * 512],
                    in_=_ap(wu_f, (sup * 512) * D + hc * 128, [[D, 512], [1, 128]]),
                    transpose=True), incs=[("fwld", 16)])
            for oc in range(4):
                for cb in range(4):
                    p.op("sync", lambda e, sup=sup, oc=oc, cb=cb: e.dma_start(
                        out=FWD[:, (oc * 4 + cb) * 512:(oc * 4 + cb) * 512 + 512],
                        in_=_ap(wd_f, (cb * 512) * INTER + sup * 512 + oc * 128,
                                [[INTER, 512], [1, 128]]),
                        transpose=True), incs=[("fwld", 16)])
            fwld = p.val("fwld")
            for tb in range(4):
                for oc in range(4):
                    for ch in range(16):
                        w = []
                        if ch == 0:
                            w = ([("fwld", fwld), ("xlnorm", xln_tgt)]
                                 + sl_psC.wait_free() + sl_psD.wait_free())
                        p.op("tensor", lambda e, ch=ch, oc=oc, tb=tb: e.matmul(
                            PS2[:, :],
                            FWG[:, ch * 512 + oc * 128: ch * 512 + oc * 128 + 128],
                            XLN[:, ch * TOK + tb * 512: ch * TOK + tb * 512 + 512],
                            start=(ch == 0), stop=(ch == 15)),
                            waits=w, incs=([("ffg", 1)] if ch == 15 else []))
                        p.op("tensor", lambda e, ch=ch, oc=oc, tb=tb: e.matmul(
                            PS3[:, :],
                            FWU[:, ch * 512 + oc * 128: ch * 512 + oc * 128 + 128],
                            XLN[:, ch * TOK + tb * 512: ch * TOK + tb * 512 + 512],
                            start=(ch == 0), stop=(ch == 15)),
                            incs=([("ffg", 1)] if ch == 15 else []))
                        if ch == 0:
                            sl_psC.wrote()
                            sl_psD.wrote()
                    ffg = p.val("ffg")
                    fslot = oc % 2
                    wsilu = [("ffg", ffg)]
                    if p.val("ffin") >= 1:
                        wsilu.append(("ffin", p.val("ffin") - 1))
                    p.op("scalar", lambda e, fslot=fslot: e.activation(
                        FSG[:, fslot * 512:(fslot + 1) * 512], PS2[:, :], AF.Silu),
                        waits=wsilu, incs=[("ffs", 1), sl_psC.consumed()])
                    wfin = sl_fin.wait_free() if oc == 0 else []
                    p.op("vector", lambda e, oc=oc, fslot=fslot: e.tensor_tensor(
                        FIN[:, oc * 512:(oc + 1) * 512],
                        FSG[:, fslot * 512:(fslot + 1) * 512], PS3[:, :], ALU.mult),
                        waits=[("ffs", p.val("ffs"))] + wfin,
                        incs=[("ffin", 1), sl_psD.consumed()])
                    if oc == 0:
                        sl_fin.wrote()
                    p.edge("vector")
                ffin = p.val("ffin")
                for cc in range(16):
                    k2 = cc % 2
                    PSX, slx, psn = ((PS0, sl_psA, "psA") if k2 == 0
                                     else (PS1, sl_psB, "psB"))
                    for oc in range(4):
                        w = ([("ffin", ffin)] + slx.wait_free()) if oc == 0 else []
                        p.op("tensor", lambda e, oc=oc, cc=cc, PSX=PSX: e.matmul(
                            PSX[:, :],
                            FWD[:, (oc * 4 + cc // 4) * 512 + (cc % 4) * 128:
                                   (oc * 4 + cc // 4) * 512 + (cc % 4) * 128 + 128],
                            FIN[:, oc * 512:(oc + 1) * 512],
                            start=(oc == 0), stop=(oc == 3)),
                            waits=w,
                            incs=([(psn, 1)] if oc == 3 else [])
                                 + ([sl_fin.consumed()] if (cc == 15 and oc == 3) else [])
                                 + ([sl_fwg.consumed()] if (tb == 3 and cc == 15 and oc == 3) else []))
                        if oc == 0:
                            slx.wrote()
                    p.op("vector", lambda e, cc=cc, PSX=PSX: e.tensor_copy(
                        FEV[:, (cc % 4) * 512:(cc % 4) * 512 + 512], PSX[:, :]),
                        waits=[(psn, p.val(psn))]
                              + (sl_fev.wait_free() if cc % 4 == 0 else []),
                        incs=[("ffev", 1), slx.consumed()])
                    if cc % 4 == 0:
                        sl_fev.wrote()
                    p.edge("vector")
                    if cc % 4 == 3:
                        cg = cc // 4
                        ffev = p.val("ffev")
                        dims = [[TOK, 128], [128 * TOK, 4], [1, 512]]
                        wacc = [("ffev", ffev)]
                        if sup == 0:
                            wacc.append(("ffinit", ffinit))
                        p.op("gpsimd", lambda e, tb=tb, cg=cg, rdst=rdst: e.dma_start(
                            out=_ap(rdst, (cg * 4 * 128) * TOK + tb * 512, dims),
                            in_=FEV[:].rearrange("p (c t) -> p c t", c=4),
                            accum_op=ALU.add),
                            waits=wacc, incs=[sl_fev.consumed(16)])
        return sl_fev.done_waits()

    wc_done = wconv_phase()
    wg_tgt = spline_phase() if "spline" not in dev_skip else 0

    xt = ln_phase(residA, waits0=[("rst", 16)]) if "ln" not in dev_skip else 0
    qkv_done = qkv_phase("a", xt, wc_done) if "qkv" not in dev_skip else []
    for (i_, o_) in ((qsh, qga), (ksh, kga), (vsh, vga), (nsh, nga)):
        p.op("gpsimd", lambda e, i_=i_, o_=o_: e.collective_compute(
            "AllGather", ALU.bypass, replica_groups=G4,
            ins=[i_.opt()], outs=[o_.opt()]),
            waits=qkv_done, incs=[("cc", 1)])
    ao_done = attn_phase("a", p.val("cc")) if "attn" not in dev_skip else []
    p.op("gpsimd", lambda e: e.collective_compute(
        "AllGather", ALU.bypass, replica_groups=G4,
        ins=[aosh.opt()], outs=[aoga.opt()]),
        waits=ao_done, incs=[("cc", 1)])
    cc_ao = p.val("cc")
    p.barrier()
    op1_done = []
    if "outproj" not in dev_skip:
        op1_done = outproj_phase(woutB, residA, residB, cc_ao, wc_done)
    p.barrier()

    ffn1_done = None
    if n_sub >= 2:
        xt2 = ln_phase(residB, waits0=op1_done)
        ffn1_done = ffn_phase(residB, residA, wg_tgt, xt2, op1_done)
        p.barrier()
    if n_sub >= 3:
        xt3 = ln_phase(residA, waits0=ffn1_done)
        qkv_done3 = qkv_phase("b", xt3, wc_done)
        for (i_, o_) in ((qsh, qga), (kshb, kgab), (vshb, vgab), (nsh, nga)):
            p.op("gpsimd", lambda e, i_=i_, o_=o_: e.collective_compute(
                "AllGather", ALU.bypass, replica_groups=G4,
                ins=[i_.opt()], outs=[o_.opt()]),
                waits=qkv_done3, incs=[("cc", 1)])
        ao_done3 = attn_phase("b", p.val("cc"))
        p.op("gpsimd", lambda e: e.collective_compute(
            "AllGather", ALU.bypass, replica_groups=G4,
            ins=[aosh.opt()], outs=[aoga.opt()]),
            waits=ao_done3, incs=[("cc", 1)])
        cc_ao = p.val("cc")
        p.barrier()
        op2_done = outproj_phase(woB, residA, residB, cc_ao, wc_done)
        p.barrier()
    if n_sub >= 4:
        xt4 = ln_phase(residB, waits0=op2_done)
        ffn2_done = ffn_phase(residB, out_t, wg_tgt, xt4, op2_done)
        p.op("sync", None, waits=ffn2_done)
    else:
        final = {0: residA, 1: residB, 2: residA, 3: residB}[n_sub]
        wfin = ([("rst", p.val("rst"))] if n_sub == 0 else
                op1_done if n_sub == 1 else
                op2_done if n_sub == 3 else ffn1_done)
        p.op("sync", lambda e, final=final: e.dma_start(out=out_t[:], in_=final[:]),
             waits=wfin, incs=[("outst", 16)])
        p.op("sync", None, waits=[("outst", p.val("outst"))])

    if dbg:
        hold = {"qsh": qsh, "ksh": ksh, "vsh": vsh, "nsh": nsh, "qga": qga,
                "kga": kga, "vga": vga, "nga": nga, "aosh": aosh, "aoga": aoga,
                "kloc": kloc, "vloc": vloc, "qloc": qloc, "aoloc": aoloc,
                "nlock": nlock, "nlocq": nlocq,
                "kshb": kshb, "vshb": vshb, "kgab": kgab, "vgab": vgab,
                "wg_f": wg_f, "wu_f": wu_f, "wd_f": wd_f,
                "residA": residA, "residB": residB}
        sbufs = {"XLN": XLN, "STAT": STAT, "STCOL": STCOL, "BIASB": BIASB,
                 "COSS": COSS, "SINS": SINS}
        p.barrier()
        for nm in dbg:
            if nm in hold:
                t = hold[nm]
                o = nc.dram_tensor("dbg_" + nm, list(t.shape), t.dtype,
                                   kind="ExternalOutput").ap()
                p.op("sync", lambda e, o=o, t=t: e.dma_start(out=o, in_=t),
                     incs=[("dbgst", 16)])
            else:
                t = sbufs[nm]
                sh = list(t.shape) if hasattr(t, "shape") else None
                o = nc.dram_tensor("dbg_" + nm, sh, t.dtype,
                                   kind="ExternalOutput").ap()
                p.op("sync", lambda e, o=o, t=t: e.dma_start(out=o, in_=t[:]),
                     incs=[("dbgst", 16)])
        p.op("sync", None, waits=[("dbgst", p.val("dbgst"))])

    p.replay()
    es.close()
    return nc


_CACHE = {}


def kernel(L0_attn_a_x, L0_attn_a_w_qkv, L0_attn_a_w_out, L0_attn_a_g,
           L0_attn_b_Wq, L0_attn_b_Wk, L0_attn_b_Wv, L0_attn_b_Wo,
           L0_ffn_gate_cp, L0_ffn_up_cp, L0_ffn_down_cp, n_sub=NSUB, dbg=(), dev_skip=()):
    x = np.asarray(L0_attn_a_x, dtype=np.float32)
    if "grid" not in _CACHE:
        _CACHE["grid"] = _spline_chunk_grid()
    i0, f0 = _CACHE["grid"]
    swin_g = _swin(np.asarray(L0_ffn_gate_cp, np.float32), i0, f0)
    swin_u = _swin(np.asarray(L0_ffn_up_cp, np.float32), i0, f0)
    swin_d = _swin(np.asarray(L0_ffn_down_cp, np.float32), i0, f0)
    cos, sinS = _rotary_tables()
    slopes = _alibi_slopes()
    slopes = slopes[np.arange(NH) // 4]
    stile = np.ascontiguousarray(np.broadcast_to(
        (np.arange(128, dtype=np.float64) * SP_STEP).astype(np.float32)[None, :],
        (128, 128)))
    kgrid = (np.arange(64, dtype=np.float32)[None, :] * 128.0 +
             np.arange(128, dtype=np.float32)[:, None]).astype(np.float32)
    relg_h = (np.arange(128, dtype=np.float32)[:, None] -
              np.arange(512, dtype=np.float32)[None, :]).astype(np.float32)

    wqkvT_h = np.ascontiguousarray(np.asarray(L0_attn_a_w_qkv, np.float32).T)
    woutT_h = np.ascontiguousarray(np.asarray(L0_attn_a_w_out, np.float32).T)
    wq_h = np.ascontiguousarray(np.asarray(L0_attn_b_Wq, np.float32))
    wk_h = np.ascontiguousarray(np.asarray(L0_attn_b_Wk, np.float32))
    wv_h = np.ascontiguousarray(np.asarray(L0_attn_b_Wv, np.float32))
    wo_h = np.ascontiguousarray(np.asarray(L0_attn_b_Wo, np.float32))
    xt = np.ascontiguousarray(np.transpose(x, (0, 2, 1)))

    in_maps = []
    npc = NCH // R
    for r in range(R):
        b, q = r // 4, r % 4
        cinfo = np.zeros((1, 8), np.int32)
        cinfo[0, 0] = q * 512 * TOK
        cinfo[0, 1] = q * 512
        cinfo[0, 2] = q * 128 * TOK
        cinfo[0, 3] = q * 128
        cinfo[0, 4] = q * TOK
        cinfo[0, 5] = q * 4 * TOK
        cinfo[0, 6] = q * TOK
        tok0 = q * TOK
        in_maps.append({
            "x": np.ascontiguousarray(xt[b][:, tok0:tok0 + TOK]),
            "wqkvT": wqkvT_h, "woutT": woutT_h,
            "wq": wq_h, "wk": wk_h, "wv": wv_h, "wo": wo_h,
            "swing": np.ascontiguousarray(swin_g[r * npc:(r + 1) * npc]),
            "swinu": np.ascontiguousarray(swin_u[r * npc:(r + 1) * npc]),
            "swind": np.ascontiguousarray(swin_d[r * npc:(r + 1) * npc]),
            "cosT": np.ascontiguousarray(cos[:, tok0:tok0 + TOK]),
            "sinT": np.ascontiguousarray(sinS[:, tok0:tok0 + TOK]),
            "stile": stile,
            "kgrid": kgrid,
            "slopebc": np.ascontiguousarray(np.broadcast_to(
                slopes[q * 4:(q + 1) * 4][None, :], (128, HPC))),
            "relg": relg_h,
            "cinfo": cinfo,
        })

    nc = build(n_sub, dev_skip=dev_skip, dbg=dbg)
    res = run_bass_kernel_spmd(nc, in_maps, core_ids=list(range(R)))
    out = np.zeros((BATCH, SEQ, D), np.float32)
    for r in range(R):
        b, q = r // 4, r % 4
        out[b, q * TOK:(q + 1) * TOK, :] = res.results[r]["out"].T
    if dbg:
        dd = {nm: [np.asarray(res.results[r]["dbg_" + nm]) for r in range(R)]
              for nm in dbg}
        return out, dd
    return out


BATCH = 2


# version 05 (final)
# speedup vs baseline: 1.1669x; 1.0094x over previous
import math
from contextlib import ExitStack

import numpy as np
import ml_dtypes

import concourse.bass as bass
import concourse.mybir as mybir
from concourse.ap import AP as APClass
from concourse.bass_utils import run_bass_kernel_spmd

BFNP = ml_dtypes.bfloat16
F32 = mybir.dt.float32
BF16 = mybir.dt.bfloat16
I32 = mybir.dt.int32
U8 = mybir.dt.uint8
AF = mybir.ActivationFunctionType
ALU = mybir.AluOpType
AX = mybir.AxisListType.X

R = 8
D = 2048
SEQ = 8192
TOK = 2048
NH = 16
HD = 128
HPC = 4
NSUB = 4
QKV_F = 6144
INTER = 8192
SCALE_A = HD ** -0.5
ROT = 32
LN_EPS = 1e-5

NCTRL = 130156
NSAMP = 16777216
NCH = NSAMP // 128
SP_TILES = NCH // 128
SP_TPC = SP_TILES // R
SP_STEP = float(NCTRL - 1) / float(NSAMP - 1)

ENGS = ("sync", "tensor", "vector", "scalar", "gpsimd")
G4 = [[0, 1, 2, 3], [4, 5, 6, 7]]
G8 = [[0, 1, 2, 3, 4, 5, 6, 7]]


def _spline_chunk_grid():
    t = np.linspace(0.0, float(NCTRL - 1), NSAMP, dtype=np.float64)
    j0 = np.arange(NCH, dtype=np.int64) * 128
    t0 = t[j0]
    i0 = np.clip(np.floor(t0).astype(np.int64), 0, NCTRL - 2)
    f0 = (t0 - i0).astype(np.float32)
    return i0, f0


def _swin(cp, i0, f0):
    out = np.zeros((NCH, 8), np.float32)
    for k in range(-1, 4):
        out[:, k + 1] = cp[np.clip(i0 + k, 0, NCTRL - 1)]
    out[:, 5] = f0
    return out


def _rotary_tables():
    inv_freq = 1.0 / (10000.0 ** (np.arange(0, ROT, 2, dtype=np.float32) / ROT))
    freqs = np.arange(SEQ, dtype=np.float32)[:, None] * inv_freq[None, :]
    pos = np.concatenate([freqs, freqs], axis=-1)
    cos = np.ascontiguousarray(np.cos(pos).T.astype(np.float32))
    sin = np.sin(pos).T.astype(np.float32)
    sin[:16] *= -1.0
    return cos, np.ascontiguousarray(sin)


def _alibi_slopes():
    start = 2.0 ** (-(2.0 ** (-(math.log2(NH) - 3))))
    return np.array([start * (start ** i) for i in range(NH)], dtype=np.float32)


class Prog:
    def __init__(self, nc, es):
        self.nc = nc
        self.es = es
        self.ops = {e: [] for e in ENGS}
        self.sems = {}
        self.stash = {}

    ALIAS = {"act_sq": "acta", "act_nq": "acta", "act_ln": "acta",
             "act_c": "acta", "ffs": "acta",
             "dve_ln": "dvea", "dve_bc": "dvea", "dve_c": "dvea",
             "dve_nrm": "dvea", "dve_vt": "dvea", "dve_op": "dvea",
             "stat_done": "dvea", "dve_rc": "dvea", "dve_ao": "dvea",
             "ali": "dvea",
             "pe_ln": "pea", "pe_bc": "pea", "pe_nrm": "pea",
             "pe_cb": "pea", "pe_vt": "pea", "pe_rb": "pea",
             "qkev": "mix", "cready": "mix", "wcg": "mix", "cvt": "mix"}

    def sem(self, name):
        name = self.ALIAS.get(name, name)
        if name not in self.sems:
            h = self.es.enter_context(self.nc.semaphore(name))
            self.sems[name] = [h, 0]
        return self.sems[name]

    def val(self, name):
        return self.sem(self.ALIAS.get(name, name))[1]

    def op(self, eng, fn, waits=(), incs=()):
        wl = [(self.sem(n)[0], v) for (n, v) in waits if v > 0]
        il = []
        for (n, a) in incs:
            s = self.sem(n)
            s[1] += a
            il.append((s[0], a))
        if fn is None or len(il) <= 1:
            self.ops[eng].append((fn, wl, il))
            return
        first = il[0]
        self.ops[eng].append((fn, wl, [first]))
        self.ops[eng].append((None, [(first[0], self._cur(first[0]))], il[1:]))

    def _cur(self, handle):
        for name, s in self.sems.items():
            if s[0] is handle:
                return s[1]
        raise KeyError

    def edge(self, eng):
        fn, wl, il = self.ops[eng][-1]
        if il:
            self.ops[eng].append((None, [(il[0][0], self._cur(il[0][0]))], []))
            return
        assert fn is not None, "edge after a bare wait"
        name = "edge_" + eng
        s = self.sem(name)
        s[1] += 1
        il.append((s[0], 1))
        self.ops[eng][-1] = (fn, wl, il)
        self.ops[eng].append((None, [(s[0], s[1])], []))

    def barrier(self):
        for e in ENGS:
            self.op(e, None, incs=[("bar", 1)])
        tgt = self.val("bar")
        for e in ENGS:
            self.op(e, None, waits=[("bar", tgt)])

    def replay(self):
        nc = self.nc
        with nc.Block() as block:
            for ename in ENGS:
                ops = self.ops[ename]

                def body(eng, ops=ops):
                    for fn, wl, il in ops:
                        for (h, v) in wl:
                            eng.wait_ge(h, v)
                        if fn is None:
                            for (h, a) in il:
                                eng.sem_inc(h, a)
                        else:
                            inst = fn(eng)
                            for (h, a) in il:
                                inst = inst.then_inc(h, a)

                getattr(block, ename)(body)


class Slots:

    def __init__(self, p, name, nslots):
        self.p = p
        self.name = name
        self.n = nslots
        self.w = 0
        self.c = 0
        self.ccum = [0] * nslots
        self.crel = [[] for _ in range(nslots)]
        self.pcum = [0] * nslots

    def wait_free(self):
        i = self.w
        if i < self.n:
            return []
        sl = i % self.n
        k = i // self.n - 1
        assert len(self.crel[sl]) > k, f"slot {self.name}: missing consumed()"
        self.p.sem(f"{self.name}_c{sl}")
        return [(f"{self.name}_c{sl}", self.crel[sl][k])]

    def prod_inc(self, amount=16):
        sl = self.w % self.n
        self.pcum[sl] += amount
        self.p.sem(f"{self.name}_p{sl}")
        return (f"{self.name}_p{sl}", amount)

    def prod_wait(self):
        sl = (self.w - 1) % self.n
        return [(f"{self.name}_p{sl}", self.pcum[sl])]

    def wrote(self):
        self.w += 1

    def consumed(self, amount=1):
        sl = self.c % self.n
        self.c += 1
        self.ccum[sl] += amount
        self.crel[sl].append(self.ccum[sl])
        self.p.sem(f"{self.name}_c{sl}")
        return (f"{self.name}_c{sl}", amount)

    def done_waits(self):
        return [(f"{self.name}_c{i}", self.ccum[i])
                for i in range(self.n) if self.ccum[i] > 0]


def _ap(t, offset, dims):
    return APClass(t.tensor, offset, [list(d) for d in dims])


def build(n_sub=NSUB, dev_skip=(), dbg=()):
    nc = bass.Bass("TRN2", target_bir_lowering=False)
    es = ExitStack()
    p = Prog(nc, es)

    def ein(name, shape, dt=F32):
        return nc.dram_tensor(name, shape, dt, kind="ExternalInput").ap()

    x_in = ein("x", [D, TOK])
    wqkvT = ein("wqkvT", [D, QKV_F])
    woutT = ein("woutT", [D, D])
    wq = ein("wq", [D, D])
    wk = ein("wk", [D, 512])
    wv = ein("wv", [D, 512])
    wo = ein("wo", [D, D])
    swing = ein("swing", [NCH // R, 8])
    swinu = ein("swinu", [NCH // R, 8])
    swind = ein("swind", [NCH // R, 8])
    cosT = ein("cosT", [ROT, TOK])
    sinT = ein("sinT", [ROT, TOK])
    stile = ein("stile", [128, 128])
    kgrid = ein("kgrid", [128, 64])
    slopebc = ein("slopebc", [128, HPC])
    relg = ein("relg", [128, 512])
    cinfo = ein("cinfo", [1, 8], I32)

    out_t = nc.dram_tensor("out", [D, TOK], F32, kind="ExternalOutput").ap()

    def idram(name, shape, dt=F32):
        return nc.dram_tensor(name, shape, dt, kind="Internal").ap()

    residA = idram("residA", [D, TOK])
    residB = idram("residB", [D, TOK])
    qsh = idram("qsh", [D, TOK], BF16)
    ksh = idram("ksh", [D, TOK], BF16)
    vsh = idram("vsh", [TOK, D], BF16)
    nsh = idram("nsh", [2, NH * TOK])
    qga = idram("qga", [4 * D, TOK], BF16)
    kga = idram("kga", [4 * D, TOK], BF16)
    vga = idram("vga", [4 * TOK, D], BF16)
    nga = idram("nga", [8, NH * TOK])
    aosh = idram("aosh", [HPC * HD, SEQ], BF16)
    aoga = idram("aoga", [4 * HPC * HD, SEQ], BF16)
    kshb = idram("kshb", [512, TOK], BF16)
    vshb = idram("vshb", [TOK, 512], BF16)
    kgab = idram("kgab", [4 * 512, TOK], BF16)
    vgab = idram("vgab", [4 * TOK, 512], BF16)
    wqkvB = idram("wqkvB", [D, QKV_F], BF16)
    woutB = idram("woutB", [D, D], BF16)
    wqB = idram("wqB", [D, D], BF16)
    wkB = idram("wkB", [D, 512], BF16)
    wvB = idram("wvB", [D, 512], BF16)
    woB = idram("woB", [D, D], BF16)
    kloc = idram("kloc", [512, SEQ], BF16)
    vloc = idram("vloc", [SEQ, 512], BF16)
    qloc = idram("qloc", [512, SEQ], BF16)
    aoloc = idram("aoloc", [D, TOK], BF16)
    nlock = idram("nlock", [16, TOK])
    nlocq = idram("nlocq", [16, TOK])
    wg_sh = idram("wg_sh", [NSAMP // R, 1], BF16)
    wu_sh = idram("wu_sh", [NSAMP // R, 1], BF16)
    wd_sh = idram("wd_sh", [NSAMP // R, 1], BF16)
    wg_f = idram("wg_f", [INTER, D], BF16)
    wu_f = idram("wu_f", [INTER, D], BF16)
    wd_f = idram("wd_f", [D, INTER], BF16)

    def sbuf(name, shape, dt):
        return es.enter_context(nc.sbuf_tensor(name, shape, dt))

    XLN = sbuf("XLN", [128, 16 * TOK], BF16)
    LNRS = sbuf("LNRS", [128, 2 * 512], F32)
    RSC = sbuf("RSC", [128, 2 * 512], F32)
    FEV = sbuf("FEV", [128, 4 * 512], F32)
    WCV = sbuf("WCV", [128, 2 * 512], F32)
    WBF = sbuf("WBF", [128, 16 * 512], BF16)
    QKVS = sbuf("QKVS", [128, 2 * 512], BF16)
    VTST = sbuf("VTST", [128, 2 * 512], BF16)
    SHPRE = sbuf("SHPRE", [32, 512], F32)
    SH32 = sbuf("SH32", [32, 512], F32)
    RT1 = sbuf("RT1", [32, 512], F32)
    RT2 = sbuf("RT2", [32, 512], F32)
    COSS = sbuf("COSS", [ROT, TOK], BF16)
    SINS = sbuf("SINS", [ROT, TOK], BF16)
    SQB = sbuf("SQB", [128, 2 * 512], BF16)
    ROWS = sbuf("ROWS", [1, 4 * 512], F32)
    NROW = ROWS[0:1, 512:1024]
    BCA = sbuf("BCA", [128, 512], F32)
    BCB = sbuf("BCB", [128, 512], F32)
    ONE_B = sbuf("ONE_B", [128, 1], BF16)
    ONE_F = sbuf("ONE_F", [128, 1], F32)
    ONE_RF = sbuf("ONE_RF", [1, 128], F32)
    IDENT = sbuf("IDENT", [128, 128], BF16)
    CINF = sbuf("CINF", [1, 8], I32)
    POOL = sbuf("POOL", [128, 26624], BF16)
    KSB = POOL[:, 0:SEQ]
    VSB = POOL[:, SEQ:2 * SEQ]
    QTL = POOL[:, 2 * SEQ:2 * SEQ + 512]
    PT = POOL[:, 2 * SEQ + 512:2 * SEQ + 512 + 1536]
    NMAX = sbuf("NMAX", [1, 512], F32)
    STAT = sbuf("STAT", [1, 64], F32)
    STCOL = sbuf("STCOL", [128, 8], F32)
    AOST = sbuf("AOST", [128, 512], BF16)
    RECR = ROWS[0:1, 0:512]
    SWALL = sbuf("SWALL", [128, SP_TPC * 8], F32)
    SPF = sbuf("SPF", [128, 128], F32)
    SPF2 = sbuf("SPF2", [128, 128], F32)
    SPF3 = sbuf("SPF3", [128, 128], F32)
    SPM = sbuf("SPM", [128, 128], U8)
    SPU = sbuf("SPU", [128, 128], F32)
    SPV = sbuf("SPV", [128, 128], F32)
    SPA = sbuf("SPA", [128, 128], F32)
    SPB = sbuf("SPB", [128, 128], F32)
    SPCOL = sbuf("SPCOL", [128, 32], F32)
    SPW = sbuf("SPW", [128, 4 * 128], BF16)
    STILE = sbuf("STILE", [128, 128], F32)
    SLOPE = sbuf("SLOPE", [128, HPC], F32)
    RELG = sbuf("RELG", [128, 512], F32)
    FWG = POOL[:, 0:8192]
    FWU = POOL[:, 8192:16384]
    FWD = POOL[:, 16384:24576]
    FIN = POOL[:, 24576:26624]
    FSG = sbuf("FSG", [128, 2 * 512], F32)
    FSD = FSG

    def psum(name, shape, dt=F32):
        return es.enter_context(nc.psum_tensor(name, shape, dt))

    PS0 = psum("PS0", [128, 512])
    PS1 = psum("PS1", [128, 512])
    PS2 = psum("PS2", [128, 512])
    PS3 = psum("PS3", [128, 512])
    PROW = psum("PROW", [1, 512])
    PROW2 = psum("PROW2", [1, 512])
    PBC = psum("PBC", [128, 512])
    PTR = psum("PTR", [128, 128], BF16)

    sl_lnrs = Slots(p, "s_lnrs", 2)
    sl_rsc = Slots(p, "s_rsc", 2)
    sl_fev = Slots(p, "s_fev", 1)
    sl_fsd = Slots(p, "s_fsd", 2)
    sl_wcv = Slots(p, "s_wcv", 2)
    sl_wbf = Slots(p, "s_wbf", 1)
    sl_qkvs = Slots(p, "s_qkvs", 2)
    sl_vtst = Slots(p, "s_vtst", 2)
    sl_sqb = Slots(p, "s_sqb", 2)
    sl_psA = Slots(p, "s_psA", 1)
    sl_psB = Slots(p, "s_psB", 1)
    sl_psC = Slots(p, "s_psC", 1)
    sl_psD = Slots(p, "s_psD", 1)
    sl_prow = Slots(p, "s_prow", 1)
    sl_prow2 = Slots(p, "s_prow2", 1)
    sl_pbc = Slots(p, "s_pbc", 1)
    sl_ptr = Slots(p, "s_ptr", 1)
    sl_bc = Slots(p, "s_bc", 1)
    sl_xln = Slots(p, "s_xln", 1)
    sl_pt = Slots(p, "s_pt", 3)
    sl_qtl = Slots(p, "s_qtl", 1)
    sl_ksb = Slots(p, "s_ksb", 1)
    sl_kst = Slots(p, "s_kst", 1)
    sl_aol = Slots(p, "s_aol", 1)
    sl_aost = Slots(p, "s_aost", 1)
    sl_fin = Slots(p, "s_fin", 1)
    sl_fwg = Slots(p, "s_fwg", 1)
    sl_swall = Slots(p, "s_swall", 1)
    sl_spw = Slots(p, "s_spw", 1)
    sl_nmax = Slots(p, "s_nmax", 1)
    sl_bcol = Slots(p, "s_bcol", 2)

    for buf, src in ((CINF, cinfo), (STILE, stile),
                     (SLOPE, slopebc), (RELG, relg)):
        p.op("sync", lambda e, b=buf, s=src: e.dma_start(out=b[:], in_=s[:]),
             incs=[("ld", 16)])
    for (buf, src) in ((COSS, cosT), (SINS, sinT)):
        for half in range(2):
            p.op("sync", lambda e, src=src, half=half: e.dma_start(
                out=WCV[0:32, 0:1024],
                in_=src[0:32, half * 1024:(half + 1) * 1024]),
                waits=[("cvt", p.val("cvt"))], incs=[("ld", 16)])
            p.op("vector", lambda e, buf=buf, half=half: e.tensor_copy(
                buf[0:32, half * 1024:(half + 1) * 1024], WCV[0:32, 0:1024]),
                waits=[("ld", p.val("ld"))], incs=[("cvt", 1)])
            p.edge("vector")
    ld0 = p.val("ld")
    p.op("vector", lambda e: e.memset(ONE_B[:], 1.0))
    p.op("vector", lambda e: e.memset(ONE_F[:], 1.0))
    p.op("vector", lambda e: e.memset(ONE_RF[:], 1.0), incs=[("cst", 1)])
    p.op("gpsimd", lambda e: e.memset(IDENT[:], 0.0))
    p.edge("gpsimd")
    p.op("gpsimd", lambda e: e.affine_select(
        IDENT[:], IDENT[:], [[-1, 128]], ALU.not_equal, 1.0, base=0,
        channel_multiplier=1), incs=[("cst", 1)])

    _DYN = {"hK": 0, "hV": 1, "hKb": 2, "hVb": 3, "tl": 4, "hN": 5, "hNk": 6}

    def _loadregs(e):
        p.stash["r0"] = e.alloc_register("dynbase")
        p.stash["r1"] = e.alloc_register("dynoff")
        return e.nop()

    def dyn(e, name, const=0):
        i = _DYN[name]
        e.load(p.stash["r0"], CINF[0:1, i:i + 1])
        e.reg_add(p.stash["r1"], p.stash["r0"], int(const))
        return p.stash["r1"]

    p.op("sync", _loadregs, waits=[("ld", ld0)], incs=[("cst", 1)])
    p.op("sync", lambda e: e.dma_start(out=residA[:], in_=x_in[:]),
         incs=[("rst", 16)])

    def wconv_phase():
        jobs = [(wqkvT, wqkvB, QKV_F, 2048), (woutT, woutB, D, 0),
                (wq, wqB, D, D), (wk, wkB, 512, 0), (wv, wvB, 512, 0),
                (wo, woB, D, 0)]
        nw = 0
        for (src, dst, F, scale_until) in jobs:
            for ch in range(16):
                for fb in range(F // 512):
                    slot = nw % 2
                    nw += 1
                    sap = _ap(src, (128 * ch) * F + fb * 512, [[F, 128], [1, 512]])
                    p.op("scalar", lambda e, sap=sap, slot=slot: e.dma_start(
                        out=WCV[:, slot * 512:(slot + 1) * 512], in_=sap),
                        waits=sl_wcv.wait_free()
                              + ([("ld", ld0), ("cvt", p.val("cvt"))] if nw == 1 else []),
                        incs=[sl_wcv.prod_inc()])
                    sl_wcv.wrote()
                    wcw = sl_wcv.prod_wait()
                    do_scale = (fb * 512 < scale_until)
                    if do_scale:
                        p.op("gpsimd", lambda e, slot=slot: e.tensor_scalar(
                            QKVS[:, slot * 512:(slot + 1) * 512],
                            WCV[:, slot * 512:(slot + 1) * 512],
                            SCALE_A, None, ALU.mult),
                            waits=wcw + sl_qkvs.wait_free(),
                            incs=[sl_wcv.consumed(), ("wcg", 1)])
                    else:
                        p.op("gpsimd", lambda e, slot=slot: e.tensor_copy(
                            QKVS[:, slot * 512:(slot + 1) * 512],
                            WCV[:, slot * 512:(slot + 1) * 512]),
                            waits=wcw + sl_qkvs.wait_free(),
                            incs=[sl_wcv.consumed(), ("wcg", 1)])
                    sl_qkvs.wrote()
                    dap = _ap(dst, (128 * ch) * F + fb * 512, [[F, 128], [1, 512]])
                    p.op("scalar", lambda e, dap=dap, slot=slot: e.dma_start(
                        out=dap, in_=QKVS[:, slot * 512:(slot + 1) * 512]),
                        waits=[("wcg", p.val("wcg"))], incs=[sl_qkvs.consumed(16)])
        return sl_qkvs.done_waits()

    def spline_phase():
        mats = [(swing, wg_sh), (swinu, wu_sh), (swind, wd_sh)]
        C = SPCOL

        def vop(fn, waits=(), incs=(), eng="vector", edge=True):
            p.op(eng, fn, waits=waits, incs=incs)
            if edge:
                p.edge(eng)

        for mi, (src, dst) in enumerate(mats):
            dims = [[8, 128], [8 * 128, SP_TPC], [1, 8]]
            ap = _ap(src, 0, dims)
            p.op("sync", lambda e, ap=ap: e.dma_start(out=SWALL[:], in_=ap),
                 waits=sl_swall.wait_free(), incs=[("spld", 16)])
            sl_swall.wrote()
            spld = p.val("spld")
            for tt in range(SP_TPC):
                cw = SWALL[:, tt * 8:(tt + 1) * 8]
                wm1, w0, w1, w2, w3 = (cw[:, 0:1], cw[:, 1:2], cw[:, 2:3],
                                       cw[:, 3:4], cw[:, 4:5])
                f0c = cw[:, 5:6]
                wA = [("spld", spld)] if tt == 0 else []
                vop(lambda e, a=C[:, 0:1], x=w0: e.tensor_copy(a, x), waits=wA, edge=False)
                vop(lambda e, a=C[:, 1:2], x=w1, y=wm1: e.tensor_sub(a, x, y))
                vop(lambda e, a=C[:, 1:2]: e.tensor_scalar(a, a, 0.5, None, ALU.mult), edge=False)
                vop(lambda e, a=C[:, 8:9], x=wm1: e.tensor_scalar(a, x, 2.0, None, ALU.mult), edge=False)
                vop(lambda e, a=C[:, 9:10], x=w0: e.tensor_scalar(a, x, -5.0, None, ALU.mult), edge=False)
                vop(lambda e, a=C[:, 10:11], x=w1: e.tensor_scalar(a, x, 4.0, None, ALU.mult))
                vop(lambda e, a=C[:, 2:3], x=C[:, 8:9], y=C[:, 9:10]: e.tensor_add(a, x, y))
                vop(lambda e, a=C[:, 2:3], y=C[:, 10:11]: e.tensor_add(a, a, y))
                vop(lambda e, a=C[:, 2:3], y=w2: e.tensor_sub(a, a, y))
                vop(lambda e, a=C[:, 2:3]: e.tensor_scalar(a, a, 0.5, None, ALU.mult), edge=False)
                vop(lambda e, a=C[:, 11:12], x=w0, y=w1: e.tensor_sub(a, x, y))
                vop(lambda e, a=C[:, 3:4], x=C[:, 11:12]: e.tensor_scalar(a, x, 3.0, None, ALU.mult))
                vop(lambda e, a=C[:, 3:4], y=wm1: e.tensor_sub(a, a, y))
                vop(lambda e, a=C[:, 3:4], y=w2: e.tensor_add(a, a, y))
                vop(lambda e, a=C[:, 3:4]: e.tensor_scalar(a, a, 0.5, None, ALU.mult), edge=False)
                vop(lambda e, a=C[:, 12:13], x=w2, y=w0: e.tensor_sub(a, x, y))
                vop(lambda e, a=C[:, 12:13]: e.tensor_scalar(a, a, 0.5, None, ALU.mult), edge=False)
                vop(lambda e, a=C[:, 13:14], x=w0: e.tensor_scalar(a, x, 2.0, None, ALU.mult), edge=False)
                vop(lambda e, a=C[:, 14:15], x=w1: e.tensor_scalar(a, x, -5.0, None, ALU.mult), edge=False)
                vop(lambda e, a=C[:, 15:16], x=w2: e.tensor_scalar(a, x, 4.0, None, ALU.mult))
                vop(lambda e, a=C[:, 13:14], y=C[:, 14:15]: e.tensor_add(a, a, y))
                vop(lambda e, a=C[:, 13:14], y=C[:, 15:16]: e.tensor_add(a, a, y))
                vop(lambda e, a=C[:, 13:14], y=w3: e.tensor_sub(a, a, y))
                vop(lambda e, a=C[:, 13:14]: e.tensor_scalar(a, a, 0.5, None, ALU.mult), edge=False)
                vop(lambda e, a=C[:, 16:17], x=w1, y=w2: e.tensor_sub(a, x, y))
                vop(lambda e, a=C[:, 14:15], x=C[:, 16:17]: e.tensor_scalar(a, x, 3.0, None, ALU.mult))
                vop(lambda e, a=C[:, 14:15], y=w0: e.tensor_sub(a, a, y))
                vop(lambda e, a=C[:, 14:15], y=w3: e.tensor_add(a, a, y))
                vop(lambda e, a=C[:, 14:15]: e.tensor_scalar(a, a, 0.5, None, ALU.mult))
                vop(lambda e, a=C[:, 4:5], x=w1, y=C[:, 12:13]: e.tensor_sub(a, x, y))
                vop(lambda e, a=C[:, 4:5], y=C[:, 13:14]: e.tensor_add(a, a, y))
                vop(lambda e, a=C[:, 4:5], y=C[:, 14:15]: e.tensor_sub(a, a, y), edge=False)
                vop(lambda e, a=C[:, 17:18], x=C[:, 13:14]: e.tensor_scalar(a, x, -2.0, None, ALU.mult), edge=False)
                vop(lambda e, a=C[:, 18:19], x=C[:, 14:15]: e.tensor_scalar(a, x, 3.0, None, ALU.mult))
                vop(lambda e, a=C[:, 5:6], x=C[:, 12:13], y=C[:, 17:18]: e.tensor_add(a, x, y))
                vop(lambda e, a=C[:, 5:6], y=C[:, 18:19]: e.tensor_add(a, a, y), edge=False)
                vop(lambda e, a=C[:, 6:7], x=C[:, 13:14], y=C[:, 18:19]: e.tensor_sub(a, x, y), edge=False)
                vop(lambda e, a=C[:, 7:8], x=C[:, 14:15]: e.tensor_copy(a, x))
                vop(lambda e, f0c=f0c: e.tensor_scalar(SPF[:], STILE[:], f0c, None, ALU.add),
                    waits=[("cst", 3)] if (mi == 0 and tt == 0) else [])
                vop(lambda e: e.tensor_tensor(SPF2[:], SPF[:], SPF[:], ALU.mult))
                vop(lambda e: e.tensor_tensor(SPF3[:], SPF2[:], SPF[:], ALU.mult), edge=False)
                vop(lambda e: e.tensor_single_scalar(SPM[:], SPF[:], 1.0, ALU.is_lt))
                vop(lambda e: e.tensor_scalar(SPU[:], SPF[:], C[:, 1:2], C[:, 0:1], ALU.mult, ALU.add))
                vop(lambda e: e.scalar_tensor_tensor(SPU[:], SPF2[:], C[:, 2:3], SPU[:], ALU.mult, ALU.add))
                vop(lambda e: e.scalar_tensor_tensor(SPA[:], SPF3[:], C[:, 3:4], SPU[:], ALU.mult, ALU.add), edge=False)
                vop(lambda e: e.tensor_scalar(SPV[:], SPF[:], C[:, 5:6], C[:, 4:5], ALU.mult, ALU.add))
                vop(lambda e: e.scalar_tensor_tensor(SPV[:], SPF2[:], C[:, 6:7], SPV[:], ALU.mult, ALU.add))
                vop(lambda e: e.scalar_tensor_tensor(SPB[:], SPF3[:], C[:, 7:8], SPV[:], ALU.mult, ALU.add))
                slot = tt % 4
                wsel = sl_spw.wait_free() if slot == 0 else []
                vop(lambda e, slot=slot: e.select(
                    SPW[:, slot * 128:(slot + 1) * 128], SPM[:], SPA[:], SPB[:]),
                    waits=wsel, incs=[("spsel", 1)])
                if slot == 3:
                    base = (tt - 3) * 16384
                    dims2 = [[128, 128], [16384, 4], [1, 128]]
                    dap = _ap(dst, base, dims2)
                    sap = SPW[:].rearrange("p (t s) -> p t s", t=4)
                    p.op("sync", lambda e, dap=dap, sap=sap: e.dma_start(
                        out=dap, in_=sap),
                        waits=[("spsel", p.val("spsel"))],
                        incs=[sl_spw.consumed(16)])
                    sl_spw.wrote()
            p.op("vector", None, waits=[], incs=[sl_swall.consumed()])
        spw_done = sl_spw.done_waits()
        for sh, full in ((wg_sh, wg_f), (wu_sh, wu_f), (wd_sh, wd_f)):
            p.op("gpsimd", lambda e, sh=sh, full=full: e.collective_compute(
                "AllGather", ALU.bypass, replica_groups=G8,
                ins=[sh.opt()], outs=[full.opt()]),
                waits=spw_done, incs=[("wgath", 1)])
        return p.val("wgath")

    def ln_phase(resid, waits0=()):
        for tb in range(4):
            for ch in range(16):
                slot = ch % 2
                src = _ap(resid, (128 * ch) * TOK + tb * 512, [[TOK, 128], [1, 512]])
                w = list(waits0) + sl_lnrs.wait_free()
                p.op("sync", lambda e, src=src, slot=slot: e.dma_start(
                    out=LNRS[:, slot * 512:(slot + 1) * 512], in_=src),
                    waits=w, incs=[sl_lnrs.prod_inc()])
                sl_lnrs.wrote()
                lnw = sl_lnrs.prod_wait()
                wsq = list(lnw) + sl_sqb.wait_free()
                p.op("scalar", lambda e, slot=slot: e.activation(
                    SQB[:, slot * 512:(slot + 1) * 512],
                    LNRS[:, slot * 512:(slot + 1) * 512], AF.Square),
                    waits=wsq, incs=[("act_sq", 1)])
                sl_sqb.wrote()
                wmm = list(lnw) + [("act_sq", p.val("act_sq"))] + \
                      (([("cst", 3)] + sl_prow.wait_free()) if ch == 0 else [])
                p.op("tensor", lambda e, ch=ch, slot=slot: e.matmul(
                    PROW[0:1, :], ONE_F[:, 0:1], LNRS[:, slot * 512:(slot + 1) * 512],
                    start=(ch == 0), stop=(ch == 15)),
                    waits=wmm,
                    incs=[sl_lnrs.consumed()] + ([("pe_ln", 1)] if ch == 15 else []))
                if ch == 0:
                    sl_prow.wrote()
                wm2 = (sl_prow2.wait_free() if ch == 0 else [])
                p.op("tensor", lambda e, slot=slot, ch=ch: e.matmul(
                    PROW2[0:1, :], ONE_B[:, 0:1], SQB[:, slot * 512:(slot + 1) * 512],
                    start=(ch == 0), stop=(ch == 15)),
                    waits=wm2,
                    incs=[sl_sqb.consumed()] + ([("pe_ln", 1)] if ch == 15 else []))
                if ch == 0:
                    sl_prow2.wrote()
            pe_ln = p.val("pe_ln")
            mrow, vrow, rrow, nrow = (ROWS[0:1, 0:512], ROWS[0:1, 512:1024],
                                      ROWS[0:1, 1024:1536], ROWS[0:1, 1536:2048])
            p.op("vector", lambda e, m=mrow: e.tensor_scalar(
                m, PROW[0:1, :], 1.0 / D, None, ALU.mult),
                waits=[("pe_ln", pe_ln)],
                incs=[sl_prow.consumed()])
            p.op("vector", lambda e, v=vrow: e.tensor_scalar(
                v, PROW2[0:1, :], 1.0 / D, None, ALU.mult),
                incs=[sl_prow2.consumed()])
            p.edge("vector")
            p.op("vector", lambda e, r=rrow, m=mrow: e.tensor_tensor(r, m, m, ALU.mult))
            p.edge("vector")
            p.op("vector", lambda e, v=vrow, r=rrow: e.tensor_sub(v, v, r),
                 incs=[("dve_ln", 1)])
            p.edge("vector")
            p.op("vector", lambda e, v=vrow: e.tensor_scalar(
                v, v, LN_EPS, None, ALU.add), incs=[("dve_ln", 1)])
            p.edge("vector")
            p.op("scalar", lambda e, r=rrow, v=vrow: e.activation(r, v, AF.Ln),
                 waits=[("dve_ln", p.val("dve_ln"))], incs=[("act_ln", 1)])
            p.edge("scalar")
            p.op("vector", lambda e, r=rrow: e.tensor_scalar(
                r, r, -0.5, None, ALU.mult),
                waits=[("act_ln", p.val("act_ln"))], incs=[("dve_ln", 1)])
            p.edge("vector")
            p.op("scalar", lambda e, r=rrow: e.activation(r, r, AF.Exp),
                 waits=[("dve_ln", p.val("dve_ln"))], incs=[("act_ln", 1)])
            p.edge("scalar")
            p.op("vector", lambda e, n=nrow, m=mrow, r=rrow: e.tensor_tensor(n, m, r, ALU.mult),
                 waits=[("act_ln", p.val("act_ln"))])
            p.edge("vector")
            p.op("vector", lambda e, n=nrow: e.tensor_scalar(n, n, -1.0, None, ALU.mult),
                 incs=[("dve_ln", 1)])
            p.edge("vector")
            p.op("tensor", lambda e, r=rrow: e.matmul(
                PBC[:, :], ONE_RF[0:1, :], r, start=True, stop=True),
                waits=[("dve_ln", p.val("dve_ln"))] + sl_pbc.wait_free() + sl_bc.wait_free(),
                incs=[("pe_bc", 1)])
            sl_pbc.wrote()
            sl_bc.wrote()
            p.op("vector", lambda e: e.tensor_copy(BCA[:], PBC[:, :]),
                 waits=[("pe_bc", p.val("pe_bc"))],
                 incs=[sl_pbc.consumed(), ("dve_bc", 1)])
            p.op("tensor", lambda e, n=nrow: e.matmul(
                PBC[:, :], ONE_RF[0:1, :], n, start=True, stop=True),
                waits=[("dve_bc", p.val("dve_bc"))] + sl_pbc.wait_free(),
                incs=[("pe_bc", 1)])
            sl_pbc.wrote()
            p.op("vector", lambda e: e.tensor_copy(BCB[:], PBC[:, :]),
                 waits=[("pe_bc", p.val("pe_bc"))],
                 incs=[sl_pbc.consumed(), ("stat_done", 1)])
            p.edge("vector")
            for ch in range(16):
                slot = ch % 2
                src = _ap(resid, (128 * ch) * TOK + tb * 512, [[TOK, 128], [1, 512]])
                p.op("sync", lambda e, src=src, slot=slot: e.dma_start(
                    out=LNRS[:, slot * 512:(slot + 1) * 512], in_=src),
                    waits=list(waits0) + sl_lnrs.wait_free(),
                    incs=[sl_lnrs.prod_inc()])
                sl_lnrs.wrote()
                lnw = sl_lnrs.prod_wait()
                p.op("vector", lambda e, slot=slot: e.tensor_tensor(
                    FSD[:, slot * 512:(slot + 1) * 512],
                    LNRS[:, slot * 512:(slot + 1) * 512], BCA[:], ALU.mult),
                    waits=list(lnw) + [("stat_done", p.val("stat_done"))]
                          + sl_fsd.wait_free())
                sl_fsd.wrote()
                p.edge("vector")
                last = (ch == 15)
                p.op("vector", lambda e, ch=ch, tb=tb, slot=slot: e.tensor_tensor(
                    XLN[:, ch * TOK + tb * 512: ch * TOK + tb * 512 + 512],
                    FSD[:, slot * 512:(slot + 1) * 512], BCB[:], ALU.add),
                    incs=[("xlnorm", 1), sl_lnrs.consumed(), sl_fsd.consumed()]
                         + ([sl_bc.consumed()] if last else []))
                p.edge("vector")
        return p.val("xlnorm")

    def convert_wblock(wsrc_b, fb, wconv_done):
        F = wsrc_b.shape[1]
        dims = [[F, 128], [128 * F, 16], [1, 512]]
        sap = _ap(wsrc_b, fb * 512, dims)
        p.op("sync", lambda e, sap=sap: e.dma_start(out=WBF[:], in_=sap),
             waits=sl_wbf.wait_free() + list(wconv_done),
             incs=[sl_wbf.prod_inc()])
        sl_wbf.wrote()
        return sl_wbf.prod_wait()

    def qkv_phase(kind, xln_tgt, wc_done):
        if kind == "a":
            blocks = [("q", wqkvB, 0, 4), ("k", wqkvB, 4, 4), ("v", wqkvB, 8, 4)]
        else:
            blocks = [("q", wqB, 0, 4), ("k", wkB, 0, 1), ("v", wvB, 0, 1)]
        kdst = ksh if kind == "a" else kshb
        vdst = vsh if kind == "a" else vshb
        for (role, wsrc, fb0, nfb) in blocks:
            for fbl in range(nfb):
                wtgt = convert_wblock(wsrc, fb0 + fbl, wc_done)
                for fsub in range(4):
                    fc = fbl * 4 + fsub
                    for tb in range(4):
                        k2 = (fsub * 4 + tb) % 2
                        PSX, slx, psn = ((PS0, sl_psA, "psA") if k2 == 0
                                         else (PS1, sl_psB, "psB"))
                        for ch in range(16):
                            w = (list(wtgt) + [("xlnorm", xln_tgt)]
                                 + slx.wait_free()) if ch == 0 else []
                            lastw = (fsub == 3 and tb == 3 and ch == 15)
                            p.op("tensor", lambda e, ch=ch, fsub=fsub, tb=tb, PSX=PSX: e.matmul(
                                PSX[:, :],
                                WBF[:, ch * 512 + fsub * 128: ch * 512 + fsub * 128 + 128],
                                XLN[:, ch * TOK + tb * 512: ch * TOK + tb * 512 + 512],
                                start=(ch == 0), stop=(ch == 15)),
                                waits=w, incs=([(psn, 1)] if ch == 15 else [])
                                     + ([sl_wbf.consumed()] if lastw else []))
                            if ch == 0:
                                slx.wrote()
                        pstgt = p.val(psn)
                        slot = tb % 2
                        rot = (kind == "a" and role in ("q", "k"))
                        p.op("scalar", lambda e, slot=slot, PSX=PSX: e.activation(
                            QKVS[:, slot * 512:(slot + 1) * 512], PSX[:, :], AF.Copy),
                            waits=[(psn, pstgt)] + sl_qkvs.wait_free(),
                            incs=[("qkev", 1)] + ([slx.consumed()] if not rot else []))
                        sl_qkvs.wrote()
                        qkev = p.val("qkev")
                        if rot:
                            ts = slice(tb * 512, tb * 512 + 512)
                            p.op("scalar", lambda e, PSX=PSX: e.copy(SHPRE[:, :], PSX[0:32, :]),
                                 incs=[("qkev", 1)])
                            p.edge("scalar")
                            qkev = p.val("qkev")
                            p.op("vector", lambda e, PSX=PSX, t=ts: e.tensor_tensor(
                                RT1[:, :], PSX[0:32, :], COSS[:, t], ALU.mult),
                                waits=[("qkev", qkev), ("cst", 3)],
                                incs=[slx.consumed()])
                            p.op("vector", lambda e: e.stream_shuffle(
                                SH32[:, :], SHPRE[:, :], [(i + 16) % 32 for i in range(32)]))
                            p.edge("vector")
                            p.op("vector", lambda e, t=ts: e.tensor_tensor(
                                RT2[:, :], SH32[:, :], SINS[:, t], ALU.mult))
                            p.edge("vector")
                            p.op("vector", lambda e: e.tensor_add(RT1[:, :], RT1[:, :], RT2[:, :]))
                            p.edge("vector")
                            p.op("vector", lambda e, slot=slot: e.tensor_copy(
                                QKVS[0:32, slot * 512:slot * 512 + 512], RT1[:, :]),
                                incs=[("qkev", 1)])
                            p.edge("vector")
                            qkev = p.val("qkev")
                        if role in ("q", "k"):
                            slot2 = tb % 2
                            p.op("scalar", lambda e, slot=slot, slot2=slot2: e.activation(
                                SQB[:, slot2 * 512:(slot2 + 1) * 512],
                                QKVS[:, slot * 512:(slot + 1) * 512], AF.Square),
                                waits=[("qkev", qkev)] + sl_sqb.wait_free(),
                                incs=[("act_nq", 1)])
                            sl_sqb.wrote()
                            p.op("tensor", lambda e, slot2=slot2: e.matmul(
                                PROW[0:1, :], ONE_B[:, 0:1],
                                SQB[:, slot2 * 512:(slot2 + 1) * 512],
                                start=True, stop=True),
                                waits=[("act_nq", p.val("act_nq"))] + sl_prow.wait_free(),
                                incs=[sl_sqb.consumed(), ("pe_nrm", 1)])
                            sl_prow.wrote()
                            p.op("vector", lambda e: e.tensor_copy(NROW[0:1, :], PROW[0:1, :]),
                                 waits=[("pe_nrm", p.val("pe_nrm")), ("nst", p.val("nst"))],
                                 incs=[sl_prow.consumed(), ("dve_nrm", 1)])
                            p.edge("vector")
                            row = 1 if role == "q" else 0
                            off = row * (NH * TOK) + fc * TOK + tb * 512
                            p.op("scalar", lambda e, off=off: e.dma_start(
                                out=_ap(nsh, off, [[1, 1], [1, 512]]), in_=NROW[0:1, :]),
                                waits=[("dve_nrm", p.val("dve_nrm"))], incs=[("nst", 16)])
                        if role == "v":
                            vslot = tb % 2
                            for u in range(4):
                                wv_ = [("qkev", qkev)] if u == 0 else []
                                p.op("tensor", lambda e, slot=slot, u=u: e.transpose(
                                    PTR[:, :],
                                    QKVS[:, slot * 512 + u * 128: slot * 512 + u * 128 + 128],
                                    IDENT[:]),
                                    waits=wv_ + sl_ptr.wait_free()
                                          + (sl_vtst.wait_free() if u == 0 else []),
                                    incs=[("pe_vt", 1)]
                                         + ([sl_qkvs.consumed()] if u == 3 else []))
                                sl_ptr.wrote()
                                if u == 0:
                                    sl_vtst.wrote()
                                p.op("vector", lambda e, vslot=vslot, u=u: e.tensor_copy(
                                    VTST[:, vslot * 512 + u * 128: vslot * 512 + u * 128 + 128],
                                    PTR[:, :]),
                                    waits=[("pe_vt", p.val("pe_vt"))],
                                    incs=[sl_ptr.consumed(), ("dve_vt", 1)])
                                p.edge("vector")
                            vw = vdst.shape[1]
                            dims = [[vw, 128], [128 * vw, 4], [1, 128]]
                            dap = _ap(vdst, (tb * 512) * vw + fc * 128, dims)
                            sap = VTST[:, vslot * 512: vslot * 512 + 512].rearrange(
                                "p (u d) -> p u d", u=4)
                            p.op("sync", lambda e, dap=dap, sap=sap: e.dma_start(
                                out=dap, in_=sap),
                                waits=[("dve_vt", p.val("dve_vt"))],
                                incs=[sl_vtst.consumed(16)])
                        else:
                            dst = qsh if role == "q" else kdst
                            dims = [[dst.shape[1], 128], [1, 512]]
                            dap = _ap(dst, (fc * 128) * dst.shape[1] + tb * 512, dims)
                            p.op("sync", lambda e, dap=dap, slot=slot: e.dma_start(
                                out=dap, in_=QKVS[:, slot * 512:(slot + 1) * 512]),
                                waits=[("qkev", qkev)], incs=[sl_qkvs.consumed(16)])
        return (sl_qkvs.done_waits() + sl_vtst.done_waits() + [("nst", p.val("nst"))])

    def attn_phase(kind, cc_tgt):
        st = p.stash
        if kind == "a":
            exts = [
                (kloc, "hK", 0, [[TOK, 512], [D * TOK, 4], [1, TOK]]),
                (qloc, "hK", 0, [[TOK, 512], [D * TOK, 4], [1, TOK]], qga),
                (vloc, "hV", 0, [[D, 4 * TOK], [1, 512]]),
                (nlock, "hN", 0, [[2 * NH * TOK, 4], [TOK, 4], [1, TOK]]),
                (nlocq, "hN", NH * TOK, [[2 * NH * TOK, 4], [TOK, 4], [1, TOK]]),
            ]
            srcs = {id(kloc): kga, id(vloc): vga, id(nlock): nga, id(nlocq): nga}
        else:
            exts = [
                (kloc, "hKb", 0, [[TOK, 128], [512 * TOK, 4], [1, TOK]]),
                (qloc, "hK", 0, [[TOK, 512], [D * TOK, 4], [1, TOK]], qga),
                (vloc, "hVb", 0, [[512, 4 * TOK], [1, 128]]),
                (nlock, "hNk", 0, [[2 * NH * TOK, 4], [1, TOK]]),
                (nlocq, "hN", NH * TOK, [[2 * NH * TOK, 4], [TOK, 4], [1, TOK]]),
            ]
            srcs = {id(kloc): kgab, id(vloc): vgab, id(nlock): nga, id(nlocq): nga}
        for ei, ext in enumerate(exts):
            dst, nm, cadd, dims = ext[0], ext[1], ext[2], ext[3]
            srct = ext[4] if len(ext) > 4 else srcs[id(dst)]
            nelem = 1
            for (_, c) in dims:
                nelem *= c
            dstap = _ap(dst, 0, [[1, nelem]])
            p.op("sync", lambda e, dstap=dstap, srct=srct, nm=nm, cadd=cadd, dims=dims:
                 e.dma_start(out=dstap,
                             in_=APClass(srct.tensor, dyn(e, nm, cadd), dims)),
                 waits=[("cc", cc_tgt)] + (sl_ksb.wait_free() if ei == 0 else []),
                 incs=[("exld", 16)])
            if ei == 0:
                sl_ksb.wrote()
        exld = p.val("exld")

        for h in range(HPC):
            for (kind2, cell, nl) in (("k", 0, nlock), ("q", 1, nlocq)):
                nrb = 4
                for rb2 in range(16):
                    rb = rb2 // 4
                    if kind == "b" and kind2 == "k":
                        roff = rb * TOK
                    else:
                        roff = (rb * 4 + h) * TOK
                    roff += (rb2 % 4) * 512
                    p.op("sync", lambda e, roff=roff, nl=nl: e.dma_start(
                            out=NMAX[0:1, :],
                            in_=_ap(nl, roff, [[1, 1], [1, 512]])),
                        waits=[("exld", exld)] + sl_nmax.wait_free(),
                        incs=[("nmld", 16)])
                    sl_nmax.wrote()
                    p.op("vector", lambda e, rb2=rb2: e.reduce_max(
                        STAT[0:1, 32 + rb2: 33 + rb2], NMAX[0:1, :], AX),
                        waits=[("nmld", p.val("nmld"))],
                        incs=[sl_nmax.consumed(), ("dve_c", 1)])
                    p.edge("vector")
                for (a, b, c) in ((32, 33, 48), (34, 35, 49), (36, 37, 50),
                                  (38, 39, 51), (40, 41, 52), (42, 43, 53),
                                  (44, 45, 54), (46, 47, 55), (48, 49, 56),
                                  (50, 51, 57), (52, 53, 58), (54, 55, 59),
                                  (56, 57, 60), (58, 59, 61)):
                    p.op("vector", lambda e, a=a, b=b, c=c: e.tensor_max(
                        STAT[0:1, c:c + 1], STAT[0:1, a:a + 1], STAT[0:1, b:b + 1]))
                    p.edge("vector")
                p.op("vector", lambda e, cell=cell, h=h: e.tensor_max(
                    STAT[0:1, h * 8 + cell: h * 8 + cell + 1],
                    STAT[0:1, 60:61], STAT[0:1, 61:62]),
                    incs=[("dve_c", 1)])
                p.edge("vector")
            p.op("vector", lambda e, h=h: e.tensor_tensor(
                STAT[0:1, h * 8 + 2: h * 8 + 3], STAT[0:1, h * 8: h * 8 + 1],
                STAT[0:1, h * 8 + 1: h * 8 + 2], ALU.mult), incs=[("dve_c", 1)])
            p.edge("vector")
            p.op("scalar", lambda e, h=h: e.activation(
                STAT[0:1, h * 8 + 3: h * 8 + 4], STAT[0:1, h * 8 + 2: h * 8 + 3], AF.Ln),
                waits=[("dve_c", p.val("dve_c"))], incs=[("act_c", 1)])
            p.edge("scalar")
            p.op("vector", lambda e, h=h: e.tensor_scalar(
                STAT[0:1, h * 8 + 3: h * 8 + 4], STAT[0:1, h * 8 + 3: h * 8 + 4],
                0.5, None, ALU.mult),
                waits=[("act_c", p.val("act_c"))], incs=[("dve_c", 1)])
            p.edge("vector")
            p.op("scalar", lambda e, h=h: e.activation(
                STAT[0:1, h * 8 + 4: h * 8 + 5], STAT[0:1, h * 8 + 3: h * 8 + 4],
                AF.Exp), waits=[("dve_c", p.val("dve_c"))], incs=[("act_c", 1)])
            p.edge("scalar")
            p.op("vector", lambda e, h=h: e.tensor_scalar(
                STAT[0:1, h * 8 + 5: h * 8 + 6], STAT[0:1, h * 8 + 4: h * 8 + 5],
                -1.0, None, ALU.mult),
                waits=[("act_c", p.val("act_c"))], incs=[("dve_c", 1)])
            p.edge("vector")
            p.op("tensor", lambda e, h=h: e.matmul(
                PBC[:, 0:1], ONE_RF[0:1, :], STAT[0:1, h * 8 + 5: h * 8 + 6],
                start=True, stop=True),
                waits=[("dve_c", p.val("dve_c"))] + sl_pbc.wait_free(),
                incs=[("pe_cb", 1)])
            sl_pbc.wrote()
            p.op("vector", lambda e, h=h: e.tensor_copy(STCOL[:, h:h + 1], PBC[:, 0:1]),
                 waits=[("pe_cb", p.val("pe_cb"))],
                 incs=[sl_pbc.consumed(), ("cready", 1)])
            p.edge("vector")
        cready = p.val("cready")

        for h in range(HPC):
            hl = h if kind == "a" else 0
            if kind == "a" or h == 0:
                p.op("sync", lambda e, hl=hl: e.dma_start(
                    out=KSB[:, :],
                    in_=_ap(kloc, (hl * 128) * SEQ, [[SEQ, 128], [1, SEQ]])),
                    waits=[("exld", exld)] + sl_kst.wait_free(),
                    incs=[("kld", 16)])
                sl_kst.wrote()
                vw = 512 if kind == "a" else 128
                p.op("sync", lambda e, hl=hl, vw=vw: e.dma_start(
                    out=VSB[:, :],
                    in_=_ap(vloc, hl * 128, [[vw, 128], [128 * vw, 64], [1, 128]])),
                    incs=[("kld", 16)])
            kld = p.val("kld")
            for qb in range(16):
                p.op("sync", lambda e, qb=qb, h=h: e.dma_start(
                    out=QTL[:, :],
                    in_=_ap(qloc, (h * 128) * SEQ + qb * 512, [[SEQ, 128], [1, 512]])),
                    waits=sl_qtl.wait_free(), incs=[("qld", 16)])
                sl_qtl.wrote()
                qld = p.val("qld")
                nkt = 4 * qb + 4

                def emit_qk(kt, qb=qb, h=h, qld=qld, kld=kld, nkt=nkt):
                    k3 = kt % 3
                    PSX, slx = ((PS0, sl_psA) if k3 == 0 else
                                (PS1, sl_psB) if k3 == 1 else (PS3, sl_psD))
                    w = slx.wait_free()
                    if kt == 0:
                        w += [("qld", qld)]
                        if qb == 0:
                            w += [("kld", kld), ("cready", cready)]
                    p.op("tensor", lambda e, kt=kt, PSX=PSX: e.matmul(
                        PSX[:, :], KSB[:, kt * 128:(kt + 1) * 128], QTL[:, :],
                        start=True, stop=True),
                        waits=w, incs=[("qk", 1)]
                             + ([sl_qtl.consumed()] if kt == nkt - 1 else []))
                    slx.wrote()
                    qk = p.val("qk")
                    if kind == "a":
                        bias = STCOL[:, h:h + 1]
                        exw = [("qk", qk)]
                    else:
                        bslot = 4 + (kt % 2)
                        dlt = float(kt * 128 - qb * 512)
                        p.op("vector", lambda e, h=h, bslot=bslot, dlt=dlt: e.tensor_scalar(
                            STCOL[:, bslot:bslot + 1], SLOPE[:, h:h + 1],
                            dlt, STCOL[:, h:h + 1], ALU.mult, ALU.add),
                            waits=sl_bcol.wait_free())
                        sl_bcol.wrote()
                        p.edge("vector")
                        p.op("vector", lambda e, h=h, PSX=PSX: e.scalar_tensor_tensor(
                            PSX[:, :], RELG[:, :], SLOPE[:, h:h + 1], PSX[:, :],
                            ALU.mult, ALU.add),
                            waits=[("qk", qk)], incs=[("ali", 1)])
                        p.edge("vector")
                        bias = STCOL[:, bslot:bslot + 1]
                        exw = [("ali", p.val("ali"))]
                    p.op("scalar", lambda e, kt=kt, PSX=PSX, bias=bias: e.activation(
                        PT[:, (kt % 3) * 512:(kt % 3) * 512 + 512], PSX[:, :],
                        AF.Exp, bias=bias),
                        waits=exw + sl_pt.wait_free(),
                        incs=[("ex", 1), slx.consumed()]
                             + ([sl_bcol.consumed()] if kind == "b" else []))
                    sl_pt.wrote()
                    ex = p.val("ex")
                    if kt >= 4 * qb:
                        base = qb * 512 - kt * 128
                        p.op("gpsimd", lambda e, kt=kt, base=base: e.affine_select(
                            PT[:, (kt % 3) * 512:(kt % 3) * 512 + 512],
                            PT[:, (kt % 3) * 512:(kt % 3) * 512 + 512],
                            [[1, 512]], ALU.is_ge, 0.0, base=base,
                            channel_multiplier=-1),
                            waits=[("ex", ex)], incs=[("msk", 1)])
                        return [("msk", p.val("msk"))]
                    return [("ex", ex)]

                def emit_sumpv(kt, wsum, qb=qb, h=h, nkt=nkt):
                    if kt == 0:
                        wsum = wsum + [("dve_rc", p.val("dve_rc")),
                                       ("dve_ao", p.val("dve_ao"))]
                    p.op("tensor", lambda e, kt=kt, nkt=nkt: e.matmul(
                        PROW[0:1, :], ONE_B[:, 0:1],
                        PT[:, (kt % 3) * 512:(kt % 3) * 512 + 512],
                        start=(kt == 0), stop=(kt == nkt - 1)),
                        waits=wsum + (sl_prow.wait_free() if kt == 0 else []))
                    if kt == 0:
                        sl_prow.wrote()
                    last_stripe = (kt == nkt - 1 and qb == 15 and
                                   (kind == "a" or h == HPC - 1))
                    p.op("tensor", lambda e, kt=kt, nkt=nkt: e.matmul(
                        PS2[:, :], VSB[:, kt * 128:(kt + 1) * 128],
                        PT[:, (kt % 3) * 512:(kt % 3) * 512 + 512],
                        start=(kt == 0), stop=(kt == nkt - 1)),
                        incs=[("pv", 1), sl_pt.consumed()] +
                             ([("pe_q", 1)] if kt == nkt - 1 else []) +
                             ([sl_kst.consumed()] if last_stripe else []))

                DEPTH = 3
                pend = []
                for kt in range(min(DEPTH, nkt)):
                    pend.append(emit_qk(kt))
                for kt in range(nkt):
                    emit_sumpv(kt, pend[kt])
                    if kt + DEPTH < nkt:
                        pend.append(emit_qk(kt + DEPTH))
                petgt = p.val("pe_q")
                p.op("vector", lambda e: e.reciprocal(RECR, PROW[0:1, :]),
                     waits=[("pe_q", petgt)],
                     incs=[sl_prow.consumed(), ("dve_rc", 1)])
                p.edge("vector")
                p.op("tensor", lambda e: e.matmul(
                    PBC[:, :], ONE_RF[0:1, :], RECR, start=True, stop=True),
                    waits=[("dve_rc", p.val("dve_rc"))] + sl_pbc.wait_free(),
                    incs=[("pe_rb", 1)])
                sl_pbc.wrote()
                p.op("vector", lambda e: e.tensor_copy(BCA[:], PBC[:, :]),
                     waits=[("pe_rb", p.val("pe_rb"))],
                     incs=[sl_pbc.consumed()])
                p.edge("vector")
                p.op("vector", lambda e: e.tensor_tensor(
                    AOST[:, :], PS2[:, :], BCA[:], ALU.mult),
                    waits=sl_aost.wait_free(), incs=[("dve_ao", 1)])
                sl_aost.wrote()
                p.edge("vector")
                p.op("sync", lambda e, h=h, qb=qb: e.dma_start(
                    out=_ap(aosh, (h * 128) * SEQ + qb * 512, [[SEQ, 128], [1, 512]]),
                    in_=AOST[:, :]),
                    waits=[("dve_ao", p.val("dve_ao"))], incs=[sl_aost.consumed(16)])
            if h == HPC - 1:
                p.op("tensor", None, incs=[sl_ksb.consumed()])
        return sl_aost.done_waits()

    def outproj_phase(wsrcT, rold, rdst, cc_tgt, wc_done):
        st = p.stash
        p.op("sync", lambda e: e.dma_start(
            out=_ap(aoloc, 0, [[1, D * TOK]]),
            in_=APClass(aoga.tensor, dyn(e, "tl", 0), [[SEQ, D], [1, TOK]])),
            waits=[("cc", cc_tgt)] + sl_aol.wait_free(), incs=[("exld", 16)])
        sl_aol.wrote()
        aoex = p.val("exld")
        for tb in range(4):
            dims = [[TOK, 128], [128 * TOK, 16], [1, 512]]
            lastld = (tb == 3)
            p.op("sync", lambda e, tb=tb, dims=dims: e.dma_start(
                out=XLN[:, 0:16 * 512],
                in_=_ap(aoloc, tb * 512, dims)),
                waits=[("exld", aoex)] + sl_xln.wait_free(),
                incs=[("aold", 16)] + ([sl_aol.consumed(16)] if lastld else []))
            sl_xln.wrote()
            aold = p.val("aold")
            for cb in range(4):
                wtgt = convert_wblock(wsrcT, cb, wc_done)
                for csub in range(4):
                    cc = cb * 4 + csub
                    k2 = cc % 2
                    PSX, slx, psn = ((PS0, sl_psA, "psA") if k2 == 0
                                     else (PS1, sl_psB, "psB"))
                    rsrc = _ap(rold, (cc * 128) * TOK + tb * 512, [[TOK, 128], [1, 512]])
                    rslot = cc % 2
                    p.op("sync", lambda e, rsrc=rsrc, rslot=rslot: e.dma_start(
                        out=RSC[:, rslot * 512:(rslot + 1) * 512], in_=rsrc),
                        waits=sl_rsc.wait_free(), incs=[sl_rsc.prod_inc()])
                    sl_rsc.wrote()
                    rcw = sl_rsc.prod_wait()
                    for ch in range(16):
                        w = (list(wtgt) + [("aold", aold)] + slx.wait_free()) \
                            if ch == 0 else []
                        last_xln = (cb == 3 and csub == 3 and ch == 15)
                        last_wbf = (csub == 3 and ch == 15)
                        p.op("tensor", lambda e, ch=ch, csub=csub, PSX=PSX: e.matmul(
                            PSX[:, :],
                            WBF[:, ch * 512 + csub * 128: ch * 512 + csub * 128 + 128],
                            XLN[:, ch * 512:(ch + 1) * 512],
                            start=(ch == 0), stop=(ch == 15)),
                            waits=w,
                            incs=([(psn, 1)] if ch == 15 else [])
                                 + ([sl_xln.consumed()] if last_xln else [])
                                 + ([sl_wbf.consumed()] if last_wbf else []))
                        if ch == 0:
                            slx.wrote()
                    p.op("vector", lambda e, rslot=rslot, PSX=PSX: e.tensor_tensor(
                        FSD[:, rslot * 512:(rslot + 1) * 512], PSX[:, :],
                        RSC[:, rslot * 512:(rslot + 1) * 512], ALU.add),
                        waits=[(psn, p.val(psn))] + rcw + sl_fsd.wait_free(),
                        incs=[("dve_op", 1), slx.consumed(), sl_rsc.consumed()])
                    sl_fsd.wrote()
                    p.edge("vector")
                    dims3 = [[TOK, 128], [1, 512]]
                    p.op("sync", lambda e, cc=cc, tb=tb, rdst=rdst, rslot=rslot: e.dma_start(
                        out=_ap(rdst, (cc * 128) * TOK + tb * 512, dims3),
                        in_=FSD[:, rslot * 512:(rslot + 1) * 512]),
                        waits=[("dve_op", p.val("dve_op"))],
                        incs=[sl_fsd.consumed(16)])
        return sl_fsd.done_waits()

    def ffn_phase(rold, rdst, wg_tgt, xln_tgt, rold_done=()):
        p.op("sync", lambda e, rold=rold, rdst=rdst: e.dma_start(
            out=rdst[:], in_=rold[:]), waits=list(rold_done),
            incs=[("ffinit", 16)])
        ffinit = p.val("ffinit")
        for sup in range(16):
            for hc in range(16):
                p.op("sync", lambda e, sup=sup, hc=hc: e.dma_start(
                    out=FWG[:, hc * 512:(hc + 1) * 512],
                    in_=_ap(wg_f, (sup * 512) * D + hc * 128, [[D, 512], [1, 128]]),
                    transpose=True),
                    waits=([("wgath", wg_tgt)] + sl_fwg.wait_free()) if hc == 0 else [],
                    incs=[("fwld", 16)])
                if hc == 0:
                    sl_fwg.wrote()
                p.op("sync", lambda e, sup=sup, hc=hc: e.dma_start(
                    out=FWU[:, hc * 512:(hc + 1) * 512],
                    in_=_ap(wu_f, (sup * 512) * D + hc * 128, [[D, 512], [1, 128]]),
                    transpose=True), incs=[("fwld", 16)])
            for oc in range(4):
                for cb in range(4):
                    p.op("sync", lambda e, sup=sup, oc=oc, cb=cb: e.dma_start(
                        out=FWD[:, (oc * 4 + cb) * 512:(oc * 4 + cb) * 512 + 512],
                        in_=_ap(wd_f, (cb * 512) * INTER + sup * 512 + oc * 128,
                                [[INTER, 512], [1, 128]]),
                        transpose=True), incs=[("fwld", 16)])
            fwld = p.val("fwld")
            for tb in range(4):
                for oc in range(4):
                    for ch in range(16):
                        w = []
                        if ch == 0:
                            w = ([("fwld", fwld), ("xlnorm", xln_tgt)]
                                 + sl_psC.wait_free() + sl_psD.wait_free())
                        p.op("tensor", lambda e, ch=ch, oc=oc, tb=tb: e.matmul(
                            PS2[:, :],
                            FWG[:, ch * 512 + oc * 128: ch * 512 + oc * 128 + 128],
                            XLN[:, ch * TOK + tb * 512: ch * TOK + tb * 512 + 512],
                            start=(ch == 0), stop=(ch == 15)),
                            waits=w, incs=([("ffg", 1)] if ch == 15 else []))
                        p.op("tensor", lambda e, ch=ch, oc=oc, tb=tb: e.matmul(
                            PS3[:, :],
                            FWU[:, ch * 512 + oc * 128: ch * 512 + oc * 128 + 128],
                            XLN[:, ch * TOK + tb * 512: ch * TOK + tb * 512 + 512],
                            start=(ch == 0), stop=(ch == 15)),
                            incs=([("ffg", 1)] if ch == 15 else []))
                        if ch == 0:
                            sl_psC.wrote()
                            sl_psD.wrote()
                    ffg = p.val("ffg")
                    fslot = oc % 2
                    wsilu = [("ffg", ffg)]
                    if p.val("ffin") >= 1:
                        wsilu.append(("ffin", p.val("ffin") - 1))
                    p.op("scalar", lambda e, fslot=fslot: e.activation(
                        FSG[:, fslot * 512:(fslot + 1) * 512], PS2[:, :], AF.Silu),
                        waits=wsilu, incs=[("ffs", 1), sl_psC.consumed()])
                    wfin = sl_fin.wait_free() if oc == 0 else []
                    p.op("vector", lambda e, oc=oc, fslot=fslot: e.tensor_tensor(
                        FIN[:, oc * 512:(oc + 1) * 512],
                        FSG[:, fslot * 512:(fslot + 1) * 512], PS3[:, :], ALU.mult),
                        waits=[("ffs", p.val("ffs"))] + wfin,
                        incs=[("ffin", 1), sl_psD.consumed()])
                    if oc == 0:
                        sl_fin.wrote()
                    p.edge("vector")
                ffin = p.val("ffin")
                for cc in range(16):
                    k2 = cc % 2
                    PSX, slx, psn = ((PS0, sl_psA, "psA") if k2 == 0
                                     else (PS1, sl_psB, "psB"))
                    for oc in range(4):
                        w = ([("ffin", ffin)] + slx.wait_free()) if oc == 0 else []
                        p.op("tensor", lambda e, oc=oc, cc=cc, PSX=PSX: e.matmul(
                            PSX[:, :],
                            FWD[:, (oc * 4 + cc // 4) * 512 + (cc % 4) * 128:
                                   (oc * 4 + cc // 4) * 512 + (cc % 4) * 128 + 128],
                            FIN[:, oc * 512:(oc + 1) * 512],
                            start=(oc == 0), stop=(oc == 3)),
                            waits=w,
                            incs=([(psn, 1)] if oc == 3 else [])
                                 + ([sl_fin.consumed()] if (cc == 15 and oc == 3) else [])
                                 + ([sl_fwg.consumed()] if (tb == 3 and cc == 15 and oc == 3) else []))
                        if oc == 0:
                            slx.wrote()
                    p.op("vector", lambda e, cc=cc, PSX=PSX: e.tensor_copy(
                        FEV[:, (cc % 4) * 512:(cc % 4) * 512 + 512], PSX[:, :]),
                        waits=[(psn, p.val(psn))]
                              + (sl_fev.wait_free() if cc % 4 == 0 else []),
                        incs=[("ffev", 1), slx.consumed()])
                    if cc % 4 == 0:
                        sl_fev.wrote()
                    p.edge("vector")
                    if cc % 4 == 3:
                        cg = cc // 4
                        ffev = p.val("ffev")
                        dims = [[TOK, 128], [128 * TOK, 4], [1, 512]]
                        wacc = [("ffev", ffev)]
                        if sup == 0:
                            wacc.append(("ffinit", ffinit))
                        p.op("gpsimd", lambda e, tb=tb, cg=cg, rdst=rdst: e.dma_start(
                            out=_ap(rdst, (cg * 4 * 128) * TOK + tb * 512, dims),
                            in_=FEV[:].rearrange("p (c t) -> p c t", c=4),
                            accum_op=ALU.add),
                            waits=wacc, incs=[sl_fev.consumed(16)])
        return sl_fev.done_waits()

    wc_done = wconv_phase()
    wg_tgt = spline_phase() if "spline" not in dev_skip else 0

    xt = ln_phase(residA, waits0=[("rst", 16)]) if "ln" not in dev_skip else 0
    qkv_done = qkv_phase("a", xt, wc_done) if "qkv" not in dev_skip else []
    for (i_, o_) in ((qsh, qga), (ksh, kga), (vsh, vga), (nsh, nga)):
        p.op("gpsimd", lambda e, i_=i_, o_=o_: e.collective_compute(
            "AllGather", ALU.bypass, replica_groups=G4,
            ins=[i_.opt()], outs=[o_.opt()]),
            waits=qkv_done, incs=[("cc", 1)])
    ao_done = attn_phase("a", p.val("cc")) if "attn" not in dev_skip else []
    p.op("gpsimd", lambda e: e.collective_compute(
        "AllGather", ALU.bypass, replica_groups=G4,
        ins=[aosh.opt()], outs=[aoga.opt()]),
        waits=ao_done, incs=[("cc", 1)])
    cc_ao = p.val("cc")
    p.barrier()
    op1_done = []
    if "outproj" not in dev_skip:
        op1_done = outproj_phase(woutB, residA, residB, cc_ao, wc_done)
    p.barrier()

    ffn1_done = None
    if n_sub >= 2:
        xt2 = ln_phase(residB, waits0=op1_done)
        ffn1_done = ffn_phase(residB, residA, wg_tgt, xt2, op1_done)
        p.barrier()
    if n_sub >= 3:
        xt3 = ln_phase(residA, waits0=ffn1_done)
        qkv_done3 = qkv_phase("b", xt3, wc_done)
        for (i_, o_) in ((qsh, qga), (kshb, kgab), (vshb, vgab), (nsh, nga)):
            p.op("gpsimd", lambda e, i_=i_, o_=o_: e.collective_compute(
                "AllGather", ALU.bypass, replica_groups=G4,
                ins=[i_.opt()], outs=[o_.opt()]),
                waits=qkv_done3, incs=[("cc", 1)])
        ao_done3 = attn_phase("b", p.val("cc"))
        p.op("gpsimd", lambda e: e.collective_compute(
            "AllGather", ALU.bypass, replica_groups=G4,
            ins=[aosh.opt()], outs=[aoga.opt()]),
            waits=ao_done3, incs=[("cc", 1)])
        cc_ao = p.val("cc")
        p.barrier()
        op2_done = outproj_phase(woB, residA, residB, cc_ao, wc_done)
        p.barrier()
    if n_sub >= 4:
        xt4 = ln_phase(residB, waits0=op2_done)
        ffn2_done = ffn_phase(residB, out_t, wg_tgt, xt4, op2_done)
        p.op("sync", None, waits=ffn2_done)
    else:
        final = {0: residA, 1: residB, 2: residA, 3: residB}[n_sub]
        wfin = ([("rst", p.val("rst"))] if n_sub == 0 else
                op1_done if n_sub == 1 else
                op2_done if n_sub == 3 else ffn1_done)
        p.op("sync", lambda e, final=final: e.dma_start(out=out_t[:], in_=final[:]),
             waits=wfin, incs=[("outst", 16)])
        p.op("sync", None, waits=[("outst", p.val("outst"))])

    if dbg:
        hold = {"wqkvB": wqkvB, "woutB": woutB, "wqB": wqB,
                "qsh": qsh, "ksh": ksh, "vsh": vsh, "nsh": nsh, "qga": qga,
                "kga": kga, "vga": vga, "nga": nga, "aosh": aosh, "aoga": aoga,
                "kloc": kloc, "vloc": vloc, "qloc": qloc, "aoloc": aoloc,
                "nlock": nlock, "nlocq": nlocq,
                "kshb": kshb, "vshb": vshb, "kgab": kgab, "vgab": vgab,
                "wg_f": wg_f, "wu_f": wu_f, "wd_f": wd_f,
                "residA": residA, "residB": residB}
        sbufs = {"XLN": XLN, "STAT": STAT, "STCOL": STCOL,
                 "COSS": COSS, "SINS": SINS}
        p.barrier()
        for nm in dbg:
            if nm in hold:
                t = hold[nm]
                o = nc.dram_tensor("dbg_" + nm, list(t.shape), t.dtype,
                                   kind="ExternalOutput").ap()
                p.op("sync", lambda e, o=o, t=t: e.dma_start(out=o, in_=t),
                     incs=[("dbgst", 16)])
            else:
                t = sbufs[nm]
                sh = list(t.shape) if hasattr(t, "shape") else None
                o = nc.dram_tensor("dbg_" + nm, sh, t.dtype,
                                   kind="ExternalOutput").ap()
                p.op("sync", lambda e, o=o, t=t: e.dma_start(out=o, in_=t[:]),
                     incs=[("dbgst", 16)])
        p.op("sync", None, waits=[("dbgst", p.val("dbgst"))])

    p.replay()
    es.close()
    return nc


_CACHE = {}


def kernel(L0_attn_a_x, L0_attn_a_w_qkv, L0_attn_a_w_out, L0_attn_a_g,
           L0_attn_b_Wq, L0_attn_b_Wk, L0_attn_b_Wv, L0_attn_b_Wo,
           L0_ffn_gate_cp, L0_ffn_up_cp, L0_ffn_down_cp, n_sub=NSUB, dbg=(), dev_skip=()):
    x = np.asarray(L0_attn_a_x, dtype=np.float32)
    if "grid" not in _CACHE:
        _CACHE["grid"] = _spline_chunk_grid()
    i0, f0 = _CACHE["grid"]
    swin_g = _swin(np.asarray(L0_ffn_gate_cp, np.float32), i0, f0)
    swin_u = _swin(np.asarray(L0_ffn_up_cp, np.float32), i0, f0)
    swin_d = _swin(np.asarray(L0_ffn_down_cp, np.float32), i0, f0)
    cos, sinS = _rotary_tables()
    slopes = _alibi_slopes()
    slopes = slopes[np.arange(NH) // 4]
    stile = np.ascontiguousarray(np.broadcast_to(
        (np.arange(128, dtype=np.float64) * SP_STEP).astype(np.float32)[None, :],
        (128, 128)))
    kgrid = (np.arange(64, dtype=np.float32)[None, :] * 128.0 +
             np.arange(128, dtype=np.float32)[:, None]).astype(np.float32)
    relg_h = (np.arange(128, dtype=np.float32)[:, None] -
              np.arange(512, dtype=np.float32)[None, :]).astype(np.float32)

    wqkvT_h = np.ascontiguousarray(np.asarray(L0_attn_a_w_qkv, np.float32).T)
    woutT_h = np.ascontiguousarray(np.asarray(L0_attn_a_w_out, np.float32).T)
    wq_h = np.ascontiguousarray(np.asarray(L0_attn_b_Wq, np.float32))
    wk_h = np.ascontiguousarray(np.asarray(L0_attn_b_Wk, np.float32))
    wv_h = np.ascontiguousarray(np.asarray(L0_attn_b_Wv, np.float32))
    wo_h = np.ascontiguousarray(np.asarray(L0_attn_b_Wo, np.float32))
    xt = np.ascontiguousarray(np.transpose(x, (0, 2, 1)))

    in_maps = []
    npc = NCH // R
    for r in range(R):
        b, q = r // 4, r % 4
        cinfo = np.zeros((1, 8), np.int32)
        cinfo[0, 0] = q * 512 * TOK
        cinfo[0, 1] = q * 512
        cinfo[0, 2] = q * 128 * TOK
        cinfo[0, 3] = q * 128
        cinfo[0, 4] = q * TOK
        cinfo[0, 5] = q * 4 * TOK
        cinfo[0, 6] = q * TOK
        tok0 = q * TOK
        in_maps.append({
            "x": np.ascontiguousarray(xt[b][:, tok0:tok0 + TOK]),
            "wqkvT": wqkvT_h, "woutT": woutT_h,
            "wq": wq_h, "wk": wk_h, "wv": wv_h, "wo": wo_h,
            "swing": np.ascontiguousarray(swin_g[r * npc:(r + 1) * npc]),
            "swinu": np.ascontiguousarray(swin_u[r * npc:(r + 1) * npc]),
            "swind": np.ascontiguousarray(swin_d[r * npc:(r + 1) * npc]),
            "cosT": np.ascontiguousarray(cos[:, tok0:tok0 + TOK]),
            "sinT": np.ascontiguousarray(sinS[:, tok0:tok0 + TOK]),
            "stile": stile,
            "kgrid": kgrid,
            "slopebc": np.ascontiguousarray(np.broadcast_to(
                slopes[q * 4:(q + 1) * 4][None, :], (128, HPC))),
            "relg": relg_h,
            "cinfo": cinfo,
        })

    nc = build(n_sub, dev_skip=dev_skip, dbg=dbg)
    res = run_bass_kernel_spmd(nc, in_maps, core_ids=list(range(R)))
    out = np.zeros((BATCH, SEQ, D), np.float32)
    for r in range(R):
        b, q = r // 4, r % 4
        out[b, q * TOK:(q + 1) * TOK, :] = res.results[r]["out"].T
    if dbg:
        dd = {nm: [np.asarray(res.results[r]["dbg_" + nm]) for r in range(R)]
              for nm in dbg}
        return out, dd
    return out


BATCH = 2
